# version 18
# baseline (speedup 1.0000x reference)
import sys
if "/opt/trn_rl_repo" not in sys.path:
    sys.path.insert(0, "/opt/trn_rl_repo")

import os
import numpy as np
import ml_dtypes
from contextlib import ExitStack

import concourse.bass as bass
import concourse.mybir as mybir
from concourse.bass_utils import run_bass_kernel_spmd
from concourse.bass_core_rust import IndirectOffsetOnAxis as IOA
from concourse.aot_env import allow_ici_all_to_all

F32 = mybir.dt.float32
BF16 = mybir.dt.bfloat16
U32 = mybir.dt.uint32
AOT = mybir.AluOpType
ACTF = mybir.ActivationFunctionType
AXL = mybir.AxisListType
BF = ml_dtypes.bfloat16

DEBUG = bool(int(os.environ.get("MK_DEBUG", "0")))
MAXPH = int(os.environ.get("MK_MAXPH", "99"))


class _StopEmit(Exception):
    pass

os.environ.setdefault("TRN_ALLOW_ICI_ALL_TO_ALL", "1")

B, S, C = 4, 4096, 1024
NTOK = B * S
PTOK = 2048
NCH = 8
HALO = 16
W_X = PTOK + 2 * HALO
W_A0 = 1032
W_H0 = 1026
W_A1 = 515
W_B0 = 1026
NSLOT = 2048
YBLK = 384
HF = 2048
LN_EPS = 1e-5
ATTN_EPS = 1e-6
RG8 = [list(range(8))]
RGP = [[0, 1], [2, 3], [4, 5], [6, 7]]

MS = dict(a0L=(0, 3), a0R=(3, 5), h0L=(8, 1), h0R=(9, 1), a1L=(10, 1),
          a1R=(11, 2), laL=(13, 1), laR=(14, 2), b0L=(16, 1), b0R=(17, 2))

ENGS = ("tensor", "vector", "scalar", "gpsimd", "sync")


def tiles(total, mx=512):
    r, o = [], 0
    while o < total:
        n = min(mx, total - o)
        r.append((o, n))
        o += n
    return r


class Op:
    __slots__ = ("eng", "emit", "deps", "inc", "sidx", "dma", "coll", "done")

    def __init__(self, eng, emit, deps, dma, coll=False):
        self.eng = eng
        self.emit = emit
        self.deps = [d for d in deps if d is not None]
        self.dma = dma
        self.coll = coll
        self.inc = dma or coll
        self.sidx = None
        self.done = False


class Phase:
    def __init__(self, nc, sems, scount):
        self.nc = nc
        self.sems = sems
        self.ops = []

    def add(self, eng, emit, deps=(), dma=False, coll=False):
        op = Op(eng, emit, deps, dma, coll)
        self.ops.append(op)
        return op


def emit_all(nc, sems, ops):
    for op in ops:
        for d in op.deps:
            d.inc = True
    per_eng = {}
    for op in ops:
        per_eng.setdefault(op.eng, []).append(op)
    for lst in per_eng.values():
        lst[-1].inc = True
    cum = {e: 0 for e in ENGS}
    for op in ops:
        if op.inc:
            cum[op.eng] += 16 if op.dma else 1
        op.sidx = cum[op.eng]

    with nc.Block() as blk:
        def make_body(eng_name):
            def body(engine):
                hwm = {e: 0 for e in ENGS}
                for op in per_eng.get(eng_name, []):
                    for d in op.deps:
                        if d.sidx > hwm[d.eng]:
                            engine.wait_ge(sems[d.eng], d.sidx)
                            hwm[d.eng] = d.sidx
                    ins = op.emit(engine)
                    if op.inc:
                        ins.then_inc(sems[op.eng], 16 if op.dma else 1)
                last = None
                for op in per_eng.get(eng_name, []):
                    if op.dma or op.coll:
                        last = op
                if last is not None:
                    engine.wait_ge(sems[eng_name], last.sidx)
            return body

        for eng_name in ENGS:
            if eng_name in per_eng:
                getattr(blk, eng_name)(make_body(eng_name))


def build_nc():
    try:
        return _build_nc_inner()
    except _StopEmit as e:
        nc, dbg, sems, ops = e.args
        emit_all(nc, sems, ops)
        return nc, dbg


def _build_nc_inner():
    nc = bass.Bass("TRN2", target_bir_lowering=False)

    xfm = nc.dram_tensor("xfm", [C, W_X], F32, kind="ExternalInput")
    hfd = nc.dram_tensor("hfd", [4, NCH, 128, 128], F32, kind="ExternalInput")
    gfd = nc.dram_tensor("gfd", [4, NCH, 128, 128], F32, kind="ExternalInput")
    hsd = nc.dram_tensor("hsd", [4, NCH, 128, 128], F32, kind="ExternalInput")
    gsd = nc.dram_tensor("gsd", [4, NCH, 128, 128], F32, kind="ExternalInput")
    gamc = nc.dram_tensor("gamc", [128, NCH], F32, kind="ExternalInput")
    gamr = nc.dram_tensor("gamr", [1, C], F32, kind="ExternalInput")
    betr = nc.dram_tensor("betr", [1, C], F32, kind="ExternalInput")
    betc = nc.dram_tensor("betc", [128, NCH], F32, kind="ExternalInput")
    wgt = nc.dram_tensor("wgt", [NCH, 128, 8], F32, kind="ExternalInput")
    w1td = nc.dram_tensor("w1td", [C, HF], F32, kind="ExternalInput")
    w3td = nc.dram_tensor("w3td", [C, HF], F32, kind="ExternalInput")
    w2td = nc.dram_tensor("w2td", [HF, C], F32, kind="ExternalInput")
    cst = nc.dram_tensor("cst", [128, 772], F32, kind="ExternalInput")
    cstb = nc.dram_tensor("cstb", [128, 512], BF16, kind="ExternalInput")
    onesrow = nc.dram_tensor("onesrow", [1, 128], F32, kind="ExternalInput")
    e2b = nc.dram_tensor("e2b", [2, 128], BF16, kind="ExternalInput")
    eighth8 = nc.dram_tensor("eighth8", [8, 128], F32, kind="ExternalInput")
    lt16f = nc.dram_tensor("lt16f", [16, 18], F32, kind="ExternalInput")
    mmask = nc.dram_tensor("mmask", [128, 24], F32, kind="ExternalInput")
    stc = nc.dram_tensor("stc", [128, 4], BF16, kind="ExternalInput")

    out = nc.dram_tensor("out", [PTOK, C], F32, kind="ExternalOutput")

    x1row = nc.dram_tensor("x1row", [PTOK, C], BF16)
    x1all = nc.dram_tensor("x1all", [NTOK + 1, C], BF16, addr_space="Shared")
    kv_in = nc.dram_tensor("kv_in", [128, 1088], F32)
    kv_all = nc.dram_tensor("kv_all", [8 * 128, 1088], F32, addr_space="Shared")
    pidx = nc.dram_tensor("pidx", [128, 1], U32, kind="ExternalInput")
    sc_in = nc.dram_tensor("sc_in", [8, PTOK], F32)
    sc_out = nc.dram_tensor("sc_out", [8, PTOK], F32)
    idpat = nc.dram_tensor("idpat", [8 * YBLK + 256, 1], F32,
                           kind="ExternalInput")
    idstage = nc.dram_tensor("idstage", [8 * YBLK + 256, 1], F32)
    idrecv = nc.dram_tensor("idrecv", [8 * YBLK, 1], F32)
    ystage = nc.dram_tensor("ystage", [8 * YBLK + 130, C], BF16)
    yrecv = nc.dram_tensor("yrecv", [8 * YBLK, C], BF16)
    moeacc = [nc.dram_tensor(f"moeacc{i}", [PTOK + 130, C], F32)
              for i in range(2)]

    dbg = {}
    dbgref = [None]

    def DB(name, shape, dt=F32):
        if DEBUG:
            dbg[name] = nc.dram_tensor(name, shape, dt, kind="ExternalOutput")

    DB("d_a0", [C, W_A0]); DB("d_h0", [C, W_H0]); DB("d_a1", [C, W_A1])
    DB("d_q", [C, W_A1]); DB("d_la", [C, W_A1]); DB("d_b0", [C, W_B0])
    DB("d_r1", [C, PTOK]); DB("d_x1", [C, PTOK]); DB("d_sc", [8, PTOK])
    DB("d_thr", [128, 16]); DB("d_tok", [128, 16]); DB("d_wsl", [128, 16])
    DB("d_ydst", [128, 16]); DB("d_kv", [128, 1036]); DB("d_sd", [128, 24])
    DB("d_mask", [128, 128]); DB("d_xg", [128, NCH * NSLOT], BF16)
    DB("d_hh", [128, 2048], BF16); DB("d_sm", [128, 128])
    DB("d_yst", [8 * YBLK, C], BF16); DB("d_yrv", [8 * YBLK, C], BF16)
    DB("d_mc0", [32, C]); DB("d_mc1", [32, C])
    dbgref[0] = dbg

    st0 = ExitStack()

    def SB(stk, name, shape, dt=BF16):
        return stk.enter_context(nc.sbuf_tensor(name, shape, dt))

    def PS(stk, name, shape, dt=F32):
        return stk.enter_context(nc.psum_tensor(name, shape, dt))

    sems = {e: st0.enter_context(nc.semaphore("sem_" + e)) for e in ENGS}
    scount = {e: 0 for e in ENGS}

    def phase():
        return Phase(nc, sems, scount)

    nph = [0]
    SCHED = []

    def fence(excl=()):
        last = {}
        for op in SCHED:
            last[op.eng] = op
        return [v for k, v in last.items() if k not in excl]

    runidx = [0]

    def run(ph, auto_fence=True):
        sf = os.environ.get("MK_SAFE", "")
        if sf == "1" or (sf and str(runidx[0]) in sf.split(",")):
            auto_fence = True
        runidx[0] += 1
        if nph[0] >= MAXPH:
            return
        if auto_fence and SCHED:
            fz = fence()
            seen = set()
            for op in ph.ops:
                if op.eng not in seen:
                    seen.add(op.eng)
                    op.deps = op.deps + [d for d in fz if d is not op]
        SCHED.extend(ph.ops)
        nph[0] += 1

    s_cst = SB(st0, "s_cst", [128, 772], F32)
    s_cstb = SB(st0, "s_cstb", [128, 512], BF16)
    s_gam = SB(st0, "s_gam", [128, NCH], F32)
    s_bet = SB(st0, "s_bet", [128, NCH], F32)
    s_wgtb = SB(st0, "s_wgtb", [128, NCH * 8], BF16)
    s_1row = SB(st0, "s_1row", [1, 128], F32)
    s_e2b = SB(st0, "s_e2b", [2, 128], BF16)
    s_8th = SB(st0, "s_8th", [8, 128], F32)
    s_lt16 = SB(st0, "s_lt16", [16, 18], F32)
    s_mmsk = SB(st0, "s_mmsk", [128, 24], F32)
    s_stc = SB(st0, "s_stc", [128, 4], BF16)
    stZ = ExitStack()
    s_zb = SB(stZ, "s_zb", [128, 1024], BF16)
    s_zf = s_zb.bitcast(F32)

    KONST = s_cst[:, 0:128]
    piota = s_cst[:, 128:129]
    ones_f = s_cst[:, 129:130]
    IDENTF = s_cst[:, 132:260]
    LTRIF = s_cst[:, 260:388]
    SLTRIF = s_cst[:, 388:516]
    G16F = s_cst[:, 516:644]
    DMAT = s_cst[:, 644:772]
    e127 = s_cst[:, 130:131]
    ownidx = s_cst[:, 131:132]
    IDENT = s_cstb[:, 0:128]
    LTRI = s_cstb[:, 128:256]
    HMASK = s_cstb[:, 256:384]
    ones_b = s_cstb[:, 255:256]
    ones1rb = s_cstb[0:1, 384:512]
    ones1r = bass.AP(s_1row, 0, [[128, 1], [1, 128]])

    ph = phase()
    for dst, src in [(s_cst[:, :], cst.ap()), (s_cstb[:, :], cstb.ap()),
                     (s_gam[:, :], gamc.ap()), (s_bet[:, :], betc.ap()),
                     (s_1row[:, :], onesrow.ap()), (s_e2b[:, :], e2b.ap()),
                     (s_8th[:, :], eighth8.ap()), (s_lt16[:, :], lt16f.ap()),
                     (s_mmsk[:, :], mmask.ap()), (s_stc[:, :], stc.ap())]:
        ph.add("sync", lambda e, d=dst, s=src: e.dma_start(d, s), dma=True)
    ph.add("gpsimd", lambda e: e.dma_start(
        s_wgtb.ap().rearrange("p (k e) -> p k e", k=NCH), wgt.ap().rearrange("k p e -> p k e")), dma=True)
    z1 = ph.add("vector", lambda e: e.memset(s_zb.ap(), 0.0))
    z2 = z1
    ph.add("scalar", lambda e: e.dma_start(
        x1all.ap()[NTOK:NTOK + 1, :], s_zb[0:1, :]), deps=[z1], dma=True)
    idpf = ph.add("sync", lambda e: e.dma_start(idstage.ap(), idpat.ap()),
                  dma=True)
    run(ph)

    def emit_zero_fills(ph, extra=()):
        first = [list(extra)]
        tot = 8 * YBLK + 130
        for i in range((tot + 127) // 128):
            r0, m = i * 128, min(128, tot - i * 128)
            ph.add("scalar", lambda e, r0=r0, m=m: e.dma_start(
                ystage.ap()[r0:r0 + m, :], s_zb[0:m, :]),
                deps=[z1] + first.pop() if first else [z1],
                dma=True)
            if not first:
                first = []
        tot = PTOK + 130
        for acc in moeacc:
            for i in range((tot + 127) // 128):
                r0, m = i * 128, min(128, tot - i * 128)
                for cc0 in (0, 512):
                    ph.add("scalar", lambda e, r0=r0, m=m, a=acc, cc0=cc0:
                           e.dma_start(a.ap()[r0:r0 + m, cc0:cc0 + 512],
                                       s_zf[0:m, :]), deps=[z2], dma=True)

    stA = ExitStack()
    s_x = SB(stA, "s_x", [128, NCH, W_X])
    s_r1 = SB(stA, "s_r1", [128, NCH, PTOK])
    s_scst = SB(stA, "s_scst", [8, 512], F32)
    s_st = SB(stA, "s_st", [128, 256], F32)
    s_tm = SB(stA, "s_tm", [128, 2, 1024])
    s_gb = SB(stA, "s_gb", [128, 2048])
    s_gr = SB(stA, "s_gr", [1, 2048], F32)
    s_x1t = SB(stA, "s_x1t", [128, 2, 1024])

    stB = ExitStack()
    s_a0 = SB(stB, "s_a0", [128, NCH, W_A0])
    s_h0 = SB(stB, "s_h0", [128, NCH, W_H0])
    s_a1 = SB(stB, "s_a1", [128, NCH, W_A1])
    s_h1 = SB(stB, "s_h1", [128, NCH, W_A1])
    s_la = SB(stB, "s_la", [128, NCH, W_A1])
    s_b0 = SB(stB, "s_b0", [128, NCH, W_B0])
    s_fd = SB(stB, "s_fd", [128, 4 * NCH * 128])
    stB2 = ExitStack()
    s_qsp = SB(stB2, "s_qsp", [128, 2, W_A1])
    s_q2 = SB(stB2, "s_q2", [128, 2, W_A1])
    s_qt = SB(stB2, "s_qt", [128, 8 * 128])
    s_kvf = SB(stB2, "s_kvf", [128, 1036], F32)
    s_kvp = SB(stB2, "s_kvp", [128, 544], F32)
    s_pidx = SB(stB2, "s_pidx", [128, 1], U32)
    s_kvm = SB(stB2, "s_kvm", [128, NCH * 128])
    s_z2 = SB(stB2, "s_z2", [128, 2], F32)
    s_z2b = SB(stB2, "s_z2b", [128, 2])
    s_rden = SB(stB2, "s_rden", [2, W_A1], F32)
    s_rdenb = SB(stB2, "s_rdenb", [2, W_A1])
    s_nm = SB(stB2, "s_nm", [128, W_A1])

    stPa = ExitStack()
    pf = [PS(stPa, f"pf{i}", [128, 512], F32) for i in range(6)]
    pT = PS(stPa, "pT", [128, 1024], BF16)
    pS2 = PS(stPa, "pS2", [128, 512], F32)

    def xap(k, col, n, step=1):
        return bass.AP(s_x, k * W_X + col, [[NCH * W_X, 128], [step, n]])

    def fap(t, k):
        off = (t * NCH + k) * 128
        return bass.AP(s_fd, off, [[4 * NCH * 128, 128], [1, 128]])

    def fdld(ph, src_t, deps):
        return ph.add("gpsimd", lambda e, s=src_t: e.dma_start(
            bass.AP(s_fd, 0, [[4 * NCH * 128, 128], [NCH * 128, 4],
                              [128, NCH], [1, 128]]),
            s.ap().rearrange("t k p c -> p t k c")), deps=deps, dma=True)

    def sap(sb, W, k, col, n, step=1):
        return bass.AP(sb, k * W + col, [[NCH * W, 128], [step, n]])

    def margins(ph, sb, W, k, lkey, rkey, dep):
        lo, ln = MS[lkey]
        ro, rn = MS[rkey]
        dl = dep if isinstance(dep, list) else [dep]
        o1 = ph.add("vector", lambda e, k=k, lo=lo, ln=ln: e.tensor_tensor(
            sap(sb, W, k, 0, ln), sap(sb, W, k, 0, ln),
            s_mmsk[:, lo:lo + ln], AOT.mult), deps=dl)
        return ph.add("vector", lambda e, k=k, ro=ro, rn=rn: e.tensor_tensor(
            sap(sb, W, k, W - rn, rn), sap(sb, W, k, W - rn, rn),
            s_mmsk[:, ro:ro + rn], AOT.mult), deps=[o1])

    ph = phase()
    lx = ph.add("gpsimd", lambda e: e.dma_start(
        s_x.ap(),
        xfm.ap().rearrange("(k p) w -> p k w", p=128)), dma=True)
    lhf = fdld(ph, hfd, [])
    NB = 6

    def conv_seg(ph, items, first_deps, ring_ofs=0):
        evs = []
        for i, (mk, ev) in enumerate(items):
            deps = list(first_deps) if i == 0 else []
            if i >= NB:
                deps.append(evs[i - NB])
            mm = ph.add("tensor", mk, deps=deps)
            evs.append(ph.add("vector", lambda e, ev=ev, mm=mm: ev(e), deps=[mm]))
        return evs

    def mk_down(bank, sb_src, base, k, t0, n, srcW=None, stride=2, xsrc=True):
        def mk(e, bank=bank, base=base, k=k, t0=t0, n=n):
            for tap in range(4):
                if xsrc:
                    rhs = xap(k, 2 * t0 + base + tap, n, 2)
                else:
                    rhs = sap(s_a0, W_A0, k, 2 * t0 + tap, n, 2)
                ins = e.matmul(bank[:, 0:n], fap(tap, k), rhs,
                               start=(tap == 0), stop=(tap == 3))
            return ins
        return mk

    items = []
    for k in range(NCH):
        for (t0, n) in tiles(W_A0):
            bank = pf[len(items) % NB]
            mk = mk_down(bank, None, 9, k, t0, n, xsrc=True)
            ev = (lambda e, bank=bank, k=k, t0=t0, n=n:
                  e.tensor_copy(sap(s_a0, W_A0, k, t0, n), bank[:, 0:n]))
            items.append((mk, ev))
    evs = conv_seg(ph, items, [lx, lhf])
    dep = evs[-1]
    mdeps = []
    for k in range(NCH):
        dep = margins(ph, s_a0, W_A0, k, "a0L", "a0R", dep)
        mdeps.append(dep)
    items = []
    for k in range(NCH):
        for (t0, n) in tiles(W_A1):
            bank = pf[len(items) % NB]
            mk = mk_down(bank, None, 0, k, t0, n, xsrc=False)
            ev = (lambda e, bank=bank, k=k, t0=t0, n=n:
                  e.tensor_copy(sap(s_a1, W_A1, k, t0, n), bank[:, 0:n]))
            items.append((mk, ev))
    evs = conv_seg(ph, items, [dep])
    last_a1_mm = [op for op in ph.ops if op.eng == "tensor"][-1]
    lgf = fdld(ph, gfd, [last_a1_mm])
    items = []
    for k in range(NCH):
        for (t0, n) in tiles(W_H0):
            bank = pf[len(items) % NB]
            mk = mk_down(bank, None, 13, k, t0, n, xsrc=True)
            ev = (lambda e, bank=bank, k=k, t0=t0, n=n:
                  e.tensor_copy(sap(s_h0, W_H0, k, t0, n), bank[:, 0:n]))
            items.append((mk, ev))
    evs0 = conv_seg(ph, items, [lgf, evs[-1]])
    items = []
    for k in range(NCH):
        for (t0, n) in tiles(W_A1):
            bank = pf[len(items) % NB]
            mk = mk_down(bank, None, 0, k, t0, n, xsrc=False)
            ev = (lambda e, bank=bank, k=k, t0=t0, n=n:
                  e.tensor_copy(sap(s_h1, W_A1, k, t0, n), bank[:, 0:n]))
            items.append((mk, ev))
    evs = conv_seg(ph, items, [evs0[-1]])
    dep = evs[-1]
    for k in range(NCH):
        dep = margins(ph, s_h0, W_H0, k, "h0L", "h0R", dep)
        dep = margins(ph, s_a1, W_A1, k, "a1L", "a1R", dep)
        dep = margins(ph, s_h1, W_A1, k, "a1L", "a1R", dep)
    run(ph)

    ph = phase()
    prev_q = [None, None]
    tinfo = []
    for k in range(NCH):
        par = k % 2
        mn = ph.add("vector", lambda e, k=k, par=par: e.tensor_scalar(
            s_qsp[:, par, :], sap(s_a1, W_A1, k, 0, W_A1), 0.0, None, AOT.min),
            deps=[prev_q[par]])
        ex = ph.add("scalar", lambda e, par=par: e.activation(
            s_q2[:, par, :], s_qsp[:, par, :], ACTF.Exp),
            deps=[mn, prev_q[par]])
        mx = ph.add("vector", lambda e, k=k, par=par: e.tensor_scalar(
            s_qsp[:, par, :], sap(s_a1, W_A1, k, 0, W_A1),
            0.0, None, AOT.max), deps=[ex])
        ad = ph.add("vector", lambda e, par=par: e.tensor_tensor(
            s_q2[:, par, :], s_q2[:, par, :], s_qsp[:, par, :], AOT.add),
            deps=[mx])

        def mkt(e, k=k, par=par):
            for j in range(4):
                e.transpose(pT[:, j * 128:(j + 1) * 128],
                            s_q2[:, par, 1 + 128 * j:1 + 128 * (j + 1)], IDENT)
            for j in range(4):
                ins = e.transpose(pT[:, (4 + j) * 128:(5 + j) * 128],
                                  sap(s_a1, W_A1, k, 1 + 128 * j, 128), IDENT)
            return ins
        tdeps = [ad] + ([tinfo[-1][1]] if k >= 1 else [])
        tp = ph.add("tensor", mkt, deps=tdeps)
        prev_q[par] = tp
        evt = ph.add("vector", lambda e: e.tensor_copy(s_qt[:, :], pT[:, :]),
                     deps=[tp])
        bank = pf[k % 2]
        kdeps = [evt] + ([tinfo[k - 2][2]] if k >= 2 else [])

        def mkkv(e, bank=bank):
            for j in range(4):
                e.matmul(bank[:, 0:128], s_qt[:, j * 128:(j + 1) * 128],
                         s_qt[:, (4 + j) * 128:(5 + j) * 128],
                         start=(j == 0), stop=(j == 3))
            for j in range(4):
                ins = e.matmul(bank[:, 128:129], s_qt[:, j * 128:(j + 1) * 128],
                               ones_b, start=(j == 0), stop=(j == 3))
            return ins
        kv = ph.add("tensor", mkkv, deps=kdeps)
        e1 = ph.add("vector", lambda e, k=k, bank=bank: e.tensor_copy(
            s_kvf[:, k * 128:(k + 1) * 128], bank[:, 0:128]), deps=[kv])
        e2 = ph.add("vector", lambda e, k=k, bank=bank: e.tensor_copy(
            s_kvf[:, 1024 + k:1025 + k], bank[:, 128:129]), deps=[kv, e1])
        tinfo.append((tp, evt, e2))
    run(ph)

    ph = phase()
    d1 = ph.add("sync", lambda e: e.dma_start(kv_in.ap()[:, 0:1036],
                                              s_kvf[:, :]), dma=True)
    dpi = ph.add("sync", lambda e: e.dma_start(s_pidx[:, :], pidx.ap()),
                 dma=True)
    _CCM = int(os.environ.get("MK_CCMODE", "0"))
    if _CCM == 1:
        cc = ph.add("gpsimd", lambda e: e.dma_start(
            kv_all.ap()[0:128, :], kv_in.ap()), deps=[d1], dma=True)
    elif _CCM == 2:
        cc = ph.add("gpsimd", lambda e: e.collective_compute(
            "AllGather", AOT.bypass, replica_groups=RG8,
            ins=[kv_in.ap()[0:16, 0:128]], outs=[kv_all.ap()[0:128, 0:128]]),
            deps=[d1], coll=True)
    else:
        cc = ph.add("gpsimd", lambda e: e.collective_compute(
            "AllGather", AOT.bypass, replica_groups=RG8,
            ins=[kv_in.ap()], outs=[kv_all.ap()]), deps=[d1], coll=True)
    adds = []
    _SKIPG = bool(int(os.environ.get("MK_SKIPG", "0")))
    for half, (ho, hn) in enumerate([] if _SKIPG else [(0, 544), (544, 492)]):
        gk = ph.add("gpsimd", lambda e, ho=ho: e.indirect_dma_start(
            s_kvp[:, :], None, kv_all.ap(),
            IOA(s_pidx[:, 0:1], 0), element_offset=ho),
            deps=[cc, dpi] + adds, dma=True)
        ad = ph.add("vector", lambda e, ho=ho, hn=hn: e.tensor_tensor(
            s_kvf[:, ho:ho + hn], s_kvf[:, ho:ho + hn], s_kvp[:, 0:hn],
            AOT.add), deps=[gk])
        adds.append(ad)
    if DEBUG:
        ph.add("gpsimd", lambda e: e.dma_start(dbg["d_kv"].ap(), s_kvf[:, :]),
               deps=adds, dma=True)
    run(ph)

    ph = phase()
    bankD_last = {0: None, 1: None}
    bankN_last = {0: None, 1: None}
    pS2_last = None
    prev_q = [None, None]
    lamg = {}
    tplast = tinfo[-1][0]
    for k in range(NCH):
        par = k % 2
        km = ph.add("vector", lambda e, k=k: e.tensor_tensor(
            s_kvm[:, k * 128:(k + 1) * 128], s_kvf[:, k * 128:(k + 1) * 128],
            HMASK, AOT.mult), deps=adds)
        za = ph.add("vector", lambda e, k=k: e.tensor_tensor(
            s_z2[:, 0:1], s_kvf[:, 1024 + k:1025 + k], HMASK[:, 0:1], AOT.mult),
            deps=adds)
        zb_ = ph.add("vector", lambda e, k=k: e.tensor_tensor(
            s_z2[:, 1:2], s_kvf[:, 1024 + k:1025 + k], HMASK[:, 64:65], AOT.mult),
            deps=[za])
        zc = ph.add("vector", lambda e: e.tensor_copy(s_z2b[:, :], s_z2[:, :]),
                    deps=[zb_])
        mn = ph.add("vector", lambda e, k=k, par=par: e.tensor_scalar(
            s_qsp[:, par, :], sap(s_a1, W_A1, k, 0, W_A1), 0.0, None, AOT.min),
            deps=[prev_q[par], tplast])
        ex = ph.add("scalar", lambda e, par=par: e.activation(
            s_q2[:, par, :], s_qsp[:, par, :], ACTF.Exp),
            deps=[mn, prev_q[par]])
        mx = ph.add("vector", lambda e, k=k, par=par: e.tensor_scalar(
            s_qsp[:, par, :], sap(s_a1, W_A1, k, 0, W_A1),
            0.0, None, AOT.max), deps=[ex])
        ad = ph.add("vector", lambda e, par=par: e.tensor_tensor(
            s_q2[:, par, :], s_q2[:, par, :], s_qsp[:, par, :], AOT.add),
            deps=[mx])
        la_last = None
        for (off, n) in tiles(W_A1):
            bankD = pf[2 + par]
            bankN = pf[4 + par]
            dmm = ph.add("tensor", lambda e, par=par, off=off, n=n, bankD=bankD:
                         e.matmul(bankD[0:2, 0:n], s_z2b[:, :],
                                  s_q2[:, par, off:off + n], start=True,
                                  stop=True), deps=[zc, ad, bankD_last[par]])
            de = ph.add("vector", lambda e, off=off, n=n, bankD=bankD:
                        e.tensor_scalar(s_rden[0:2, off:off + n],
                                        bankD[0:2, 0:n], ATTN_EPS, None, AOT.add),
                        deps=[dmm, pS2_last])
            bankD_last[par] = de
            dr = ph.add("vector", lambda e, off=off, n=n: e.reciprocal(
                s_rden[0:2, off:off + n], s_rden[0:2, off:off + n]),
                deps=[de])
            db = ph.add("vector", lambda e, off=off, n=n: e.tensor_copy(
                s_rdenb[0:2, off:off + n], s_rden[0:2, off:off + n]),
                deps=[dr])
            xm = ph.add("tensor", lambda e, off=off, n=n: e.matmul(
                pS2[:, 0:n], s_e2b[:, :], s_rdenb[0:2, off:off + n],
                start=True, stop=True), deps=[db, pS2_last])
            nmm = ph.add("tensor", lambda e, k=k, par=par, off=off, n=n,
                         bankN=bankN:
                         e.matmul(bankN[:, 0:n], s_kvm[:, k * 128:(k + 1) * 128],
                                  s_q2[:, par, off:off + n], start=True,
                                  stop=True), deps=[km, ad, bankN_last[par]])
            ne = ph.add("vector", lambda e, off=off, n=n, bankN=bankN:
                        e.tensor_copy(s_nm[:, off:off + n], bankN[:, 0:n]),
                        deps=[nmm])
            bankN_last[par] = ne
            la_last = ph.add("vector", lambda e, k=k, off=off, n=n:
                             e.tensor_tensor(sap(s_la, W_A1, k, off, n),
                                             s_nm[:, off:off + n],
                                             pS2[:, 0:n], AOT.mult),
                             deps=[ne, xm])
            pS2_last = la_last
        prev_q[par] = la_last
        lamg[k] = margins(ph, s_la, W_A1, k, "laL", "laR", la_last)
    la_final = la_last
    lhs_ = fdld(ph, hsd, [tplast, cc])
    emit_zero_fills(ph, extra=[cc])
    run(ph, auto_fence=False)
    stB2.close()
    stB3 = ExitStack()
    s_fd2 = SB(stB3, "s_fd2", [128, 4 * NCH * 128])

    def fap2(t, k):
        off = (t * NCH + k) * 128
        return bass.AP(s_fd2, off, [[4 * NCH * 128, 128], [1, 128]])

    def fdld2(ph, src_t, deps):
        return ph.add("gpsimd", lambda e, s=src_t: e.dma_start(
            bass.AP(s_fd2, 0, [[4 * NCH * 128, 128], [NCH * 128, 4],
                               [128, NCH], [1, 128]]),
            s.ap().rearrange("t k p c -> p t k c")), deps=deps, dma=True)

    ph = phase()
    lgs_ = fdld2(ph, gsd, [la_final])

    def up_items(level):
        its = []
        for k in range(NCH):
            for even in (True, False):
                W = 513 if level == 1 else 1024
                for (t0, n) in tiles(W):
                    its.append((k, even, t0, n))
        return its

    evs1 = []
    tmk = {k: [] for k in range(NCH)}
    i = 0
    for (k, even, t0, n) in up_items(1):
        bank = pf[i % NB]
        dl = [-1, 0, 0, 1] if even else [0, 0, 1, 1]
        base = 1 if even else 0
        deps = [lhs_, lgs_] if i == 0 else []
        deps.append(lamg[k])
        if i >= NB:
            deps.append(evs1[i - NB])
        else:
            deps.append(la_final)

        def mk(e, bank=bank, k=k, t0=t0, n=n, dl=dl, base=base):
            for tap in range(4):
                e.matmul(bank[:, 0:n], fap(tap, k),
                         sap(s_la, W_A1, k, base + dl[tap] + t0, n),
                         start=(tap == 0), stop=False)
            for tap in range(4):
                ins = e.matmul(bank[:, 0:n], fap2(tap, k),
                               sap(s_h1, W_A1, k, base + dl[tap] + t0, n),
                               start=False, stop=(tap == 3))
            return ins
        mm = ph.add("tensor", mk, deps=deps)
        foff = (1 if even else 0) + 2 * t0
        ev = ph.add("vector", lambda e, k=k, foff=foff, n=n, bank=bank:
                    e.tensor_copy(sap(s_b0, W_B0, k, foff, n, 2),
                                  bank[:, 0:n]), deps=[mm])
        evs1.append(ev)
        tmk[k].append(ev)
        i += 1
    b0m = {}
    mprev = None
    for k in range(NCH):
        mprev = margins(ph, s_b0, W_B0, k, "b0L", "b0R",
                        tmk[k] + ([mprev] if mprev else []))
        b0m[k] = mprev

    evs0 = []
    i = 0
    n1 = len(evs1)
    for (k, even, t0, n) in up_items(0):
        bank = pf[i % NB]
        dl = [-1, 0, 0, 1] if even else [0, 0, 1, 1]
        base = 1
        deps = [b0m[k]]
        if i >= NB:
            deps.append(evs0[i - NB])
        else:
            deps.append(evs1[n1 - NB + i])

        def mk(e, bank=bank, k=k, t0=t0, n=n, dl=dl, base=base):
            for tap in range(4):
                e.matmul(bank[:, 0:n], fap(tap, k),
                         sap(s_b0, W_B0, k, base + dl[tap] + t0, n),
                         start=(tap == 0), stop=False)
            for tap in range(4):
                ins = e.matmul(bank[:, 0:n], fap2(tap, k),
                               sap(s_h0, W_H0, k, base + dl[tap] + t0, n),
                               start=False, stop=(tap == 3))
            return ins
        mm = ph.add("tensor", mk, deps=deps)
        foff = (0 if even else 1) + 2 * t0
        ev = ph.add("vector", lambda e, k=k, foff=foff, n=n, bank=bank:
                    e.tensor_tensor(sap(s_r1, PTOK, k, foff, n, 2),
                                    bank[:, 0:n],
                                    xap(k, HALO + foff, n, 2), AOT.add),
                    deps=[mm])
        evs0.append(ev)
        i += 1
    run(ph, auto_fence=False)

    if DEBUG:
        ph = phase()
        for nm, sb, W in [("d_a0", s_a0, W_A0), ("d_h0", s_h0, W_H0),
                          ("d_a1", s_a1, W_A1),
                          ("d_la", s_la, W_A1), ("d_b0", s_b0, W_B0),
                          ("d_r1", s_r1, PTOK)]:
            ph.add("gpsimd", lambda e, nm=nm, sb=sb, W=W: e.dma_start(
                dbg[nm].ap().rearrange("(k p) w -> p k w", p=128),
                sb.ap()), dma=True)
        run(ph)

    def rap(k, j):
        return bass.AP(s_r1, k * PTOK + 512 * j, [[NCH * PTOK, 128], [1, 512]])

    gamb = s_gb[:, 0:1024]
    betb = s_gb[:, 1024:2048]
    pfb = [pf[i].bitcast(BF16) for i in range(4)]
    pS2b = pS2.bitcast(BF16)

    def bap(h, c0, n):
        return bass.AP(h, c0, [[1024, 128], [1, n]])
    ph = phase()
    lgr = ph.add("sync", lambda e: e.dma_start(s_gr[0:1, 0:1024], gamr.ap()),
                 dma=True)
    lbr = ph.add("sync", lambda e: e.dma_start(s_gr[0:1, 1024:2048],
                                               betr.ap()), dma=True)
    gbe = []
    prevgb = {4: None, 5: None}
    for bi, srccol in ((4, 1024), (5, 1024)):
        pass
    for h in range(2):
        for bi, c0 in ((4, 0), (5, 1024)):
            mmg = ph.add("tensor", lambda e, bi=bi, c0=c0, h=h: e.matmul(
                pf[bi][:, 0:512], ones1r,
                s_gr[0:1, c0 + 512 * h:c0 + 512 * (h + 1)],
                start=True, stop=True),
                deps=[lgr, lbr, prevgb[bi]])
            evg = ph.add("vector", lambda e, bi=bi, c0=c0, h=h:
                         e.tensor_copy(s_gb[:, c0 + 512 * h:c0 + 512 * (h + 1)],
                                       pf[bi][:, 0:512]), deps=[mmg])
            prevgb[bi] = evg
            gbe.append(evg)

    _P4N = int(os.environ.get("MK_P4N", "16"))
    _P4OPS = int(os.environ.get("MK_P4OPS", "99"))

    def ckpt(n):
        if _P4OPS <= n:
            raise _StopEmit(nc, dbgref[0], sems, SCHED + list(ph.ops))
    xrow_d = []
    sc_ev = []
    prev_scst = None
    tpb_ops = [None] * 16
    bke_ops = [None] * 16
    ln1_ops = [None] * 16
    s2_ops = [None] * 16
    x1_ops = [None] * 16
    dmax_ops = [None] * 16
    sq_last = [None] * 16
    for T in range(_P4N):
        par = T % 2
        A = pfb[par]
        B = pfb[2 + par]
        BK = pT if par == 0 else pS2b
        F = 16 * T

        def mtr_(e, T=T, A=A):
            for k in range(NCH):
                ins = e.transpose(bap(A, k * 128, 128),
                                  bass.AP(s_r1, k * PTOK + 128 * T,
                                          [[NCH * PTOK, 128], [1, 128]]),
                                  IDENT)
            return ins
        tpR = ph.add("tensor", mtr_,
                     deps=([ln1_ops[T - 2]] if T >= 2 else []))

        def mtx_(e, T=T, B=B):
            for k in range(NCH):
                ins = e.transpose(bap(B, k * 128, 128),
                                  bass.AP(s_x, k * W_X + HALO + 128 * T,
                                          [[NCH * W_X, 128], [1, 128]]),
                                  IDENT)
            return ins
        tpX = ph.add("tensor", mtx_,
                     deps=[tpR] + ([s2_ops[T - 2]] if T >= 2 else []))
        ckpt(2)
        if T >= 1:
            Tp = T - 1
            BKp = pT if Tp % 2 == 0 else pS2b
            def mtb_(e, Tp=Tp, BKp=BKp):
                for k in range(NCH):
                    ins = e.transpose(bap(BKp, k * 128, 128),
                                      s_x1t[:, Tp % 2, k * 128:(k + 1) * 128],
                                      IDENT)
                return ins
            tpb = ph.add("tensor", mtb_,
                         deps=[x1_ops[Tp]]
                         + ([bke_ops[Tp - 2]] if Tp >= 2 else []))
            tpb_ops[Tp] = tpb
            bke = ph.add("scalar", lambda e, Tp=Tp, BKp=BKp: e.activation(
                bass.AP(s_r1, 128 * Tp,
                        [[NCH * PTOK, 128], [PTOK, NCH], [1, 128]]),
                bap(BKp, 0, 1024).rearrange("p (k c) -> p k c", k=NCH),
                ACTF.Copy), deps=[tpb])
            bke_ops[Tp] = bke

        tmp1 = s_tm[:, par, :]
        x1sl = s_x1t[:, par, :]
        rsum = ph.add("vector", lambda e, A=A, F=F: e.reduce_sum(
            s_st[:, F:F + 1], bap(A, 0, 1024), axis=AXL.X), deps=[tpR])
        ckpt(3)
        cpA = ph.add("vector", lambda e, A=A, tmp1=tmp1: e.tensor_scalar(
            tmp1, bap(A, 0, 1024), 1.0, None, AOT.mult),
            deps=[tpR] + ([x1_ops[T - 2]] if T >= 2 else []))
        sq = ph.add("scalar", lambda e, F=F, tmp1=tmp1, x1sl=x1sl:
                    e.activation(x1sl, tmp1, ACTF.Square,
                                 accum_out=s_st[:, F + 1:F + 2]),
                    deps=[cpA]
                    + ([tpb_ops[T - 2], dmax_ops[T - 2]] if T >= 2 else []))
        ckpt(4)
        mu = ph.add("vector", lambda e, F=F: e.tensor_scalar(
            s_st[:, F:F + 2], s_st[:, F:F + 2], 1.0 / C, None, AOT.mult),
            deps=[rsum, sq])
        v2 = ph.add("vector", lambda e, F=F: e.tensor_scalar(
            s_st[:, F + 2:F + 3], s_st[:, F:F + 1], s_st[:, F:F + 1],
            s_st[:, F + 1:F + 2], AOT.mult, AOT.subtract), deps=[mu])
        v3 = ph.add("vector", lambda e, F=F: e.tensor_scalar(
            s_st[:, F + 2:F + 3], s_st[:, F + 2:F + 3], -1.0, LN_EPS,
            AOT.mult, AOT.add), deps=[v2])
        v4 = ph.add("vector", lambda e, F=F: e.reciprocal(
            s_st[:, F + 2:F + 3], s_st[:, F + 2:F + 3]), deps=[v3])
        v5 = ph.add("scalar", lambda e, F=F: e.activation(
            s_st[:, F + 3:F + 4], s_st[:, F + 2:F + 3], ACTF.Sqrt), deps=[v4])
        v6 = ph.add("vector", lambda e, F=F: e.tensor_scalar(
            s_st[:, F + 4:F + 5], s_st[:, F:F + 1], -1.0, s_st[:, F + 3:F + 4],
            AOT.mult, AOT.mult), deps=[v5])
        ckpt(5)
        ln1a = ph.add("vector", lambda e, F=F, tmp1=tmp1: e.tensor_scalar(
            tmp1, tmp1, s_st[:, F + 3:F + 4], None, AOT.mult),
            deps=[v6, sq])
        ln1 = ph.add("vector", lambda e, F=F, tmp1=tmp1: e.tensor_scalar(
            tmp1, tmp1, s_st[:, F + 4:F + 5], None, AOT.add), deps=[ln1a])
        ln1_ops[T] = ln1
        ckpt(6)
        g1 = ph.add("vector", lambda e, tmp1=tmp1: e.tensor_tensor(
            tmp1, tmp1, gamb, AOT.mult),
            deps=[ln1] + (gbe if T == 0 else []))
        b1 = ph.add("vector", lambda e, tmp1=tmp1: e.tensor_tensor(
            tmp1, tmp1, betb, AOT.add), deps=[g1])
        s2 = ph.add("vector", lambda e, B=B, tmp1=tmp1: e.tensor_tensor(
            tmp1, tmp1, bap(B, 0, 1024), AOT.add), deps=[b1, tpX])
        s2_ops[T] = s2
        ckpt(8)
        rs2 = ph.add("vector", lambda e, F=F, tmp1=tmp1: e.reduce_sum(
            s_st[:, F + 5:F + 6], tmp1, axis=AXL.X), deps=[s2])
        sq2 = ph.add("scalar", lambda e, F=F, tmp1=tmp1, x1sl=x1sl:
                     e.activation(x1sl, tmp1, ACTF.Square,
                                  accum_out=s_st[:, F + 6:F + 7]),
                     deps=[s2]
                     + ([tpb_ops[T - 2], dmax_ops[T - 2]] if T >= 2 else []))
        mu2 = ph.add("vector", lambda e, F=F: e.tensor_scalar(
            s_st[:, F + 5:F + 7], s_st[:, F + 5:F + 7], 1.0 / C, None,
            AOT.mult), deps=[rs2, sq2])
        w2_ = ph.add("vector", lambda e, F=F: e.tensor_scalar(
            s_st[:, F + 7:F + 8], s_st[:, F + 5:F + 6], s_st[:, F + 5:F + 6],
            s_st[:, F + 6:F + 7], AOT.mult, AOT.subtract), deps=[mu2])
        w3_ = ph.add("vector", lambda e, F=F: e.tensor_scalar(
            s_st[:, F + 7:F + 8], s_st[:, F + 7:F + 8], -1.0, LN_EPS,
            AOT.mult, AOT.add), deps=[w2_])
        w4_ = ph.add("vector", lambda e, F=F: e.reciprocal(
            s_st[:, F + 7:F + 8], s_st[:, F + 7:F + 8]), deps=[w3_])
        w5_ = ph.add("scalar", lambda e, F=F: e.activation(
            s_st[:, F + 8:F + 9], s_st[:, F + 7:F + 8], ACTF.Sqrt), deps=[w4_])
        w6_ = ph.add("vector", lambda e, F=F: e.tensor_scalar(
            s_st[:, F + 9:F + 10], s_st[:, F + 5:F + 6], -1.0,
            s_st[:, F + 8:F + 9], AOT.mult, AOT.mult), deps=[w5_])
        x1o = ph.add("vector", lambda e, F=F, tmp1=tmp1, x1sl=x1sl:
                     e.tensor_scalar(x1sl, tmp1,
                                     s_st[:, F + 8:F + 9],
                                     s_st[:, F + 9:F + 10],
                                     AOT.mult, AOT.add), deps=[w6_, sq2])
        x1_ops[T] = x1o
        ckpt(10)
        dmx = ph.add("sync", lambda e, T=T, x1sl=x1sl: e.dma_start(
            x1row.ap()[128 * T:128 * (T + 1), :], x1sl),
            deps=[x1o], dma=True)
        dmax_ops[T] = dmx
        xrow_d.append(dmx)
        if T >= 3 and (T + 1) % 4 == 0 and T >= 4:
            pass
    if _P4N < 16:
        raise _StopEmit(nc, dbgref[0], sems, SCHED + list(ph.ops))

    def mtb15(e):
        for k in range(NCH):
            ins = e.transpose(bap(pS2b, k * 128, 128),
                              s_x1t[:, 1, k * 128:(k + 1) * 128], IDENT)
        return ins
    tpb15 = ph.add("tensor", mtb15, deps=[x1_ops[15], bke_ops[13]])
    bke15 = ph.add("scalar", lambda e: e.activation(
        bass.AP(s_r1, 128 * 15, [[NCH * PTOK, 128], [PTOK, NCH], [1, 128]]),
        bap(pS2b, 0, 1024).rearrange("p (k c) -> p k c", k=NCH),
        ACTF.Copy), deps=[tpb15])
    bke_ops[15] = bke15

    for j in range(4):
        bank = pf[4 + (j % 2)]

        def mks(e, j=j, bank=bank):
            for k in range(NCH):
                ins = e.matmul(bank[0:8, 0:512], s_wgtb[:, k * 8:(k + 1) * 8],
                               rap(k, j), start=(k == 0), stop=(k == NCH - 1))
            return ins
        mm = ph.add("tensor", mks,
                    deps=[bke_ops[4 * j + 3], prevgb[4 + (j % 2)],
                          sc_ev[j - 2] if j >= 2 else None])
        ev = ph.add("vector", lambda e, bank=bank: e.tensor_copy(
            s_scst[0:8, 0:512], bank[0:8, 0:512]), deps=[mm, prev_scst])
        dsc = ph.add("sync", lambda e, j=j: e.dma_start(
            sc_in.ap()[:, 512 * j:512 * (j + 1)], s_scst[0:8, 0:512]),
            deps=[ev], dma=True)
        prev_scst = dsc
        sc_ev.append(dsc)
    d_sc = sc_ev[-1]
    cc_sc = ph.add("gpsimd", lambda e: e.collective_compute(
        "AllToAll", AOT.bypass, replica_groups=RG8,
        ins=[sc_in.ap()], outs=[sc_out.ap()]), deps=[d_sc], coll=True)
    cc_x1 = ph.add("gpsimd", lambda e: e.collective_compute(
        "AllGather", AOT.bypass, replica_groups=RG8,
        ins=[x1row.ap()], outs=[x1all.ap()[0:NTOK, :]]),
        deps=xrow_d + [cc_sc], coll=True)
    if DEBUG:
        ph.add("gpsimd", lambda e: e.dma_start(
            dbg["d_x1"].ap().rearrange("(k p) w -> p k w", p=128),
            s_r1.ap()), deps=[bke_ops[15], cc_x1], dma=True)
    run(ph)

    stPa.close()
    stB3.close()
    stB.close()
    stA.close()
    stZ.close()

    stC = ExitStack()
    s_wsl = SB(stC, "s_wsl", [128, 16], F32)
    s_gidx = SB(stC, "s_gidx", [128, 16], U32)
    s_ydst = SB(stC, "s_ydst", [128, 16], U32)
    s_sdst = SB(stC, "s_sdst", [128, 24], U32)
    s_lidf = SB(stC, "s_lidf", [128, 16], F32)

    stW = ExitStack()
    s_w1 = SB(stW, "s_w1", [128, NCH, HF])
    s_w3 = SB(stW, "s_w3", [128, NCH, HF])
    s_w2 = SB(stW, "s_w2", [128, 16, C])

    stC2 = ExitStack()
    s_wtok = SB(stC2, "s_wtok", [128, 128], F32)
    s_cmp = SB(stC2, "s_cmp", [128, 128], F32)
    s_rl = SB(stC2, "s_rl", [128, 128], F32)
    s_d2 = SB(stC2, "s_d2", [128, 128], F32)
    s_O = SB(stC2, "s_O", [128, 8], F32)
    s_B = SB(stC2, "s_B", [128, 8], F32)
    s_csel = SB(stC2, "s_csel", [128, 128], F32)
    s_esel = SB(stC2, "s_esel", [128, 128], F32)
    s_tokf = SB(stC2, "s_tokf", [128, 16], F32)
    stC2b = ExitStack()
    s_smine = SB(stC2b, "s_smine", [128, 128], F32)
    s_mask = SB(stC2b, "s_mask", [128, 128], F32)
    s_sml = SB(stC2b, "s_sml", [128, 16], F32)
    s_sb1 = SB(stC2b, "s_sb1", [1, 16], F32)
    s_cs = SB(stC2b, "s_cs", [128, 1], F32)
    s_cnt = SB(stC2b, "s_cnt", [128, 8], F32)
    s_P1 = SB(stC2b, "s_P1", [128, 128], F32)
    s_B2 = SB(stC2b, "s_B2", [128, 4], F32)
    s_row = SB(stC2b, "s_row", [1, 128], F32)

    stPb = ExitStack()
    qf = [PS(stPb, f"qf{i}", [128, 512], F32) for i in range(4)]

    FZ5 = fence(excl=("gpsimd",))
    FZ5c = fence(excl=("gpsimd", "sync"))
    ph = phase()
    wl = []
    for dst, wsrc in [(s_w1, w1td), (s_w3, w3td)]:
        wl.append(ph.add("gpsimd", lambda e, d=dst, s=wsrc: e.dma_start(
            d.ap(),
            s.ap().rearrange("(k p) h -> p k h", p=128)),
            deps=FZ5 + [cc_x1], dma=True))
    wl.append(ph.add("gpsimd", lambda e: e.dma_start(
        s_w2.ap(),
        w2td.ap().rearrange("(k p) c -> p k c", p=128)), dma=True))

    ls = ph.add("scalar", lambda e: e.dma_start(s_smine[:, :], sc_out.ap()),
                deps=[cc_sc] + FZ5c, dma=True)
    r1_ = ph.add("vector", lambda e: e.reduce_max(
        s_sml[:, 0:1], s_smine[:, :], axis=AXL.X), deps=[ls])
    r2_ = ph.add("vector", lambda e: e.tensor_reduce(
        s_sml[:, 1:2], s_smine[:, :], axis=AXL.X, op=AOT.min), deps=[ls])
    tr_ = ph.add("tensor", lambda e: e.transpose(
        qf[0][0:1, 0:128], s_sml[:, 0:1], IDENTF), deps=[r1_])
    tr2_ = ph.add("tensor", lambda e: e.transpose(
        qf[1][0:1, 0:128], s_sml[:, 1:2], IDENTF), deps=[r2_])
    mx_ = ph.add("vector", lambda e: e.reduce_max(
        qf[0][0:1, 128:129], qf[0][0:1, 0:128], axis=AXL.X), deps=[tr_])
    mn_ = ph.add("vector", lambda e: e.tensor_reduce(
        qf[1][0:1, 128:129], qf[1][0:1, 0:128], axis=AXL.X, op=AOT.min),
        deps=[tr2_])
    cpa = ph.add("vector", lambda e: e.tensor_copy(s_sb1[0:1, 0:1],
                                                   qf[0][0:1, 128:129]),
                 deps=[mx_])
    cpb = ph.add("vector", lambda e: e.tensor_copy(s_sb1[0:1, 1:2],
                                                   qf[1][0:1, 128:129]),
                 deps=[mn_])
    bc_ = ph.add("tensor", lambda e: e.matmul(
        qf[1][:, 0:2], ones1r, s_sb1[0:1, 0:2], start=True, stop=True),
        deps=[cpa, cpb])
    lo1 = ph.add("vector", lambda e: e.tensor_copy(s_sml[:, 3:4],
                                                   qf[1][:, 0:1]), deps=[bc_])
    lo2 = ph.add("vector", lambda e: e.tensor_copy(s_sml[:, 2:3],
                                                   qf[1][:, 1:2]), deps=[bc_])
    ng = ph.add("vector", lambda e: e.tensor_scalar(
        s_sml[:, 7:8], s_sml[:, 3:4], -1.0, None, AOT.mult), deps=[lo1])
    cz0 = ph.add("vector", lambda e: e.memset(s_cnt[:, 0:1], 0.0), deps=FZ5)
    st1 = ph.add("vector", lambda e: e.tensor_tensor(
        s_sml[:, 4:5], s_sml[:, 3:4], s_sml[:, 2:3], AOT.subtract),
        deps=[lo1, lo2])
    st2 = ph.add("vector", lambda e: e.tensor_scalar(
        s_sml[:, 4:5], s_sml[:, 4:5], 0.125, None, AOT.mult), deps=[st1])
    st3 = ph.add("vector", lambda e: e.reciprocal(
        s_sml[:, 5:6], s_sml[:, 4:5]), deps=[st2])
    dep = st3
    su2 = None
    for rnd in range(7):
        uu = ph.add("vector", lambda e: e.tensor_scalar(
            s_cmp[:, :], s_smine[:, :], s_sml[:, 2:3], s_sml[:, 5:6],
            AOT.subtract, AOT.mult), deps=[dep, su2])
        cs = []
        for jj in range(1, 8):
            cj = ph.add("vector", lambda e, jj=jj: e.tensor_scalar(
                s_esel[:, :], s_cmp[:, :], float(jj), None, AOT.is_ge,
                op1=AOT.add, accum_out=s_cnt[:, jj:jj + 1]), deps=[uu])
            cs.append(cj)
        smm = ph.add("tensor", lambda e: e.matmul(
            qf[2][0:8, 0:1], s_cnt[:, :], ones_f, start=True, stop=True),
            deps=cs + [cz0])
        sel = ph.add("vector", lambda e: e.tensor_scalar(
            s_O[0:8, 5:6], qf[2][0:8, 0:1], float(NSLOT) - 0.5, None,
            AOT.is_ge), deps=[smm])
        lmm = ph.add("tensor", lambda e: e.matmul(
            qf[2][0:1, 2:3], s_O[0:8, 5:6], ones_f[0:8, 0:1],
            start=True, stop=True), deps=[sel])
        lcp = ph.add("vector", lambda e: e.tensor_copy(
            s_sb1[0:1, 2:3], qf[2][0:1, 2:3]), deps=[lmm])
        lbc = ph.add("tensor", lambda e: e.matmul(
            qf[3][:, 0:1], ones1r, s_sb1[0:1, 2:3], start=True, stop=True),
            deps=[lcp])
        lev = ph.add("vector", lambda e: e.tensor_copy(
            s_sml[:, 8:9], qf[3][:, 0:1]), deps=[lbc])
        l1 = ph.add("vector", lambda e: e.tensor_tensor(
            s_sml[:, 8:9], s_sml[:, 8:9], s_sml[:, 4:5], AOT.mult), deps=[lev])
        l2 = ph.add("vector", lambda e: e.tensor_tensor(
            s_sml[:, 2:3], s_sml[:, 2:3], s_sml[:, 8:9], AOT.add), deps=[l1])
        dep = l2
        if rnd < 6:
            ph.add("vector", lambda e: e.tensor_scalar(
                s_sml[:, 4:5], s_sml[:, 4:5], 0.125, None, AOT.mult),
                deps=[l1])
            su2 = ph.add("vector", lambda e: e.tensor_scalar(
                s_sml[:, 5:6], s_sml[:, 5:6], 8.0, None, AOT.mult),
                deps=[uu])
    tm2 = dep
    eLO = ph.add("vector", lambda e: e.tensor_tensor(
        s_sml[:, 11:12], s_sml[:, 2:3], s_sml[:, 4:5], AOT.subtract),
        deps=[dep])
    eHI = ph.add("vector", lambda e: e.tensor_scalar(
        s_sml[:, 12:13], s_sml[:, 4:5], 2.0, s_sml[:, 2:3], AOT.mult,
        AOT.add), deps=[dep])
    mk_ = ph.add("vector", lambda e: e.tensor_scalar(
        s_mask[:, :], s_smine[:, :], s_sml[:, 12:13], None, AOT.is_ge),
        deps=[eHI])
    cnd1 = ph.add("vector", lambda e: e.tensor_scalar(
        s_cmp[:, :], s_smine[:, :], s_sml[:, 11:12], None, AOT.is_ge),
        deps=[eLO])
    cnd2 = ph.add("vector", lambda e: e.tensor_tensor(
        s_cmp[:, :], s_cmp[:, :], s_mask[:, :], AOT.subtract),
        deps=[cnd1, mk_])
    chi = ph.add("vector", lambda e: e.tensor_scalar(
        s_esel[:, :], s_mask[:, :], 0.5, None, AOT.is_ge, op1=AOT.add,
        accum_out=s_cnt[:, 1:2]), deps=[mk_, dep])
    qct = ph.add("vector", lambda e: e.tensor_scalar(
        s_esel[:, :], s_cmp[:, :], 0.5, None, AOT.is_ge, op1=AOT.add,
        accum_out=s_cnt[:, 2:3]), deps=[cnd2, chi])
    chm = ph.add("tensor", lambda e: e.matmul(
        qf[2][0:1, 3:4], s_cnt[:, 1:2], ones_f, start=True, stop=True),
        deps=[chi])
    qcm = ph.add("tensor", lambda e: e.matmul(
        qf[2][0:1, 4:5], s_cnt[:, 2:3], ones_f, start=True, stop=True),
        deps=[qct])
    chc = ph.add("vector", lambda e: e.tensor_copy(
        s_sb1[0:1, 4:5], qf[2][0:1, 3:4]), deps=[chm])
    qcc = ph.add("vector", lambda e: e.tensor_copy(
        s_sb1[0:1, 5:6], qf[2][0:1, 4:5]), deps=[qcm])
    cbc = ph.add("tensor", lambda e: e.matmul(
        qf[3][:, 2:4], ones1r, s_sb1[0:1, 4:6], start=True, stop=True),
        deps=[chc, qcc])
    cbe = ph.add("vector", lambda e: e.tensor_copy(
        s_sml[:, 13:15], qf[3][:, 2:4]), deps=[cbc])
    xm1 = ph.add("tensor", lambda e: e.transpose(
        qf[0][:, 0:128], s_cmp[:, :], IDENTF), deps=[cnd2, qct])
    xm2 = ph.add("vector", lambda e: e.tensor_copy(s_cmp[:, :],
                                                   qf[0][:, 0:128]),
                 deps=[xm1])
    xp1 = ph.add("tensor", lambda e: e.matmul(
        qf[1][:, 0:128], LTRIF, s_cmp[:, :], start=True, stop=True),
        deps=[xm2])
    xp2 = ph.add("vector", lambda e: e.tensor_copy(s_P1[:, :], qf[1][:, 0:128]),
                 deps=[xp1])
    xr1 = ph.add("tensor", lambda e: e.transpose(
        qf[2][:, 0:128], s_P1[:, :], IDENTF), deps=[xp2, chc, qcc])
    xr2 = ph.add("vector", lambda e: e.tensor_copy(s_rl[:, :], qf[2][:, 0:128]),
                 deps=[xr1])
    xt1 = ph.add("tensor", lambda e: e.matmul(
        qf[3][0:1, 0:128], e127, s_P1[:, :], start=True, stop=True),
        deps=[xp2, cbe])
    xt2 = ph.add("vector", lambda e: e.tensor_copy(s_row[0:1, :],
                                                   qf[3][0:1, 0:128]),
                 deps=[xt1])
    xu1 = ph.add("tensor", lambda e: e.matmul(
        qf[3][:, 256:257], s_row[0:1, :], ones_f[0:1, 0:1], start=True,
        stop=True), deps=[xt2])
    xu2 = ph.add("vector", lambda e: e.tensor_copy(s_O[:, 2:3],
                                                   qf[3][:, 256:257]),
                 deps=[xu1])
    xo1 = ph.add("tensor", lambda e: e.matmul(
        qf[3][:, 3:4], SLTRIF, s_O[:, 2:3], start=True, stop=True),
        deps=[xu2])
    xo2 = ph.add("vector", lambda e: e.tensor_copy(s_O[:, 0:1], qf[3][:, 3:4]),
                 deps=[xo1])
    xoc = ph.add("vector", lambda e: e.tensor_copy(s_O[:, 2:3], piota),
                 deps=[xo1, xu2])
    xc1 = ph.add("vector", lambda e: e.tensor_scalar(
        s_cmp[:, :], KONST, 0.0, s_O[:, 0:1], AOT.add, AOT.is_ge),
        deps=[xo2, xm2])
    xd1 = ph.add("tensor", lambda e: e.matmul(
        qf[3][:, 128:256], DMAT, s_cmp[:, :], start=True, stop=True),
        deps=[xc1])
    xd2 = ph.add("vector", lambda e: e.tensor_copy(s_d2[:, :],
                                                   qf[3][:, 128:256]),
                 deps=[xd1])
    xb1 = ph.add("tensor", lambda e: e.matmul(
        qf[0][:, 0:4], s_d2[:, :], s_O[:, 0:4], start=True, stop=True),
        deps=[xd2, xoc])
    xb2 = ph.add("vector", lambda e: e.tensor_copy(s_B[:, 0:4], qf[0][:, 0:4]),
                 deps=[xb1])
    xs1 = ph.add("tensor", lambda e: e.matmul(
        qf[1][:, 0:128], s_d2[:, :], s_rl[:, :], start=True, stop=True),
        deps=[xd2, xr2, xp1])
    xq1 = ph.add("vector", lambda e: e.tensor_scalar(
        s_B[:, 4:5], piota, 0.0, s_B[:, 0:1], AOT.add, AOT.subtract),
        deps=[xb2])
    xcs = ph.add("vector", lambda e: e.tensor_scalar(
        s_csel[:, :], qf[1][:, 0:128], s_B[:, 4:5], None, AOT.is_le),
        deps=[xs1, xq1])
    xgg = ph.add("vector", lambda e: e.reduce_sum(
        s_B[:, 5:6], s_csel[:, :], axis=AXL.X), deps=[xcs])
    xtk = ph.add("vector", lambda e: e.tensor_scalar(
        s_B2[:, 0:1], s_B[:, 2:3], 128.0, s_B[:, 5:6], AOT.mult, AOT.add),
        deps=[xgg])
    xcl = ph.add("vector", lambda e: e.tensor_scalar(
        s_B2[:, 0:1], s_B2[:, 0:1], float(NTOK - 1), None, AOT.min),
        deps=[xtk])
    xpb = ph.add("vector", lambda e: e.tensor_copy(s_B2[:, 1:2], s_B[:, 2:3]),
                 deps=[xtk])
    xpg = ph.add("vector", lambda e: e.tensor_copy(s_B2[:, 2:3], s_B[:, 5:6]),
                 deps=[xtk])
    xhf0 = ph.add("vector", lambda e: e.tensor_scalar(
        s_csel[:, :], KONST, s_B2[:, 2:3], None, AOT.is_equal),
        deps=[xpg, xcs])
    xsc1 = ph.add("tensor", lambda e: e.matmul(
        qf[2][:, 128:256], s_d2[:, :], s_smine[:, :], start=True, stop=True),
        deps=[xd2, ls, chc, qcc])
    xsc2 = ph.add("vector", lambda e: e.tensor_copy(
        s_cmp[:, :], qf[2][:, 128:256]), deps=[xsc1, xd1])
    xsc3 = ph.add("vector", lambda e: e.tensor_tensor(
        s_cmp[:, :], s_cmp[:, :], s_csel[:, :], AOT.mult),
        deps=[xsc2, xhf0])
    xsc4 = ph.add("vector", lambda e: e.reduce_sum(
        s_cs[:, 0:1], s_cmp[:, :], axis=AXL.X), deps=[xsc3])
    xv1 = ph.add("tensor", lambda e: e.transpose(
        qf[0][0:1, 0:128], s_cs[:, 0:1], IDENTF), deps=[xsc4, xb1])
    xv2 = ph.add("vector", lambda e: e.tensor_copy(s_row[0:1, :],
                                                   qf[0][0:1, 0:128]),
                 deps=[xv1, xu1])
    xw1 = ph.add("tensor", lambda e: e.matmul(
        qf[1][:, 0:128], ones1r, s_row[0:1, :], start=True, stop=True),
        deps=[xv2, xcs])
    xw2 = ph.add("vector", lambda e: e.tensor_copy(s_esel[:, :],
                                                   qf[1][:, 0:128]),
                 deps=[xw1, qct])
    xv3 = ph.add("tensor", lambda e: e.transpose(
        qf[2][0:1, 0:128], s_B2[:, 0:1], IDENTF), deps=[xcl, xw1])
    xv4 = ph.add("vector", lambda e: e.tensor_copy(s_row[0:1, :],
                                                   qf[2][0:1, 0:128]),
                 deps=[xv3, xw1])
    xw3 = ph.add("tensor", lambda e: e.matmul(
        qf[3][:, 0:128], ones1r, s_row[0:1, :], start=True, stop=True),
        deps=[xv4, xt1])
    xw4 = ph.add("vector", lambda e: e.tensor_copy(s_P1[:, :], qf[3][:, 0:128]),
                 deps=[xw3, xs1])
    xgt = ph.add("vector", lambda e: e.tensor_scalar(
        s_csel[:, :], s_esel[:, :], s_cs[:, 0:1], None, AOT.is_gt),
        deps=[xw2, xcs])
    xeq = ph.add("vector", lambda e: e.tensor_scalar(
        s_cmp[:, :], s_esel[:, :], s_cs[:, 0:1], None, AOT.is_equal),
        deps=[xw2, xsc4])
    xil = ph.add("vector", lambda e: e.tensor_scalar(
        s_rl[:, :], s_P1[:, :], s_B2[:, 0:1], None, AOT.is_lt),
        deps=[xw4, xs1])
    xti = ph.add("vector", lambda e: e.tensor_tensor(
        s_cmp[:, :], s_cmp[:, :], s_rl[:, :], AOT.mult), deps=[xeq, xil])
    xr_ = ph.add("vector", lambda e: e.tensor_tensor(
        s_csel[:, :], s_csel[:, :], s_cmp[:, :], AOT.add), deps=[xgt, xti])
    xvr = ph.add("vector", lambda e: e.tensor_scalar(
        s_rl[:, :], KONST, s_sml[:, 14:15], None, AOT.is_lt),
        deps=[cbe, xti])
    xrm = ph.add("vector", lambda e: e.tensor_tensor(
        s_csel[:, :], s_csel[:, :], s_rl[:, :], AOT.mult), deps=[xr_, xvr])
    xrr = ph.add("vector", lambda e: e.reduce_sum(
        s_B[:, 0:1], s_csel[:, :], axis=AXL.X), deps=[xrm])
    xkp = ph.add("vector", lambda e: e.tensor_scalar(
        s_B[:, 1:2], s_B[:, 0:1], s_sml[:, 13:14], float(NSLOT) - 0.5,
        AOT.add, AOT.is_le), deps=[xrr, cbe])
    xvp = ph.add("vector", lambda e: e.tensor_scalar(
        s_B[:, 2:3], piota, s_sml[:, 14:15], None, AOT.is_lt),
        deps=[cbe, xb2])
    xkf = ph.add("vector", lambda e: e.tensor_tensor(
        s_B2[:, 3:4], s_B[:, 1:2], s_B[:, 2:3], AOT.mult), deps=[xkp, xvp])
    xhp = ph.add("vector", lambda e: e.tensor_scalar(
        s_esel[:, :], KONST, s_B2[:, 1:2], None, AOT.is_equal),
        deps=[xpb, xgt, xeq])
    xhf = ph.add("vector", lambda e: e.tensor_scalar(
        s_csel[:, :], KONST, s_B2[:, 2:3], None, AOT.is_equal),
        deps=[xpg, xrr])
    xhk = ph.add("vector", lambda e: e.tensor_scalar(
        s_esel[:, :], s_esel[:, :], s_B2[:, 3:4], None, AOT.mult),
        deps=[xhp, xkf])
    xum = ph.add("tensor", lambda e: e.matmul(
        qf[2][:, 0:128], s_esel[:, :], s_csel[:, :], start=True, stop=True),
        deps=[xhk, xhf, xv3])
    xue = ph.add("vector", lambda e: e.tensor_copy(s_d2[:, :],
                                                   qf[2][:, 0:128]),
                 deps=[xum])
    bet_ = ph.add("vector", lambda e: e.tensor_scalar(
        s_sml[:, 9:10], s_sml[:, 14:15], 128.5, None, AOT.is_le), deps=[cbe])
    nb_ = ph.add("vector", lambda e: e.tensor_scalar(
        s_sml[:, 5:6], s_sml[:, 9:10], -1.0, 1.0, AOT.mult, AOT.add),
        deps=[bet_])
    avl_ = ph.add("vector", lambda e: e.tensor_scalar(
        s_sml[:, 10:11], s_sml[:, 13:14], -1.0, float(NSLOT), AOT.mult,
        AOT.add), deps=[cbe])
    icn1 = ph.add("vector", lambda e: e.tensor_scalar(
        s_rl[:, :], s_smine[:, :], s_sml[:, 11:12], None, AOT.is_ge),
        deps=[eLO, ls, xrm])
    icn2 = ph.add("vector", lambda e: e.tensor_tensor(
        s_rl[:, :], s_rl[:, :], s_mask[:, :], AOT.subtract),
        deps=[icn1, mk_])
    pio_ = ph.add("vector", lambda e: e.tensor_scalar(
        s_sml[:, 8:9], piota, 128.0, None, AOT.mult), deps=[dep])
    itx1 = ph.add("vector", lambda e: e.tensor_scalar(
        s_P1[:, :], KONST, s_sml[:, 8:9], None, AOT.add),
        deps=[pio_, xil])
    itx2 = ph.add("vector", lambda e: e.tensor_tensor(
        s_P1[:, :], s_P1[:, :], s_rl[:, :], AOT.mult), deps=[itx1, icn2])
    itx3 = ph.add("vector", lambda e: e.tensor_scalar(
        s_cmp[:, :], s_rl[:, :], -1.0e6, 1.0e6, AOT.mult, AOT.add),
        deps=[icn2, xeq])
    itx4 = ph.add("vector", lambda e: e.tensor_tensor(
        s_P1[:, :], s_P1[:, :], s_cmp[:, :], AOT.add), deps=[itx2, itx3])
    iz_ = ph.add("vector", lambda e: e.tensor_scalar(
        s_sml[:, 6:7], piota, 0.0, None, AOT.mult), deps=[dep])
    dep2 = iz_
    prevu = xue
    for istep in (2048.0, 256.0, 32.0, 4.0, 0.5):
        uu2 = ph.add("vector", lambda e, istep=istep: e.tensor_scalar(
            s_esel[:, :], s_P1[:, :], s_sml[:, 6:7], 1.0 / istep,
            AOT.subtract, AOT.mult), deps=[itx4, dep2, xue])
        cs2 = []
        for jj in range(1, 8):
            cj2 = ph.add("vector", lambda e, jj=jj: e.tensor_scalar(
                s_csel[:, :], s_esel[:, :], float(jj), None, AOT.is_lt,
                op1=AOT.add, accum_out=s_cnt[:, jj:jj + 1]),
                deps=[uu2] + ([xue] if jj == 1 else []))
            cs2.append(cj2)
        sm2 = ph.add("tensor", lambda e: e.matmul(
            qf[2][0:8, 0:1], s_cnt[:, :], ones_f, start=True, stop=True),
            deps=cs2 + [cz0, prevu])
        se2 = ph.add("vector", lambda e: e.tensor_scalar(
            s_O[0:8, 5:6], qf[2][0:8, 0:1], s_sml[0:8, 10:11], None,
            AOT.is_le), deps=[sm2, avl_])
        lm2 = ph.add("tensor", lambda e: e.matmul(
            qf[2][0:1, 2:3], s_O[0:8, 5:6], ones_f[0:8, 0:1],
            start=True, stop=True), deps=[se2])
        lc2 = ph.add("vector", lambda e: e.tensor_copy(
            s_sb1[0:1, 2:3], qf[2][0:1, 2:3]), deps=[lm2])
        lb2 = ph.add("tensor", lambda e: e.matmul(
            qf[3][:, 0:1], ones1r, s_sb1[0:1, 2:3], start=True, stop=True),
            deps=[lc2])
        le2 = ph.add("vector", lambda e: e.tensor_copy(
            s_sml[:, 4:5], qf[3][:, 0:1]), deps=[lb2])
        up2 = ph.add("vector", lambda e, istep=istep: e.tensor_scalar(
            s_sml[:, 6:7], s_sml[:, 4:5], istep, s_sml[:, 6:7], AOT.mult,
            AOT.add), deps=[le2])
        dep2 = up2
        prevu = lc2
    imask = ph.add("vector", lambda e: e.tensor_scalar(
        s_esel[:, :], s_P1[:, :], s_sml[:, 6:7], 0.0, AOT.subtract,
        AOT.is_lt), deps=[dep2, itx4])
    bl1 = ph.add("vector", lambda e: e.tensor_scalar(
        s_d2[:, :], s_d2[:, :], s_sml[:, 9:10], None, AOT.mult),
        deps=[xue, bet_])
    bl2 = ph.add("vector", lambda e: e.tensor_scalar(
        s_esel[:, :], s_esel[:, :], s_sml[:, 5:6], None, AOT.mult),
        deps=[imask, nb_])
    bl3 = ph.add("vector", lambda e: e.tensor_tensor(
        s_d2[:, :], s_d2[:, :], s_esel[:, :], AOT.add), deps=[bl1, bl2])
    mkup = ph.add("vector", lambda e: e.tensor_tensor(
        s_mask[:, :], s_mask[:, :], s_d2[:, :], AOT.add), deps=[bl3, cnd2])

    ex_ = ph.add("scalar", lambda e: e.activation(
        s_wtok[:, :], s_smine[:, :], ACTF.Exp, bias=s_sml[:, 7:8]),
        deps=[ng, ls])
    wm_ = ph.add("vector", lambda e: e.tensor_tensor(
        s_wtok[:, :], s_wtok[:, :], s_mask[:, :], AOT.mult), deps=[ex_, mkup])
    zr = ph.add("vector", lambda e: e.reduce_sum(
        s_sml[:, 9:10], s_wtok[:, :], axis=AXL.X), deps=[wm_])
    ztr = ph.add("tensor", lambda e: e.transpose(
        qf[0][0:1, 256:384], s_sml[:, 9:10], IDENTF), deps=[zr])
    zs = ph.add("vector", lambda e: e.reduce_sum(
        qf[0][0:1, 384:385], qf[0][0:1, 256:384], axis=AXL.X), deps=[ztr])
    zc = ph.add("vector", lambda e: e.tensor_copy(s_sb1[0:1, 3:4],
                                                  qf[0][0:1, 384:385]),
                deps=[zs])
    zre = ph.add("vector", lambda e: e.reciprocal(s_sb1[0:1, 3:4],
                                                  s_sb1[0:1, 3:4]), deps=[zc])
    zbc = ph.add("tensor", lambda e: e.matmul(
        qf[1][:, 4:5], ones1r, s_sb1[0:1, 3:4], start=True, stop=True),
        deps=[zre])
    zev = ph.add("vector", lambda e: e.tensor_copy(s_sml[:, 10:11],
                                                   qf[1][:, 4:5]), deps=[zbc])
    wt_ = ph.add("vector", lambda e: e.tensor_scalar(
        s_wtok[:, :], s_wtok[:, :], s_sml[:, 10:11], None, AOT.mult),
        deps=[zev, wm_])

    mtr = ph.add("tensor", lambda e: e.transpose(
        qf[0][:, 0:128], s_mask[:, :], IDENTF), deps=[mkup, xum])
    mte = ph.add("vector", lambda e: e.tensor_copy(s_cmp[:, :], qf[0][:, 0:128]),
                 deps=[mtr])
    p1m = ph.add("tensor", lambda e: e.matmul(
        qf[1][:, 0:128], LTRIF, s_cmp[:, :], start=True, stop=True),
        deps=[mte])
    p1e = ph.add("vector", lambda e: e.tensor_copy(s_P1[:, :], qf[1][:, 0:128]),
                 deps=[p1m])
    rlt = ph.add("tensor", lambda e: e.transpose(
        qf[2][:, 0:128], s_P1[:, :], IDENTF), deps=[p1e])
    rle = ph.add("vector", lambda e: e.tensor_copy(s_rl[:, :], qf[2][:, 0:128]),
                 deps=[rlt])
    trow = ph.add("tensor", lambda e: e.matmul(
        qf[3][0:1, 0:128], e127, s_P1[:, :], start=True, stop=True),
        deps=[p1e])
    trowe = ph.add("vector", lambda e: e.tensor_copy(s_row[0:1, :],
                                                     qf[3][0:1, 0:128]),
                   deps=[trow])
    tpm = ph.add("tensor", lambda e: e.matmul(
        qf[3][:, 256:257], s_row[0:1, :], ones_f[0:1, 0:1], start=True,
        stop=True), deps=[trowe])
    tpe = ph.add("vector", lambda e: e.tensor_copy(s_O[:, 2:3],
                                                   qf[3][:, 256:257]),
                 deps=[tpm])
    omm = ph.add("tensor", lambda e: e.matmul(
        qf[3][:, 3:4], SLTRIF, s_O[:, 2:3], start=True, stop=True), deps=[tpe])
    oev = ph.add("vector", lambda e: e.tensor_copy(s_O[:, 0:1], qf[3][:, 3:4]),
                 deps=[omm])
    o16 = ph.add("tensor", lambda e: e.matmul(
        qf[3][:, 4:5], G16F, s_O[:, 0:1], start=True, stop=True), deps=[oev])
    o16e = ph.add("vector", lambda e: e.tensor_copy(s_O[:, 1:2], qf[3][:, 4:5]),
                  deps=[o16])
    ocp = ph.add("vector", lambda e: e.tensor_copy(s_O[:, 2:3], piota),
                 deps=[omm])
    ocp2 = ph.add("vector", lambda e: e.tensor_copy(s_O[:, 3:4], ownidx),
                  deps=[omm])

    qprev = {0: None, 1: None, 2: None}
    prev_dread = [None]
    ydst_cp = []
    lid_ops = []
    gidx_cp = []
    for t in range(16):
        c1 = ph.add("vector", lambda e, t=t: e.tensor_scalar(
            s_cmp[:, :], KONST, float(128 * t), s_O[:, 0:1], AOT.add,
            AOT.is_ge), deps=[oev, ocp, p1m if t == 0 else None])
        dmm_ = ph.add("tensor", lambda e: e.matmul(
            qf[3][:, 128:256], DMAT, s_cmp[:, :], start=True, stop=True),
            deps=[c1, prev_dread[0]])
        d2_ = ph.add("vector", lambda e: e.tensor_copy(
            s_d2[:, :], qf[3][:, 128:256]), deps=[dmm_])
        prev_dread[0] = d2_
        bm = ph.add("tensor", lambda e: e.matmul(
            qf[0][:, 0:4], s_d2[:, :], s_O[:, 0:4], start=True, stop=True),
            deps=[d2_, o16e, ocp2, qprev[0]])
        be = ph.add("vector", lambda e: e.tensor_copy(s_B[:, 0:4],
                                                      qf[0][:, 0:4]), deps=[bm])
        qprev[0] = be
        rsm = ph.add("tensor", lambda e: e.matmul(
            qf[1][:, 0:128], s_d2[:, :], s_rl[:, :], start=True, stop=True),
            deps=[d2_, rle, qprev[1]])
        wsm = ph.add("tensor", lambda e: e.matmul(
            qf[2][:, 0:128], s_d2[:, :], s_wtok[:, :], start=True, stop=True),
            deps=[d2_, wt_, qprev[2]])
        q1 = ph.add("vector", lambda e, t=t: e.tensor_scalar(
            s_B[:, 4:5], piota, float(128 * t), s_B[:, 0:1], AOT.add,
            AOT.subtract), deps=[be])
        cse = ph.add("vector", lambda e: e.tensor_scalar(
            s_csel[:, :], qf[1][:, 0:128], s_B[:, 4:5], None, AOT.is_le),
            deps=[rsm, q1])
        qprev[1] = cse
        gg = ph.add("vector", lambda e: e.reduce_sum(
            s_B[:, 5:6], s_csel[:, :], axis=AXL.X), deps=[cse])
        tk = ph.add("vector", lambda e, t=t: e.tensor_scalar(
            s_tokf[:, t:t + 1], s_B[:, 2:3], 128.0, s_B[:, 5:6], AOT.mult,
            AOT.add), deps=[gg])
        e0 = ph.add("vector", lambda e: e.tensor_scalar(
            s_esel[:, 0:1], s_csel[:, 0:1], -1.0, 1.0, AOT.mult, AOT.add),
            deps=[cse])
        e1_ = ph.add("vector", lambda e: e.tensor_tensor(
            s_esel[:, 1:128], s_csel[:, 0:127], s_csel[:, 1:128], AOT.subtract),
            deps=[cse, e0])
        ws1 = ph.add("vector", lambda e: e.tensor_tensor(
            s_esel[:, :], s_esel[:, :], qf[2][:, 0:128], AOT.mult),
            deps=[e1_, wsm])
        qprev[2] = ws1
        ws2 = ph.add("vector", lambda e, t=t: e.reduce_sum(
            s_wsl[:, t:t + 1], s_esel[:, :], axis=AXL.X), deps=[ws1])
        rk = ph.add("vector", lambda e, t=t: e.tensor_scalar(
            s_B[:, 7:8], piota, float(128 * t), s_B[:, 1:2], AOT.add,
            AOT.subtract), deps=[be])
        yd = ph.add("vector", lambda e: e.tensor_scalar(
            s_B[:, 6:7], s_B[:, 3:4], float(YBLK), s_B[:, 7:8], AOT.mult,
            AOT.add), deps=[rk, tk])
        va = ph.add("vector", lambda e, t=t: e.tensor_scalar(
            s_B[:, 5:6], s_tokf[:, t:t + 1], float(NTOK) - 0.5, None,
            AOT.is_le), deps=[yd])
        vb = ph.add("vector", lambda e: e.tensor_scalar(
            s_B[:, 4:5], s_B[:, 7:8], float(YBLK) - 0.5, None, AOT.is_le),
            deps=[yd, cse])
        vv = ph.add("vector", lambda e: e.tensor_tensor(
            s_B[:, 5:6], s_B[:, 5:6], s_B[:, 4:5], AOT.mult), deps=[va, vb])
        ydm = ph.add("vector", lambda e: e.tensor_tensor(
            s_B[:, 6:7], s_B[:, 6:7], s_B[:, 5:6], AOT.mult), deps=[vv])
        nv = ph.add("vector", lambda e: e.tensor_scalar(
            s_B[:, 5:6], s_B[:, 5:6], -1.0, 1.0, AOT.mult, AOT.add),
            deps=[ydm])
        tr2 = ph.add("vector", lambda e: e.tensor_scalar(
            s_B[:, 4:5], piota, float(8 * YBLK), None, AOT.add), deps=[vb])
        trm = ph.add("vector", lambda e: e.tensor_tensor(
            s_B[:, 4:5], s_B[:, 4:5], s_B[:, 5:6], AOT.mult), deps=[nv, tr2])
        ydf = ph.add("vector", lambda e: e.tensor_tensor(
            s_B[:, 6:7], s_B[:, 6:7], s_B[:, 4:5], AOT.add), deps=[trm])
        ydcp = ph.add("vector", lambda e, t=t: e.tensor_copy(
            s_ydst[:, t:t + 1], s_B[:, 6:7]), deps=[ydf])
        ydst_cp.append(ydcp)
        lid = ph.add("vector", lambda e, t=t: e.tensor_scalar(
            s_lidf[:, t:t + 1], s_B[:, 3:4], -float(PTOK),
            s_tokf[:, t:t + 1], AOT.mult, AOT.add), deps=[yd, tk])
        lid_ops.append(lid)
        gcp = ph.add("vector", lambda e, t=t: e.tensor_copy(
            s_gidx[:, t:t + 1], s_tokf[:, t:t + 1]), deps=[tk])
        gidx_cp.append(gcp)

    if DEBUG:
        ph.add("gpsimd", lambda e: e.dma_start(
            dbg["d_thr"].ap(), s_sml[:, :]), deps=[tm2, cbe], dma=True)
        ph.add("gpsimd", lambda e: e.dma_start(
            dbg["d_mask"].ap(), s_mask[:, :]), deps=[mkup], dma=True)
        ph.add("gpsimd", lambda e: e.dma_start(
            dbg["d_sm"].ap(), s_smine[:, :]), deps=[mk_], dma=True)
    run(ph, auto_fence=False)

    if DEBUG:
        ph = phase()
        ph.add("gpsimd", lambda e: e.dma_start(
            dbg["d_ydst"].ap(), s_ydst[:, :]), dma=True)
        for nm, sb in [("d_tok", s_tokf), ("d_wsl", s_wsl)]:
            ph.add("gpsimd", lambda e, nm=nm, sb=sb: e.dma_start(
                dbg[nm].ap(), sb[:, :]), dma=True)
        run(ph)

    stC2b.close()
    stD = ExitStack()
    s_xg = SB(stD, "s_xg", [128, NCH, NSLOT])
    stD2 = ExitStack()
    s_xgr = SB(stD2, "s_xgr", [128, 2, 1024])
    stPc = ExitStack()
    pTb = [PS(stPc, f"pTb{i}", [128, 1024], BF16) for i in range(2)]

    CPL6 = fence(excl=("gpsimd", "sync", "scalar"))
    FZ6 = fence() if DEBUG else []
    ph = phase()
    gts = []
    prev_ev2 = None
    for t in range(16):
        par = t % 2
        g = ph.add("gpsimd", lambda e, t=t, par=par: e.indirect_dma_start(
            s_xgr[:, par, :], None, x1all.ap(),
            IOA(s_gidx[:, t:t + 1], 0)),
            deps=[cc_x1, gidx_cp[t]] + (FZ6 if t == 0 else [])
            + ([gts[t - 2][1]] if t >= 2 else []), dma=True)

        def mkt(e, par=par):
            for j in range(NCH):
                ins = e.transpose(pTb[par][:, j * 128:(j + 1) * 128],
                                  s_xgr[:, par, j * 128:(j + 1) * 128], IDENT)
            return ins
        tp = ph.add("tensor", mkt,
                    deps=[g] + ([gts[t - 2][2], gts[t - 2][1]] if t >= 2
                                else []))
        ev = ph.add("scalar", lambda e, t=t, par=par: e.activation(
            bass.AP(s_xg, 128 * t, [[NCH * NSLOT, 128], [NSLOT, NCH], [1, 128]]),
            pTb[par][:, :].rearrange("p (k c) -> p k c", k=NCH), ACTF.Copy),
            deps=[tp])
        gts.append((g, ev, tp))
    run(ph, auto_fence=False)
    stPc.close()
    stPb.close()
    if DEBUG:
        ph = phase()
        ph.add("gpsimd", lambda e: e.dma_start(
            dbg["d_xg"].ap().rearrange("p (k w) -> p k w", k=NCH), s_xg.ap()), dma=True)
        run(ph)

    stD2.close()

    stE = ExitStack()
    s_hh = SB(stE, "s_hh", [128, 16, NSLOT])
    s_sil = SB(stE, "s_sil", [128, 2, 512])
    s_y = SB(stE, "s_y", [128, 2, 1024])
    stPd = ExitStack()
    af = [PS(stPd, f"af{i}", [128, 512], F32) for i in range(4)]
    yf = [PS(stPd, f"yf{i}", [128, 512], F32) for i in range(4)]

    ph = phase()
    wdep = wl
    quads = [af, yf]
    evprev = {}
    hh_done = []
    gidx_n = 0
    for sh in range(2):
        for ht in range(16):
            qd = quads[(sh * 16 + ht) % 2]

            def mkA(e, sh=sh, ht=ht, qd=qd):
                for kc in range(NCH):
                    st_ = (kc == 0)
                    sp = (kc == NCH - 1)
                    for wi, wt in ((0, s_w1), (2, s_w3)):
                        for pc in range(2):
                            ins = e.matmul(
                                qd[wi + pc][:, 0:512],
                                bass.AP(wt, kc * HF + 128 * ht,
                                        [[NCH * HF, 128], [1, 128]]),
                                bass.AP(s_xg, kc * NSLOT + 1024 * sh + 512 * pc,
                                        [[NCH * NSLOT, 128], [1, 512]]),
                                start=st_, stop=sp)
                return ins
            gi = sh * 16 + ht
            deps = list(wdep) + list(CPL6) if gi == 0 else []
            deps += evprev.get(gi - 2, [])
            if gi == 0:
                deps.append(gts[7][1])
            if gi == 1:
                deps.append(gts[15][1])
            if gi == 16:
                deps.append(gts[15][1])
            mmA = ph.add("tensor", mkA, deps=deps)
            prev_hh = hh_done[-1] if hh_done else None
            evs_g = []
            for piece in range(2):
                sl = ph.add("scalar", lambda e, piece=piece, qd=qd: e.activation(
                    s_sil[:, piece, :], qd[piece][:, 0:512], ACTF.Silu),
                    deps=[mmA, prev_hh])
                hhop = ph.add("vector",
                              lambda e, sh=sh, ht=ht, piece=piece, qd=qd:
                              e.tensor_tensor(
                                  bass.AP(s_hh, ht * NSLOT + 1024 * sh + 512 * piece,
                                          [[16 * NSLOT, 128], [1, 512]]),
                                  s_sil[:, piece, :], qd[2 + piece][:, 0:512],
                                  AOT.mult),
                              deps=[sl])
                evs_g.append(hhop)
                prev_hh = hhop
            evprev[gi] = evs_g
            hh_done.append(evs_g[-1])
    idprev = None
    for t in range(16):
        isc = ph.add("gpsimd", lambda e, t=t: e.indirect_dma_start(
            idstage.ap(), IOA(s_ydst[:, t:t + 1], 0),
            s_lidf[:, t:t + 1], None),
            deps=[ydst_cp[t], lid_ops[t], idprev]
            + ([idpf] if t == 0 else []), dma=True)
        idprev = isc
    cc_id = ph.add("gpsimd", lambda e: e.collective_compute(
        "AllToAll", AOT.bypass, replica_groups=RG8,
        ins=[idstage.ap()[0:8 * YBLK, :]], outs=[idrecv.ap()]),
        deps=[idprev], coll=True)
    sdld = None
    for m in range(24):
        sdld = ph.add("gpsimd", lambda e, m=m: e.dma_start(
            s_sdst[:, m:m + 1],
            idrecv.ap()[128 * m:128 * (m + 1), :]),
            deps=[cc_id] if m == 0 else [], dma=True)
    yprev = [None, None, None, None]
    sc_done = []
    for st_ in range(16):
        qsel = (st_ % 2) * 2

        def mkB(e, st_=st_, qsel=qsel):
            for ht in range(16):
                first = (ht == 0)
                last = (ht == 15)
                e.matmul(yf[qsel][:, 0:512],
                         bass.AP(s_hh, ht * NSLOT + 128 * st_,
                                 [[16 * NSLOT, 128], [1, 128]]),
                         bass.AP(s_w2, ht * C, [[16 * C, 128], [1, 512]]),
                         start=first, stop=last)
                ins = e.matmul(yf[qsel + 1][:, 0:512],
                               bass.AP(s_hh, ht * NSLOT + 128 * st_,
                                       [[16 * NSLOT, 128], [1, 128]]),
                               bass.AP(s_w2, ht * C + 512, [[16 * C, 128], [1, 512]]),
                               start=first, stop=last)
            return ins
        deps = hh_done if st_ == 0 else [hh_done[-1]]
        deps = list(deps) + [yprev[qsel], yprev[qsel + 1]]
        mmB = ph.add("tensor", mkB, deps=deps)
        par = st_ % 2
        scdep = [sc_done[st_ - 2]] if st_ >= 2 else []
        ev1 = ph.add("vector", lambda e, st_=st_, par=par, qsel=qsel:
                     e.tensor_scalar(s_y[:, par, 0:512], yf[qsel][:, 0:512],
                                     s_wsl[:, st_:st_ + 1], None, AOT.mult),
                     deps=[mmB] + scdep)
        ev2 = ph.add("vector", lambda e, st_=st_, par=par, qsel=qsel:
                     e.tensor_scalar(s_y[:, par, 512:1024],
                                     yf[qsel + 1][:, 0:512],
                                     s_wsl[:, st_:st_ + 1], None, AOT.mult),
                     deps=[mmB] + scdep)
        yprev[qsel] = ev1
        yprev[qsel + 1] = ev2
        sc = ph.add("gpsimd", lambda e, st_=st_, par=par: e.indirect_dma_start(
            ystage.ap(), IOA(s_ydst[:, st_:st_ + 1], 0),
            s_y[:, par, :], None),
            deps=[ev1, ev2] + ([sc_done[st_ - 2]] if st_ >= 2 else []), dma=True)
        sc_done.append(sc)
    run(ph, auto_fence=False)
    if DEBUG:
        ph = phase()
        ph.add("gpsimd", lambda e: e.dma_start(
            dbg["d_hh"].ap(), bass.AP(s_hh, 0, [[16 * NSLOT, 128], [1, 2048]])),
            dma=True)
        ph.add("gpsimd", lambda e: e.dma_start(
            dbg["d_sd"].ap(), s_sdst[:, :]), deps=[sdld], dma=True)
        run(ph)

    ph = phase()
    cc_y = ph.add("gpsimd", lambda e: e.collective_compute(
        "AllToAll", AOT.bypass, replica_groups=RG8,
        ins=[ystage.ap()[0:8 * YBLK, :]], outs=[yrecv.ap()]),
        deps=sc_done, coll=True)
    run(ph)

    stPd.close()
    stE.close()

    stF = ExitStack()
    s_yr = SB(stF, "s_yr", [128, 24, 1024])
    ph = phase()
    lych = []
    for a in range(4):
        lych.append(ph.add("sync", lambda e, a=a: e.dma_start(
            s_yr[:, 6 * a:6 * (a + 1), :],
            yrecv.ap()[768 * a:768 * (a + 1), :].rearrange(
                "(m p) c -> p m c", p=128)),
            deps=[cc_y] if a == 0 else [], dma=True))
    prev_acc = [None, None]
    for m in range(24):
        Epar = (m // 3) % 2
        acc = moeacc[Epar]
        scm = ph.add("gpsimd", lambda e, m=m, acc=acc: e.indirect_dma_start(
            acc.ap(), IOA(s_sdst[:, m:m + 1], 0),
            s_yr[:, m, :], None, compute_op=AOT.add),
            deps=[lych[m // 6], prev_acc[Epar]] + ([sdld] if m == 0 else []),
            dma=True)
        prev_acc[Epar] = scm
    run(ph)
    if DEBUG:
        ph = phase()
        ph.add("gpsimd", lambda e: e.dma_start(
            dbg["d_yst"].ap(), ystage.ap()[0:8 * YBLK, :]), dma=True)
        ph.add("gpsimd", lambda e: e.dma_start(
            dbg["d_yrv"].ap(), yrecv.ap()), dma=True)
        ph.add("gpsimd", lambda e: e.dma_start(
            dbg["d_mc0"].ap(), moeacc[0].ap()[0:32, :]), dma=True)
        ph.add("gpsimd", lambda e: e.dma_start(
            dbg["d_mc1"].ap(), moeacc[1].ap()[0:32, :]), dma=True)
        run(ph)

    stF.close()
    stD.close()
    stC2.close()
    stW.close()

    stG = ExitStack()
    s_x1tm = SB(stG, "s_x1tm", [128, 16, 1024])
    s_m = [SB(stG, f"s_m{i}", [128, 16, 1024]) for i in range(2)]
    s_s3q = SB(stG, "s_s3q", [128, 2, 1024])
    s_fin = SB(stG, "s_fin", [128, 128], F32)
    s_ou = SB(stG, "s_ou", [128, 2, 1024], F32)
    ph = phase()
    accdone = list(prev_acc)
    lx1 = ph.add("sync", lambda e: e.dma_start(
        s_x1tm.ap(), x1row.ap().rearrange("(t p) c -> p t c", p=128)),
        deps=accdone, dma=True)
    lmt = {}
    for t in range(16):
        for a in range(2):
            lmt[(a, t)] = ph.add(
                "gpsimd", lambda e, a=a, t=t: e.dma_start(
                    s_m[a][:, t, :],
                    moeacc[a].ap()[128 * t:128 * (t + 1), :]),
                deps=accdone if (a == 0 and t == 0) else [], dma=True)
    oprev2 = [None, None]
    prev_sq2 = [None, None]
    for t in range(16):
        par = t % 2
        af_ = None
        for a in range(2):
            af_ = ph.add("vector", lambda e, t=t, a=a: e.tensor_tensor(
                s_x1tm[:, t, :], s_x1tm[:, t, :], s_m[a][:, t, :], AOT.add),
                deps=[lx1 if a == 0 else af_, lmt[(a, t)]])
        a2f = af_
        F = 8 * t
        sm = ph.add("scalar", lambda e, t=t, par=par, F=F: e.activation(
            s_s3q[:, par, :], s_x1tm[:, t, :], ACTF.Copy,
            accum_out=s_fin[:, F:F + 1]), deps=[a2f, prev_sq2[par]])
        sq = ph.add("scalar", lambda e, t=t, par=par, F=F: e.activation(
            s_s3q[:, par, :], s_x1tm[:, t, :], ACTF.Square,
            accum_out=s_fin[:, F + 1:F + 2]),
            deps=[sm])
        prev_sq2[par] = sq
        mu = ph.add("vector", lambda e, F=F: e.tensor_scalar(
            s_fin[:, F:F + 2], s_fin[:, F:F + 2], 1.0 / C, None, AOT.mult),
            deps=[sm, sq])
        v2f = ph.add("vector", lambda e, F=F: e.tensor_scalar(
            s_fin[:, F + 2:F + 3], s_fin[:, F:F + 1], s_fin[:, F:F + 1],
            s_fin[:, F + 1:F + 2], AOT.mult, AOT.subtract), deps=[mu])
        v3f = ph.add("vector", lambda e, F=F: e.tensor_scalar(
            s_fin[:, F + 2:F + 3], s_fin[:, F + 2:F + 3], -1.0, LN_EPS,
            AOT.mult, AOT.add), deps=[v2f])
        v4f = ph.add("vector", lambda e, F=F: e.reciprocal(
            s_fin[:, F + 2:F + 3], s_fin[:, F + 2:F + 3]), deps=[v3f])
        v5f = ph.add("scalar", lambda e, F=F: e.activation(
            s_fin[:, F + 3:F + 4], s_fin[:, F + 2:F + 3], ACTF.Sqrt),
            deps=[v4f])
        v6f = ph.add("vector", lambda e, F=F: e.tensor_scalar(
            s_fin[:, F + 4:F + 5], s_fin[:, F:F + 1], -1.0,
            s_fin[:, F + 3:F + 4], AOT.mult, AOT.mult), deps=[v5f])
        ap_ = ph.add("vector", lambda e, t=t, par=par, F=F: e.tensor_scalar(
            s_ou[:, par, :], s_x1tm[:, t, :], s_fin[:, F + 3:F + 4],
            s_fin[:, F + 4:F + 5], AOT.mult, AOT.add), deps=[v6f, oprev2[par]])
        od = ph.add("sync", lambda e, t=t, par=par: e.dma_start(
            out.ap()[128 * t:128 * (t + 1), :], s_ou[:, par, :]),
            deps=[ap_], dma=True)
        oprev2[par] = od
    run(ph)

    stG.close()
    stC.close()
    emit_all(nc, sems, SCHED)
    st0.close()
    return nc, dbg


_NC_CACHE = {}


def _consts():
    KONST = np.tile(np.arange(128, dtype=np.float32)[None, :], (128, 1))
    piota = np.arange(128, dtype=np.float32)[:, None]
    cstv = np.zeros((128, 772), np.float32)
    cstv[127, 130] = 1.0
    cstv[:, 131] = (np.arange(128) // 16).astype(np.float32)
    dm = np.eye(128, dtype=np.float32)
    for i in range(127):
        dm[i + 1, i] = -1.0
    cstv[:, 644:772] = dm
    cstv[:, 0:128] = KONST
    cstv[:, 128:129] = piota
    cstv[:, 129:130] = 1.0
    cstv[:, 132:260] = np.eye(128, dtype=np.float32)
    f = np.arange(128)
    cstv[:, 260:388] = (f[:, None] <= f[None, :]).astype(np.float32)
    cstv[:, 388:516] = (f[:, None] < f[None, :]).astype(np.float32)
    cstv[:, 516:644] = (f[:, None] == (16 * (f // 16))[None, :]).astype(np.float32)
    cstbv = np.zeros((128, 512), BF)
    cstbv[:, 384:512] = 1.0
    cstbv[:, 0:128] = np.eye(128, dtype=np.float32).astype(BF)
    cstbv[:, 128:256] = (f[:, None] <= f[None, :]).astype(np.float32).astype(BF)
    cstbv[:, 256:384] = ((f[:, None] // 64) == (f[None, :] // 64)).astype(
        np.float32).astype(BF)
    e2bv = np.zeros((2, 128), BF)
    for h in range(2):
        e2bv[h, 64 * h:64 * (h + 1)] = 1.0
    stcv = np.zeros((128, 4), BF)
    stcv[:, 0] = 1.0
    stcv[:, 3] = 1.0
    lt16v = np.zeros((16, 18), np.float32)
    lt16v[:, 0:16] = (np.arange(16)[:, None] <= np.arange(16)[None, :])
    lt16v[15, 16] = 1.0
    return dict(cst=cstv, cstb=cstbv, onesrow=np.ones((1, 128), np.float32),
                e2b=e2bv, eighth8=np.full((8, 128), 0.125, np.float32),
                lt16f=lt16v, stc=stcv)


def _mmask(h):
    m = np.ones((128, 24), np.float32)

    def setm(key, W, base, valid_lo, valid_hi):
        lo, ln = MS[key + "L"]
        ro, rn = MS[key + "R"]
        for i in range(ln):
            pos = base + i
            if not (valid_lo <= pos < valid_hi):
                m[:, lo + i] = 0.0
        for i in range(rn):
            pos = base + W - rn + i
            if not (valid_lo <= pos < valid_hi):
                m[:, ro + i] = 0.0
    O2 = 1024 * h
    O4 = 512 * h
    setm("a0", W_A0, O2 - 3, 0, 2048)
    setm("h0", W_H0, O2 - 1, 0, 2048)
    setm("a1", W_A1, O4 - 1, 0, 1024)
    setm("la", W_A1, O4 - 1, 0, 1024)
    setm("b0", W_B0, O2 - 1, 0, 2048)
    return m


def _diag(filt):
    d = np.zeros((4, NCH, 128, 128), np.float32)
    for t in range(4):
        for k in range(NCH):
            np.fill_diagonal(d[t, k], filt[t, 128 * k:128 * (k + 1)])
    return d


def kernel(L0_attn_x, L0_attn_h_filter, L0_attn_g_filter, L0_attn_h_synth,
           L0_attn_g_synth, L0_attn_norm_gamma, L0_attn_norm_beta,
           L0_ffn_Wg, L0_ffn_W1, L0_ffn_W2, L0_ffn_W3):
    x = np.ascontiguousarray(np.asarray(L0_attn_x, dtype=np.float32))
    hf = np.asarray(L0_attn_h_filter, np.float32)
    gf = np.asarray(L0_attn_g_filter, np.float32)
    hs = np.asarray(L0_attn_h_synth, np.float32)
    gs = np.asarray(L0_attn_g_synth, np.float32)
    gam = np.asarray(L0_attn_norm_gamma, np.float32)
    bet = np.asarray(L0_attn_norm_beta, np.float32)
    Wg = np.asarray(L0_ffn_Wg, np.float32)
    W1 = np.asarray(L0_ffn_W1, np.float32)
    W2 = np.asarray(L0_ffn_W2, np.float32)
    W3 = np.asarray(L0_ffn_W3, np.float32)

    consts = _consts()
    common = dict(consts)
    common["hfd"] = _diag(hf)
    common["gfd"] = _diag(gf)
    common["hsd"] = _diag(hs)
    common["gsd"] = _diag(gs)
    common["gamc"] = np.ascontiguousarray(gam.reshape(NCH, 128).T)
    common["betc"] = np.ascontiguousarray(bet.reshape(NCH, 128).T)
    common["gamr"] = np.ascontiguousarray(gam.reshape(1, C))
    common["betr"] = np.ascontiguousarray(bet.reshape(1, C))
    common["idpat"] = (float(PTOK)
                         + (np.arange(8 * YBLK + 256) % 128)
                         ).astype(np.float32)[:, None]
    common["wgt"] = np.ascontiguousarray(
        Wg.T.reshape(NCH, 128, 8))

    xp = np.zeros((B, S + 2 * HALO, C), np.float32)
    xp[:, HALO:HALO + S, :] = x

    in_maps = []
    for c in range(8):
        b, h = c // 2, c % 2
        o = PTOK * h
        d = dict(common)
        seg = xp[b, o:o + PTOK + 2 * HALO, :]
        d["xfm"] = np.ascontiguousarray(seg.T)
        d["mmask"] = _mmask(h)
        d["w1td"] = np.ascontiguousarray(W1[c].T)
        d["w3td"] = np.ascontiguousarray(W3[c].T)
        d["w2td"] = np.ascontiguousarray(W2[c].T)
        d["pidx"] = ((c ^ 1) * 128 + np.arange(128, dtype=np.uint32))[:, None]
        in_maps.append(d)

    key = "nc"
    if key not in _NC_CACHE:
        with allow_ici_all_to_all():
            _NC_CACHE[key] = build_nc()
    nc, dbgnames = _NC_CACHE[key]

    with allow_ici_all_to_all():
        res = run_bass_kernel_spmd(nc, in_maps, core_ids=list(range(8)),
                                   trace=globals().get("_TRACE", False))
    if globals().get("_TRACE", False):
        kernel._trace_res = res

    outs = [np.asarray(res.results[c]["out"]) for c in range(8)]
    full = np.concatenate(outs, axis=0).reshape(B, S, C)
    if DEBUG:
        kernel._dbg = res.results
    return full


# version 19 (final)
# speedup vs baseline: 1.0246x; 1.0246x over previous
import sys
if "/opt/trn_rl_repo" not in sys.path:
    sys.path.insert(0, "/opt/trn_rl_repo")

import os
import numpy as np
import ml_dtypes
from contextlib import ExitStack

import concourse.bass as bass
import concourse.mybir as mybir
from concourse.bass_utils import run_bass_kernel_spmd
from concourse.bass_core_rust import IndirectOffsetOnAxis as IOA
from concourse.aot_env import allow_ici_all_to_all

F32 = mybir.dt.float32
BF16 = mybir.dt.bfloat16
U32 = mybir.dt.uint32
AOT = mybir.AluOpType
ACTF = mybir.ActivationFunctionType
AXL = mybir.AxisListType
BF = ml_dtypes.bfloat16

DEBUG = bool(int(os.environ.get("MK_DEBUG", "0")))
MAXPH = int(os.environ.get("MK_MAXPH", "99"))


class _StopEmit(Exception):
    pass

os.environ.setdefault("TRN_ALLOW_ICI_ALL_TO_ALL", "1")

B, S, C = 4, 4096, 1024
NTOK = B * S
PTOK = 2048
NCH = 8
HALO = 16
W_X = PTOK + 2 * HALO
W_A0 = 1032
W_H0 = 1026
W_A1 = 515
W_B0 = 1026
NSLOT = 2048
YBLK = 384
HF = 2048
LN_EPS = 1e-5
ATTN_EPS = 1e-6
RG8 = [list(range(8))]
RGP = [[0, 1], [2, 3], [4, 5], [6, 7]]

MS = dict(a0L=(0, 3), a0R=(3, 5), h0L=(8, 1), h0R=(9, 1), a1L=(10, 1),
          a1R=(11, 2), laL=(13, 1), laR=(14, 2), b0L=(16, 1), b0R=(17, 2))

ENGS = ("tensor", "vector", "scalar", "gpsimd", "sync")


def tiles(total, mx=512):
    r, o = [], 0
    while o < total:
        n = min(mx, total - o)
        r.append((o, n))
        o += n
    return r


class Op:
    __slots__ = ("eng", "emit", "deps", "inc", "sidx", "dma", "coll", "done")

    def __init__(self, eng, emit, deps, dma, coll=False):
        self.eng = eng
        self.emit = emit
        self.deps = [d for d in deps if d is not None]
        self.dma = dma
        self.coll = coll
        self.inc = dma or coll
        self.sidx = None
        self.done = False


class Phase:
    def __init__(self, nc, sems, scount):
        self.nc = nc
        self.sems = sems
        self.ops = []

    def add(self, eng, emit, deps=(), dma=False, coll=False):
        op = Op(eng, emit, deps, dma, coll)
        self.ops.append(op)
        return op


def emit_all(nc, sems, ops):
    for op in ops:
        for d in op.deps:
            d.inc = True
    per_eng = {}
    for op in ops:
        per_eng.setdefault(op.eng, []).append(op)
    for lst in per_eng.values():
        lst[-1].inc = True
    cum = {e: 0 for e in ENGS}
    for op in ops:
        if op.inc:
            cum[op.eng] += 16 if op.dma else 1
        op.sidx = cum[op.eng]

    with nc.Block() as blk:
        def make_body(eng_name):
            def body(engine):
                hwm = {e: 0 for e in ENGS}
                for op in per_eng.get(eng_name, []):
                    for d in op.deps:
                        if d.sidx > hwm[d.eng]:
                            engine.wait_ge(sems[d.eng], d.sidx)
                            hwm[d.eng] = d.sidx
                    ins = op.emit(engine)
                    if op.inc:
                        ins.then_inc(sems[op.eng], 16 if op.dma else 1)
                last = None
                for op in per_eng.get(eng_name, []):
                    if op.dma or op.coll:
                        last = op
                if last is not None:
                    engine.wait_ge(sems[eng_name], last.sidx)
            return body

        for eng_name in ENGS:
            if eng_name in per_eng:
                getattr(blk, eng_name)(make_body(eng_name))


def build_nc():
    try:
        return _build_nc_inner()
    except _StopEmit as e:
        nc, dbg, sems, ops = e.args
        emit_all(nc, sems, ops)
        return nc, dbg


def _build_nc_inner():
    nc = bass.Bass("TRN2", target_bir_lowering=False)

    xfm = nc.dram_tensor("xfm", [C, W_X], F32, kind="ExternalInput")
    hfd = nc.dram_tensor("hfd", [4, NCH, 128, 128], F32, kind="ExternalInput")
    gfd = nc.dram_tensor("gfd", [4, NCH, 128, 128], F32, kind="ExternalInput")
    hsd = nc.dram_tensor("hsd", [4, NCH, 128, 128], F32, kind="ExternalInput")
    gsd = nc.dram_tensor("gsd", [4, NCH, 128, 128], F32, kind="ExternalInput")
    gamc = nc.dram_tensor("gamc", [128, NCH], F32, kind="ExternalInput")
    gamr = nc.dram_tensor("gamr", [1, C], F32, kind="ExternalInput")
    betr = nc.dram_tensor("betr", [1, C], F32, kind="ExternalInput")
    betc = nc.dram_tensor("betc", [128, NCH], F32, kind="ExternalInput")
    wgt = nc.dram_tensor("wgt", [NCH, 128, 8], F32, kind="ExternalInput")
    w1td = nc.dram_tensor("w1td", [C, HF], F32, kind="ExternalInput")
    w3td = nc.dram_tensor("w3td", [C, HF], F32, kind="ExternalInput")
    w2td = nc.dram_tensor("w2td", [HF, C], F32, kind="ExternalInput")
    cst = nc.dram_tensor("cst", [128, 772], F32, kind="ExternalInput")
    cstb = nc.dram_tensor("cstb", [128, 512], BF16, kind="ExternalInput")
    onesrow = nc.dram_tensor("onesrow", [1, 128], F32, kind="ExternalInput")
    e2b = nc.dram_tensor("e2b", [2, 128], BF16, kind="ExternalInput")
    eighth8 = nc.dram_tensor("eighth8", [8, 128], F32, kind="ExternalInput")
    lt16f = nc.dram_tensor("lt16f", [16, 18], F32, kind="ExternalInput")
    mmask = nc.dram_tensor("mmask", [128, 24], F32, kind="ExternalInput")
    stc = nc.dram_tensor("stc", [128, 4], BF16, kind="ExternalInput")

    out = nc.dram_tensor("out", [PTOK, C], F32, kind="ExternalOutput")

    x1row = nc.dram_tensor("x1row", [PTOK, C], BF16)
    x1all = nc.dram_tensor("x1all", [NTOK + 1, C], BF16, addr_space="Shared")
    kv_in = nc.dram_tensor("kv_in", [128, 1088], F32)
    kv_all = nc.dram_tensor("kv_all", [8 * 128, 1088], F32, addr_space="Shared")
    pidx = nc.dram_tensor("pidx", [128, 1], U32, kind="ExternalInput")
    sc_in = nc.dram_tensor("sc_in", [8, PTOK], F32)
    sc_out = nc.dram_tensor("sc_out", [8, PTOK], F32)
    idpat = nc.dram_tensor("idpat", [8 * YBLK + 256, 1], F32,
                           kind="ExternalInput")
    idstage = nc.dram_tensor("idstage", [8 * YBLK + 256, 1], F32)
    idrecv = nc.dram_tensor("idrecv", [8 * YBLK, 1], F32)
    ystage = nc.dram_tensor("ystage", [8 * YBLK + 130, C], BF16)
    yrecv = nc.dram_tensor("yrecv", [8 * YBLK, C], BF16)
    moeacc = [nc.dram_tensor(f"moeacc{i}", [PTOK + 130, C], F32)
              for i in range(2)]

    dbg = {}
    dbgref = [None]

    def DB(name, shape, dt=F32):
        if DEBUG:
            dbg[name] = nc.dram_tensor(name, shape, dt, kind="ExternalOutput")

    DB("d_a0", [C, W_A0]); DB("d_h0", [C, W_H0]); DB("d_a1", [C, W_A1])
    DB("d_q", [C, W_A1]); DB("d_la", [C, W_A1]); DB("d_b0", [C, W_B0])
    DB("d_r1", [C, PTOK]); DB("d_x1", [C, PTOK]); DB("d_sc", [8, PTOK])
    DB("d_thr", [128, 16]); DB("d_tok", [128, 16]); DB("d_wsl", [128, 16])
    DB("d_ydst", [128, 16]); DB("d_kv", [128, 1036]); DB("d_sd", [128, 24])
    DB("d_mask", [128, 128]); DB("d_xg", [128, NCH * NSLOT], BF16)
    DB("d_hh", [128, 2048], BF16); DB("d_sm", [128, 128])
    DB("d_yst", [8 * YBLK, C], BF16); DB("d_yrv", [8 * YBLK, C], BF16)
    DB("d_mc0", [32, C]); DB("d_mc1", [32, C])
    dbgref[0] = dbg

    st0 = ExitStack()

    def SB(stk, name, shape, dt=BF16):
        return stk.enter_context(nc.sbuf_tensor(name, shape, dt))

    def PS(stk, name, shape, dt=F32):
        return stk.enter_context(nc.psum_tensor(name, shape, dt))

    sems = {e: st0.enter_context(nc.semaphore("sem_" + e)) for e in ENGS}
    scount = {e: 0 for e in ENGS}

    def phase():
        return Phase(nc, sems, scount)

    nph = [0]
    SCHED = []

    def fence(excl=()):
        last = {}
        for op in SCHED:
            last[op.eng] = op
        return [v for k, v in last.items() if k not in excl]

    runidx = [0]

    def run(ph, auto_fence=True):
        sf = os.environ.get("MK_SAFE", "")
        if sf == "1" or (sf and str(runidx[0]) in sf.split(",")):
            auto_fence = True
        runidx[0] += 1
        if nph[0] >= MAXPH:
            return
        if auto_fence and SCHED:
            fz = fence()
            seen = set()
            for op in ph.ops:
                if op.eng not in seen:
                    seen.add(op.eng)
                    op.deps = op.deps + [d for d in fz if d is not op]
        SCHED.extend(ph.ops)
        nph[0] += 1

    s_cst = SB(st0, "s_cst", [128, 772], F32)
    s_cstb = SB(st0, "s_cstb", [128, 512], BF16)
    s_gam = SB(st0, "s_gam", [128, NCH], F32)
    s_bet = SB(st0, "s_bet", [128, NCH], F32)
    s_wgtb = SB(st0, "s_wgtb", [128, NCH * 8], BF16)
    s_1row = SB(st0, "s_1row", [1, 128], F32)
    s_e2b = SB(st0, "s_e2b", [2, 128], BF16)
    s_8th = SB(st0, "s_8th", [8, 128], F32)
    s_lt16 = SB(st0, "s_lt16", [16, 18], F32)
    s_mmsk = SB(st0, "s_mmsk", [128, 24], F32)
    s_stc = SB(st0, "s_stc", [128, 4], BF16)
    stZ = ExitStack()
    s_zb = SB(stZ, "s_zb", [128, 1024], BF16)
    s_zf = s_zb.bitcast(F32)

    KONST = s_cst[:, 0:128]
    piota = s_cst[:, 128:129]
    ones_f = s_cst[:, 129:130]
    IDENTF = s_cst[:, 132:260]
    LTRIF = s_cst[:, 260:388]
    SLTRIF = s_cst[:, 388:516]
    G16F = s_cst[:, 516:644]
    DMAT = s_cst[:, 644:772]
    e127 = s_cst[:, 130:131]
    ownidx = s_cst[:, 131:132]
    IDENT = s_cstb[:, 0:128]
    LTRI = s_cstb[:, 128:256]
    HMASK = s_cstb[:, 256:384]
    ones_b = s_cstb[:, 255:256]
    ones1rb = s_cstb[0:1, 384:512]
    ones1r = bass.AP(s_1row, 0, [[128, 1], [1, 128]])

    ph = phase()
    for dst, src in [(s_cst[:, :], cst.ap()), (s_cstb[:, :], cstb.ap()),
                     (s_gam[:, :], gamc.ap()), (s_bet[:, :], betc.ap()),
                     (s_1row[:, :], onesrow.ap()), (s_e2b[:, :], e2b.ap()),
                     (s_8th[:, :], eighth8.ap()), (s_lt16[:, :], lt16f.ap()),
                     (s_mmsk[:, :], mmask.ap()), (s_stc[:, :], stc.ap())]:
        ph.add("sync", lambda e, d=dst, s=src: e.dma_start(d, s), dma=True)
    ph.add("gpsimd", lambda e: e.dma_start(
        s_wgtb.ap().rearrange("p (k e) -> p k e", k=NCH), wgt.ap().rearrange("k p e -> p k e")), dma=True)
    z1 = ph.add("vector", lambda e: e.memset(s_zb.ap(), 0.0))
    z2 = z1
    ph.add("scalar", lambda e: e.dma_start(
        x1all.ap()[NTOK:NTOK + 1, :], s_zb[0:1, :]), deps=[z1], dma=True)
    idpf = ph.add("sync", lambda e: e.dma_start(idstage.ap(), idpat.ap()),
                  dma=True)
    run(ph)

    def emit_zero_fills(ph, extra=()):
        first = [list(extra)]
        tot = 8 * YBLK + 130
        for i in range((tot + 127) // 128):
            r0, m = i * 128, min(128, tot - i * 128)
            ph.add("scalar", lambda e, r0=r0, m=m: e.dma_start(
                ystage.ap()[r0:r0 + m, :], s_zb[0:m, :]),
                deps=[z1] + first.pop() if first else [z1],
                dma=True)
            if not first:
                first = []
        tot = PTOK + 130
        for acc in moeacc:
            for i in range((tot + 127) // 128):
                r0, m = i * 128, min(128, tot - i * 128)
                for cc0 in (0, 512):
                    ph.add("scalar", lambda e, r0=r0, m=m, a=acc, cc0=cc0:
                           e.dma_start(a.ap()[r0:r0 + m, cc0:cc0 + 512],
                                       s_zf[0:m, :]), deps=[z2], dma=True)

    stA = ExitStack()
    s_x = SB(stA, "s_x", [128, NCH, W_X])
    s_r1 = SB(stA, "s_r1", [128, NCH, PTOK])
    s_scst = SB(stA, "s_scst", [8, 512], F32)
    s_st = SB(stA, "s_st", [128, 256], F32)
    s_tm = SB(stA, "s_tm", [128, 2, 1024])
    s_gb = SB(stA, "s_gb", [128, 2048])
    s_gr = SB(stA, "s_gr", [1, 2048], F32)
    s_x1t = SB(stA, "s_x1t", [128, 2, 1024])

    stB = ExitStack()
    s_a0 = SB(stB, "s_a0", [128, NCH, W_A0])
    s_h0 = SB(stB, "s_h0", [128, NCH, W_H0])
    s_a1 = SB(stB, "s_a1", [128, NCH, W_A1])
    s_h1 = SB(stB, "s_h1", [128, NCH, W_A1])
    s_la = SB(stB, "s_la", [128, NCH, W_A1])
    s_b0 = SB(stB, "s_b0", [128, NCH, W_B0])
    s_fd = SB(stB, "s_fd", [128, 4 * NCH * 128])
    stB2 = ExitStack()
    s_qsp = SB(stB2, "s_qsp", [128, 2, W_A1])
    s_q2 = SB(stB2, "s_q2", [128, 2, W_A1])
    s_qt = SB(stB2, "s_qt", [128, 8 * 128])
    s_kvf = SB(stB2, "s_kvf", [128, 1036], F32)
    s_kvp = SB(stB2, "s_kvp", [128, 544], F32)
    s_pidx = SB(stB2, "s_pidx", [128, 1], U32)
    s_kvm = SB(stB2, "s_kvm", [128, NCH * 128])
    s_z2 = SB(stB2, "s_z2", [128, 2], F32)
    s_z2b = SB(stB2, "s_z2b", [128, 2])
    s_rden = SB(stB2, "s_rden", [2, W_A1], F32)
    s_rdenb = SB(stB2, "s_rdenb", [2, W_A1])
    s_nm = SB(stB2, "s_nm", [128, W_A1])

    stPa = ExitStack()
    pf = [PS(stPa, f"pf{i}", [128, 512], F32) for i in range(6)]
    pT = PS(stPa, "pT", [128, 1024], BF16)
    pS2 = PS(stPa, "pS2", [128, 512], F32)

    def xap(k, col, n, step=1):
        return bass.AP(s_x, k * W_X + col, [[NCH * W_X, 128], [step, n]])

    def fap(t, k):
        off = (t * NCH + k) * 128
        return bass.AP(s_fd, off, [[4 * NCH * 128, 128], [1, 128]])

    def fdld(ph, src_t, deps):
        return ph.add("gpsimd", lambda e, s=src_t: e.dma_start(
            bass.AP(s_fd, 0, [[4 * NCH * 128, 128], [NCH * 128, 4],
                              [128, NCH], [1, 128]]),
            s.ap().rearrange("t k p c -> p t k c")), deps=deps, dma=True)

    def sap(sb, W, k, col, n, step=1):
        return bass.AP(sb, k * W + col, [[NCH * W, 128], [step, n]])

    def margins(ph, sb, W, k, lkey, rkey, dep):
        lo, ln = MS[lkey]
        ro, rn = MS[rkey]
        dl = dep if isinstance(dep, list) else [dep]
        o1 = ph.add("vector", lambda e, k=k, lo=lo, ln=ln: e.tensor_tensor(
            sap(sb, W, k, 0, ln), sap(sb, W, k, 0, ln),
            s_mmsk[:, lo:lo + ln], AOT.mult), deps=dl)
        return ph.add("vector", lambda e, k=k, ro=ro, rn=rn: e.tensor_tensor(
            sap(sb, W, k, W - rn, rn), sap(sb, W, k, W - rn, rn),
            s_mmsk[:, ro:ro + rn], AOT.mult), deps=[o1])

    ph = phase()
    lx = ph.add("gpsimd", lambda e: e.dma_start(
        s_x.ap(),
        xfm.ap().rearrange("(k p) w -> p k w", p=128)), dma=True)
    lhf = fdld(ph, hfd, [])
    NB = 6

    def conv_seg(ph, items, first_deps, ring_ofs=0):
        evs = []
        for i, (mk, ev) in enumerate(items):
            deps = list(first_deps) if i == 0 else []
            if i >= NB:
                deps.append(evs[i - NB])
            mm = ph.add("tensor", mk, deps=deps)
            evs.append(ph.add("vector", lambda e, ev=ev, mm=mm: ev(e), deps=[mm]))
        return evs

    def mk_down(bank, sb_src, base, k, t0, n, srcW=None, stride=2, xsrc=True):
        def mk(e, bank=bank, base=base, k=k, t0=t0, n=n):
            for tap in range(4):
                if xsrc:
                    rhs = xap(k, 2 * t0 + base + tap, n, 2)
                else:
                    rhs = sap(s_a0, W_A0, k, 2 * t0 + tap, n, 2)
                ins = e.matmul(bank[:, 0:n], fap(tap, k), rhs,
                               start=(tap == 0), stop=(tap == 3))
            return ins
        return mk

    items = []
    for k in range(NCH):
        for (t0, n) in tiles(W_A0):
            bank = pf[len(items) % NB]
            mk = mk_down(bank, None, 9, k, t0, n, xsrc=True)
            ev = (lambda e, bank=bank, k=k, t0=t0, n=n:
                  e.tensor_copy(sap(s_a0, W_A0, k, t0, n), bank[:, 0:n]))
            items.append((mk, ev))
    evs = conv_seg(ph, items, [lx, lhf])
    dep = evs[-1]
    mdeps = []
    for k in range(NCH):
        dep = margins(ph, s_a0, W_A0, k, "a0L", "a0R", dep)
        mdeps.append(dep)
    items = []
    for k in range(NCH):
        for (t0, n) in tiles(W_A1):
            bank = pf[len(items) % NB]
            mk = mk_down(bank, None, 0, k, t0, n, xsrc=False)
            ev = (lambda e, bank=bank, k=k, t0=t0, n=n:
                  e.tensor_copy(sap(s_a1, W_A1, k, t0, n), bank[:, 0:n]))
            items.append((mk, ev))
    evs = conv_seg(ph, items, [dep])
    last_a1_mm = [op for op in ph.ops if op.eng == "tensor"][-1]
    lgf = fdld(ph, gfd, [last_a1_mm])
    items = []
    for k in range(NCH):
        for (t0, n) in tiles(W_H0):
            bank = pf[len(items) % NB]
            mk = mk_down(bank, None, 13, k, t0, n, xsrc=True)
            ev = (lambda e, bank=bank, k=k, t0=t0, n=n:
                  e.tensor_copy(sap(s_h0, W_H0, k, t0, n), bank[:, 0:n]))
            items.append((mk, ev))
    evs0 = conv_seg(ph, items, [lgf, evs[-1]])
    items = []
    for k in range(NCH):
        for (t0, n) in tiles(W_A1):
            bank = pf[len(items) % NB]
            mk = mk_down(bank, None, 0, k, t0, n, xsrc=False)
            ev = (lambda e, bank=bank, k=k, t0=t0, n=n:
                  e.tensor_copy(sap(s_h1, W_A1, k, t0, n), bank[:, 0:n]))
            items.append((mk, ev))
    evs = conv_seg(ph, items, [evs0[-1]])
    dep = evs[-1]
    for k in range(NCH):
        dep = margins(ph, s_h0, W_H0, k, "h0L", "h0R", dep)
        dep = margins(ph, s_a1, W_A1, k, "a1L", "a1R", dep)
        dep = margins(ph, s_h1, W_A1, k, "a1L", "a1R", dep)
    run(ph)

    ph = phase()
    prev_q = [None, None]
    tinfo = []
    for k in range(NCH):
        par = k % 2
        mn = ph.add("vector", lambda e, k=k, par=par: e.tensor_scalar(
            s_qsp[:, par, :], sap(s_a1, W_A1, k, 0, W_A1), 0.0, None, AOT.min),
            deps=[prev_q[par]])
        ex = ph.add("scalar", lambda e, par=par: e.activation(
            s_q2[:, par, :], s_qsp[:, par, :], ACTF.Exp),
            deps=[mn, prev_q[par]])
        mx = ph.add("vector", lambda e, k=k, par=par: e.tensor_scalar(
            s_qsp[:, par, :], sap(s_a1, W_A1, k, 0, W_A1),
            0.0, None, AOT.max), deps=[ex])
        ad = ph.add("vector", lambda e, par=par: e.tensor_tensor(
            s_q2[:, par, :], s_q2[:, par, :], s_qsp[:, par, :], AOT.add),
            deps=[mx])

        def mkt(e, k=k, par=par):
            for j in range(4):
                e.transpose(pT[:, j * 128:(j + 1) * 128],
                            s_q2[:, par, 1 + 128 * j:1 + 128 * (j + 1)], IDENT)
            for j in range(4):
                ins = e.transpose(pT[:, (4 + j) * 128:(5 + j) * 128],
                                  sap(s_a1, W_A1, k, 1 + 128 * j, 128), IDENT)
            return ins
        tdeps = [ad] + ([tinfo[-1][1]] if k >= 1 else [])
        tp = ph.add("tensor", mkt, deps=tdeps)
        prev_q[par] = tp
        evt = ph.add("vector", lambda e: e.tensor_copy(s_qt[:, :], pT[:, :]),
                     deps=[tp])
        bank = pf[k % 2]
        kdeps = [evt] + ([tinfo[k - 2][2]] if k >= 2 else [])

        def mkkv(e, bank=bank):
            for j in range(4):
                e.matmul(bank[:, 0:128], s_qt[:, j * 128:(j + 1) * 128],
                         s_qt[:, (4 + j) * 128:(5 + j) * 128],
                         start=(j == 0), stop=(j == 3))
            for j in range(4):
                ins = e.matmul(bank[:, 128:129], s_qt[:, j * 128:(j + 1) * 128],
                               ones_b, start=(j == 0), stop=(j == 3))
            return ins
        kv = ph.add("tensor", mkkv, deps=kdeps)
        e1 = ph.add("vector", lambda e, k=k, bank=bank: e.tensor_copy(
            s_kvf[:, k * 128:(k + 1) * 128], bank[:, 0:128]), deps=[kv])
        e2 = ph.add("vector", lambda e, k=k, bank=bank: e.tensor_copy(
            s_kvf[:, 1024 + k:1025 + k], bank[:, 128:129]), deps=[kv, e1])
        tinfo.append((tp, evt, e2))
    run(ph)

    ph = phase()
    d1 = ph.add("sync", lambda e: e.dma_start(kv_in.ap()[:, 0:1036],
                                              s_kvf[:, :]), dma=True)
    dpi = ph.add("sync", lambda e: e.dma_start(s_pidx[:, :], pidx.ap()),
                 dma=True)
    _CCM = int(os.environ.get("MK_CCMODE", "0"))
    if _CCM == 1:
        cc = ph.add("gpsimd", lambda e: e.dma_start(
            kv_all.ap()[0:128, :], kv_in.ap()), deps=[d1], dma=True)
    elif _CCM == 2:
        cc = ph.add("gpsimd", lambda e: e.collective_compute(
            "AllGather", AOT.bypass, replica_groups=RG8,
            ins=[kv_in.ap()[0:16, 0:128]], outs=[kv_all.ap()[0:128, 0:128]]),
            deps=[d1], coll=True)
    else:
        cc = ph.add("gpsimd", lambda e: e.collective_compute(
            "AllGather", AOT.bypass, replica_groups=RG8,
            ins=[kv_in.ap()], outs=[kv_all.ap()]), deps=[d1], coll=True)
    adds = []
    _SKIPG = bool(int(os.environ.get("MK_SKIPG", "0")))
    for half, (ho, hn) in enumerate([] if _SKIPG else [(0, 544), (544, 492)]):
        gk = ph.add("gpsimd", lambda e, ho=ho: e.indirect_dma_start(
            s_kvp[:, :], None, kv_all.ap(),
            IOA(s_pidx[:, 0:1], 0), element_offset=ho),
            deps=[cc, dpi] + adds, dma=True)
        ad = ph.add("vector", lambda e, ho=ho, hn=hn: e.tensor_tensor(
            s_kvf[:, ho:ho + hn], s_kvf[:, ho:ho + hn], s_kvp[:, 0:hn],
            AOT.add), deps=[gk])
        adds.append(ad)
    if DEBUG:
        ph.add("gpsimd", lambda e: e.dma_start(dbg["d_kv"].ap(), s_kvf[:, :]),
               deps=adds, dma=True)
    run(ph)

    ph = phase()
    bankD_last = {0: None, 1: None}
    bankN_last = {0: None, 1: None}
    pS2_last = None
    prev_q = [None, None]
    lamg = {}
    tplast = tinfo[-1][0]
    for k in range(NCH):
        par = k % 2
        km = ph.add("vector", lambda e, k=k: e.tensor_tensor(
            s_kvm[:, k * 128:(k + 1) * 128], s_kvf[:, k * 128:(k + 1) * 128],
            HMASK, AOT.mult), deps=adds)
        za = ph.add("vector", lambda e, k=k: e.tensor_tensor(
            s_z2[:, 0:1], s_kvf[:, 1024 + k:1025 + k], HMASK[:, 0:1], AOT.mult),
            deps=adds)
        zb_ = ph.add("vector", lambda e, k=k: e.tensor_tensor(
            s_z2[:, 1:2], s_kvf[:, 1024 + k:1025 + k], HMASK[:, 64:65], AOT.mult),
            deps=[za])
        zc = ph.add("vector", lambda e: e.tensor_copy(s_z2b[:, :], s_z2[:, :]),
                    deps=[zb_])
        mn = ph.add("vector", lambda e, k=k, par=par: e.tensor_scalar(
            s_qsp[:, par, :], sap(s_a1, W_A1, k, 0, W_A1), 0.0, None, AOT.min),
            deps=[prev_q[par], tplast])
        ex = ph.add("scalar", lambda e, par=par: e.activation(
            s_q2[:, par, :], s_qsp[:, par, :], ACTF.Exp),
            deps=[mn, prev_q[par]])
        mx = ph.add("vector", lambda e, k=k, par=par: e.tensor_scalar(
            s_qsp[:, par, :], sap(s_a1, W_A1, k, 0, W_A1),
            0.0, None, AOT.max), deps=[ex])
        ad = ph.add("vector", lambda e, par=par: e.tensor_tensor(
            s_q2[:, par, :], s_q2[:, par, :], s_qsp[:, par, :], AOT.add),
            deps=[mx])
        la_last = None
        for (off, n) in tiles(W_A1):
            bankD = pf[2 + par]
            bankN = pf[4 + par]
            dmm = ph.add("tensor", lambda e, par=par, off=off, n=n, bankD=bankD:
                         e.matmul(bankD[0:2, 0:n], s_z2b[:, :],
                                  s_q2[:, par, off:off + n], start=True,
                                  stop=True), deps=[zc, ad, bankD_last[par]])
            de = ph.add("vector", lambda e, off=off, n=n, bankD=bankD:
                        e.tensor_scalar(s_rden[0:2, off:off + n],
                                        bankD[0:2, 0:n], ATTN_EPS, None, AOT.add),
                        deps=[dmm, pS2_last])
            bankD_last[par] = de
            dr = ph.add("vector", lambda e, off=off, n=n: e.reciprocal(
                s_rden[0:2, off:off + n], s_rden[0:2, off:off + n]),
                deps=[de])
            db = ph.add("vector", lambda e, off=off, n=n: e.tensor_copy(
                s_rdenb[0:2, off:off + n], s_rden[0:2, off:off + n]),
                deps=[dr])
            xm = ph.add("tensor", lambda e, off=off, n=n: e.matmul(
                pS2[:, 0:n], s_e2b[:, :], s_rdenb[0:2, off:off + n],
                start=True, stop=True), deps=[db, pS2_last])
            nmm = ph.add("tensor", lambda e, k=k, par=par, off=off, n=n,
                         bankN=bankN:
                         e.matmul(bankN[:, 0:n], s_kvm[:, k * 128:(k + 1) * 128],
                                  s_q2[:, par, off:off + n], start=True,
                                  stop=True), deps=[km, ad, bankN_last[par]])
            ne = ph.add("vector", lambda e, off=off, n=n, bankN=bankN:
                        e.tensor_copy(s_nm[:, off:off + n], bankN[:, 0:n]),
                        deps=[nmm])
            bankN_last[par] = ne
            la_last = ph.add("vector", lambda e, k=k, off=off, n=n:
                             e.tensor_tensor(sap(s_la, W_A1, k, off, n),
                                             s_nm[:, off:off + n],
                                             pS2[:, 0:n], AOT.mult),
                             deps=[ne, xm])
            pS2_last = la_last
        prev_q[par] = la_last
        lamg[k] = margins(ph, s_la, W_A1, k, "laL", "laR", la_last)
    la_final = la_last
    lhs_ = fdld(ph, hsd, [tplast, cc])
    emit_zero_fills(ph, extra=[cc])
    run(ph, auto_fence=False)
    stB2.close()
    stB3 = ExitStack()
    s_fd2 = SB(stB3, "s_fd2", [128, 4 * NCH * 128])

    def fap2(t, k):
        off = (t * NCH + k) * 128
        return bass.AP(s_fd2, off, [[4 * NCH * 128, 128], [1, 128]])

    def fdld2(ph, src_t, deps):
        return ph.add("gpsimd", lambda e, s=src_t: e.dma_start(
            bass.AP(s_fd2, 0, [[4 * NCH * 128, 128], [NCH * 128, 4],
                               [128, NCH], [1, 128]]),
            s.ap().rearrange("t k p c -> p t k c")), deps=deps, dma=True)

    ph = phase()
    lgs_ = fdld2(ph, gsd, [la_final])

    def up_items(level):
        its = []
        for k in range(NCH):
            for even in (True, False):
                W = 513 if level == 1 else 1024
                for (t0, n) in tiles(W):
                    its.append((k, even, t0, n))
        return its

    evs1 = []
    tmk = {k: [] for k in range(NCH)}
    i = 0
    for (k, even, t0, n) in up_items(1):
        bank = pf[i % NB]
        dl = [-1, 0, 0, 1] if even else [0, 0, 1, 1]
        base = 1 if even else 0
        deps = [lhs_, lgs_] if i == 0 else []
        deps.append(lamg[k])
        if i >= NB:
            deps.append(evs1[i - NB])
        else:
            deps.append(la_final)

        def mk(e, bank=bank, k=k, t0=t0, n=n, dl=dl, base=base):
            for tap in range(4):
                e.matmul(bank[:, 0:n], fap(tap, k),
                         sap(s_la, W_A1, k, base + dl[tap] + t0, n),
                         start=(tap == 0), stop=False)
            for tap in range(4):
                ins = e.matmul(bank[:, 0:n], fap2(tap, k),
                               sap(s_h1, W_A1, k, base + dl[tap] + t0, n),
                               start=False, stop=(tap == 3))
            return ins
        mm = ph.add("tensor", mk, deps=deps)
        foff = (1 if even else 0) + 2 * t0
        ev = ph.add("vector", lambda e, k=k, foff=foff, n=n, bank=bank:
                    e.tensor_copy(sap(s_b0, W_B0, k, foff, n, 2),
                                  bank[:, 0:n]), deps=[mm])
        evs1.append(ev)
        tmk[k].append(ev)
        i += 1
    b0m = {}
    mprev = None
    for k in range(NCH):
        mprev = margins(ph, s_b0, W_B0, k, "b0L", "b0R",
                        tmk[k] + ([mprev] if mprev else []))
        b0m[k] = mprev

    evs0 = []
    i = 0
    n1 = len(evs1)
    for (k, even, t0, n) in up_items(0):
        bank = pf[i % NB]
        dl = [-1, 0, 0, 1] if even else [0, 0, 1, 1]
        base = 1
        deps = [b0m[k]]
        if i >= NB:
            deps.append(evs0[i - NB])
        else:
            deps.append(evs1[n1 - NB + i])

        def mk(e, bank=bank, k=k, t0=t0, n=n, dl=dl, base=base):
            for tap in range(4):
                e.matmul(bank[:, 0:n], fap(tap, k),
                         sap(s_b0, W_B0, k, base + dl[tap] + t0, n),
                         start=(tap == 0), stop=False)
            for tap in range(4):
                ins = e.matmul(bank[:, 0:n], fap2(tap, k),
                               sap(s_h0, W_H0, k, base + dl[tap] + t0, n),
                               start=False, stop=(tap == 3))
            return ins
        mm = ph.add("tensor", mk, deps=deps)
        foff = (0 if even else 1) + 2 * t0
        ev = ph.add("vector", lambda e, k=k, foff=foff, n=n, bank=bank:
                    e.tensor_tensor(sap(s_r1, PTOK, k, foff, n, 2),
                                    bank[:, 0:n],
                                    xap(k, HALO + foff, n, 2), AOT.add),
                    deps=[mm])
        evs0.append(ev)
        i += 1
    run(ph, auto_fence=False)

    if DEBUG:
        ph = phase()
        for nm, sb, W in [("d_a0", s_a0, W_A0), ("d_h0", s_h0, W_H0),
                          ("d_a1", s_a1, W_A1),
                          ("d_la", s_la, W_A1), ("d_b0", s_b0, W_B0),
                          ("d_r1", s_r1, PTOK)]:
            ph.add("gpsimd", lambda e, nm=nm, sb=sb, W=W: e.dma_start(
                dbg[nm].ap().rearrange("(k p) w -> p k w", p=128),
                sb.ap()), dma=True)
        run(ph)

    def rap(k, j):
        return bass.AP(s_r1, k * PTOK + 512 * j, [[NCH * PTOK, 128], [1, 512]])

    gamb = s_gb[:, 0:1024]
    betb = s_gb[:, 1024:2048]
    pfb = [pf[i].bitcast(BF16) for i in range(4)]
    pS2b = pS2.bitcast(BF16)

    def bap(h, c0, n):
        return bass.AP(h, c0, [[1024, 128], [1, n]])
    ph = phase()
    lgr = ph.add("sync", lambda e: e.dma_start(s_gr[0:1, 0:1024], gamr.ap()),
                 dma=True)
    lbr = ph.add("sync", lambda e: e.dma_start(s_gr[0:1, 1024:2048],
                                               betr.ap()), dma=True)
    gbe = []
    prevgb = {4: None, 5: None}
    for bi, srccol in ((4, 1024), (5, 1024)):
        pass
    for h in range(2):
        for bi, c0 in ((4, 0), (5, 1024)):
            mmg = ph.add("tensor", lambda e, bi=bi, c0=c0, h=h: e.matmul(
                pf[bi][:, 0:512], ones1r,
                s_gr[0:1, c0 + 512 * h:c0 + 512 * (h + 1)],
                start=True, stop=True),
                deps=[lgr, lbr, prevgb[bi]])
            evg = ph.add("vector", lambda e, bi=bi, c0=c0, h=h:
                         e.tensor_copy(s_gb[:, c0 + 512 * h:c0 + 512 * (h + 1)],
                                       pf[bi][:, 0:512]), deps=[mmg])
            prevgb[bi] = evg
            gbe.append(evg)

    _P4N = int(os.environ.get("MK_P4N", "16"))
    _P4OPS = int(os.environ.get("MK_P4OPS", "99"))

    def ckpt(n):
        if _P4OPS <= n:
            raise _StopEmit(nc, dbgref[0], sems, SCHED + list(ph.ops))
    xrow_d = []
    sc_ev = []
    prev_scst = None
    tpb_ops = [None] * 16
    bke_ops = [None] * 16
    ln1_ops = [None] * 16
    s2_ops = [None] * 16
    x1_ops = [None] * 16
    dmax_ops = [None] * 16
    sq_last = [None] * 16
    for T in range(_P4N):
        par = T % 2
        A = pfb[par]
        B = pfb[2 + par]
        BK = pT if par == 0 else pS2b
        F = 16 * T

        def mtr_(e, T=T, A=A):
            for k in range(NCH):
                ins = e.transpose(bap(A, k * 128, 128),
                                  bass.AP(s_r1, k * PTOK + 128 * T,
                                          [[NCH * PTOK, 128], [1, 128]]),
                                  IDENT)
            return ins
        tpR = ph.add("tensor", mtr_,
                     deps=([ln1_ops[T - 2]] if T >= 2 else []))

        def mtx_(e, T=T, B=B):
            for k in range(NCH):
                ins = e.transpose(bap(B, k * 128, 128),
                                  bass.AP(s_x, k * W_X + HALO + 128 * T,
                                          [[NCH * W_X, 128], [1, 128]]),
                                  IDENT)
            return ins
        tpX = ph.add("tensor", mtx_,
                     deps=[tpR] + ([s2_ops[T - 2]] if T >= 2 else []))
        ckpt(2)
        if T >= 1:
            Tp = T - 1
            BKp = pT if Tp % 2 == 0 else pS2b
            def mtb_(e, Tp=Tp, BKp=BKp):
                for k in range(NCH):
                    ins = e.transpose(bap(BKp, k * 128, 128),
                                      s_x1t[:, Tp % 2, k * 128:(k + 1) * 128],
                                      IDENT)
                return ins
            tpb = ph.add("tensor", mtb_,
                         deps=[x1_ops[Tp]]
                         + ([bke_ops[Tp - 2]] if Tp >= 2 else []))
            tpb_ops[Tp] = tpb
            bke = ph.add("scalar", lambda e, Tp=Tp, BKp=BKp: e.activation(
                bass.AP(s_r1, 128 * Tp,
                        [[NCH * PTOK, 128], [PTOK, NCH], [1, 128]]),
                bap(BKp, 0, 1024).rearrange("p (k c) -> p k c", k=NCH),
                ACTF.Copy), deps=[tpb])
            bke_ops[Tp] = bke

        tmp1 = s_tm[:, par, :]
        x1sl = s_x1t[:, par, :]
        rsum = ph.add("vector", lambda e, A=A, F=F: e.reduce_sum(
            s_st[:, F:F + 1], bap(A, 0, 1024), axis=AXL.X), deps=[tpR])
        ckpt(3)
        cpA = ph.add("vector", lambda e, A=A, tmp1=tmp1: e.tensor_scalar(
            tmp1, bap(A, 0, 1024), 1.0, None, AOT.mult),
            deps=[tpR] + ([x1_ops[T - 2]] if T >= 2 else []))
        sq = ph.add("scalar", lambda e, F=F, tmp1=tmp1, x1sl=x1sl:
                    e.activation(x1sl, tmp1, ACTF.Square,
                                 accum_out=s_st[:, F + 1:F + 2]),
                    deps=[cpA]
                    + ([tpb_ops[T - 2], dmax_ops[T - 2]] if T >= 2 else []))
        ckpt(4)
        mu = ph.add("vector", lambda e, F=F: e.tensor_scalar(
            s_st[:, F:F + 2], s_st[:, F:F + 2], 1.0 / C, None, AOT.mult),
            deps=[rsum, sq])
        v2 = ph.add("vector", lambda e, F=F: e.tensor_scalar(
            s_st[:, F + 2:F + 3], s_st[:, F:F + 1], s_st[:, F:F + 1],
            s_st[:, F + 1:F + 2], AOT.mult, AOT.subtract), deps=[mu])
        v3 = ph.add("vector", lambda e, F=F: e.tensor_scalar(
            s_st[:, F + 2:F + 3], s_st[:, F + 2:F + 3], -1.0, LN_EPS,
            AOT.mult, AOT.add), deps=[v2])
        v4 = ph.add("vector", lambda e, F=F: e.reciprocal(
            s_st[:, F + 2:F + 3], s_st[:, F + 2:F + 3]), deps=[v3])
        v5 = ph.add("scalar", lambda e, F=F: e.activation(
            s_st[:, F + 3:F + 4], s_st[:, F + 2:F + 3], ACTF.Sqrt), deps=[v4])
        v6 = ph.add("vector", lambda e, F=F: e.tensor_scalar(
            s_st[:, F + 4:F + 5], s_st[:, F:F + 1], -1.0, s_st[:, F + 3:F + 4],
            AOT.mult, AOT.mult), deps=[v5])
        ckpt(5)
        ln1a = ph.add("vector", lambda e, F=F, tmp1=tmp1: e.tensor_scalar(
            tmp1, tmp1, s_st[:, F + 3:F + 4], None, AOT.mult),
            deps=[v6, sq])
        ln1 = ph.add("vector", lambda e, F=F, tmp1=tmp1: e.tensor_scalar(
            tmp1, tmp1, s_st[:, F + 4:F + 5], None, AOT.add), deps=[ln1a])
        ln1_ops[T] = ln1
        ckpt(6)
        g1 = ph.add("vector", lambda e, tmp1=tmp1: e.tensor_tensor(
            tmp1, tmp1, gamb, AOT.mult),
            deps=[ln1] + (gbe if T == 0 else []))
        b1 = ph.add("vector", lambda e, tmp1=tmp1: e.tensor_tensor(
            tmp1, tmp1, betb, AOT.add), deps=[g1])
        s2 = ph.add("vector", lambda e, B=B, tmp1=tmp1: e.tensor_tensor(
            tmp1, tmp1, bap(B, 0, 1024), AOT.add), deps=[b1, tpX])
        s2_ops[T] = s2
        ckpt(8)
        rs2 = ph.add("vector", lambda e, F=F, tmp1=tmp1: e.reduce_sum(
            s_st[:, F + 5:F + 6], tmp1, axis=AXL.X), deps=[s2])
        sq2 = ph.add("scalar", lambda e, F=F, tmp1=tmp1, x1sl=x1sl:
                     e.activation(x1sl, tmp1, ACTF.Square,
                                  accum_out=s_st[:, F + 6:F + 7]),
                     deps=[s2]
                     + ([tpb_ops[T - 2], dmax_ops[T - 2]] if T >= 2 else []))
        mu2 = ph.add("vector", lambda e, F=F: e.tensor_scalar(
            s_st[:, F + 5:F + 7], s_st[:, F + 5:F + 7], 1.0 / C, None,
            AOT.mult), deps=[rs2, sq2])
        w2_ = ph.add("vector", lambda e, F=F: e.tensor_scalar(
            s_st[:, F + 7:F + 8], s_st[:, F + 5:F + 6], s_st[:, F + 5:F + 6],
            s_st[:, F + 6:F + 7], AOT.mult, AOT.subtract), deps=[mu2])
        w3_ = ph.add("vector", lambda e, F=F: e.tensor_scalar(
            s_st[:, F + 7:F + 8], s_st[:, F + 7:F + 8], -1.0, LN_EPS,
            AOT.mult, AOT.add), deps=[w2_])
        w4_ = ph.add("vector", lambda e, F=F: e.reciprocal(
            s_st[:, F + 7:F + 8], s_st[:, F + 7:F + 8]), deps=[w3_])
        w5_ = ph.add("scalar", lambda e, F=F: e.activation(
            s_st[:, F + 8:F + 9], s_st[:, F + 7:F + 8], ACTF.Sqrt), deps=[w4_])
        w6_ = ph.add("vector", lambda e, F=F: e.tensor_scalar(
            s_st[:, F + 9:F + 10], s_st[:, F + 5:F + 6], -1.0,
            s_st[:, F + 8:F + 9], AOT.mult, AOT.mult), deps=[w5_])
        x1o = ph.add("vector", lambda e, F=F, tmp1=tmp1, x1sl=x1sl:
                     e.tensor_scalar(x1sl, tmp1,
                                     s_st[:, F + 8:F + 9],
                                     s_st[:, F + 9:F + 10],
                                     AOT.mult, AOT.add), deps=[w6_, sq2])
        x1_ops[T] = x1o
        ckpt(10)
        dmx = ph.add("sync", lambda e, T=T, x1sl=x1sl: e.dma_start(
            x1row.ap()[128 * T:128 * (T + 1), :], x1sl),
            deps=[x1o], dma=True)
        dmax_ops[T] = dmx
        xrow_d.append(dmx)
        if T >= 3 and (T + 1) % 4 == 0 and T >= 4:
            pass
    if _P4N < 16:
        raise _StopEmit(nc, dbgref[0], sems, SCHED + list(ph.ops))

    def mtb15(e):
        for k in range(NCH):
            ins = e.transpose(bap(pS2b, k * 128, 128),
                              s_x1t[:, 1, k * 128:(k + 1) * 128], IDENT)
        return ins
    tpb15 = ph.add("tensor", mtb15, deps=[x1_ops[15], bke_ops[13]])
    bke15 = ph.add("scalar", lambda e: e.activation(
        bass.AP(s_r1, 128 * 15, [[NCH * PTOK, 128], [PTOK, NCH], [1, 128]]),
        bap(pS2b, 0, 1024).rearrange("p (k c) -> p k c", k=NCH),
        ACTF.Copy), deps=[tpb15])
    bke_ops[15] = bke15

    for j in range(4):
        bank = pf[4 + (j % 2)]

        def mks(e, j=j, bank=bank):
            for k in range(NCH):
                ins = e.matmul(bank[0:8, 0:512], s_wgtb[:, k * 8:(k + 1) * 8],
                               rap(k, j), start=(k == 0), stop=(k == NCH - 1))
            return ins
        mm = ph.add("tensor", mks,
                    deps=[bke_ops[4 * j + 3], prevgb[4 + (j % 2)],
                          sc_ev[j - 2] if j >= 2 else None])
        ev = ph.add("vector", lambda e, bank=bank: e.tensor_copy(
            s_scst[0:8, 0:512], bank[0:8, 0:512]), deps=[mm, prev_scst])
        dsc = ph.add("sync", lambda e, j=j: e.dma_start(
            sc_in.ap()[:, 512 * j:512 * (j + 1)], s_scst[0:8, 0:512]),
            deps=[ev], dma=True)
        prev_scst = dsc
        sc_ev.append(dsc)
    d_sc = sc_ev[-1]
    cc_sc = ph.add("gpsimd", lambda e: e.collective_compute(
        "AllToAll", AOT.bypass, replica_groups=RG8,
        ins=[sc_in.ap()], outs=[sc_out.ap()]), deps=[d_sc], coll=True)
    cc_x1 = ph.add("gpsimd", lambda e: e.collective_compute(
        "AllGather", AOT.bypass, replica_groups=RG8,
        ins=[x1row.ap()], outs=[x1all.ap()[0:NTOK, :]]),
        deps=xrow_d + [cc_sc], coll=True)
    if DEBUG:
        ph.add("gpsimd", lambda e: e.dma_start(
            dbg["d_x1"].ap().rearrange("(k p) w -> p k w", p=128),
            s_r1.ap()), deps=[bke_ops[15], cc_x1], dma=True)
    run(ph)

    stPa.close()
    stB3.close()
    stB.close()
    stA.close()
    stZ.close()

    stC = ExitStack()
    s_wsl = SB(stC, "s_wsl", [128, 16], F32)
    s_gidx = SB(stC, "s_gidx", [128, 16], U32)
    s_ydst = SB(stC, "s_ydst", [128, 16], U32)
    s_sdst = SB(stC, "s_sdst", [128, 24], U32)
    s_lidf = SB(stC, "s_lidf", [128, 16], F32)

    stW = ExitStack()
    s_w1 = SB(stW, "s_w1", [128, NCH, HF])
    s_w3 = SB(stW, "s_w3", [128, NCH, HF])
    s_w2 = SB(stW, "s_w2", [128, 16, C])

    stC2 = ExitStack()
    s_wtok = SB(stC2, "s_wtok", [128, 128], F32)
    s_cmp = SB(stC2, "s_cmp", [128, 128], F32)
    s_rl = SB(stC2, "s_rl", [128, 128], F32)
    s_d2 = SB(stC2, "s_d2", [128, 128], F32)
    s_O = SB(stC2, "s_O", [128, 8], F32)
    s_B = SB(stC2, "s_B", [128, 8], F32)
    s_csel = SB(stC2, "s_csel", [128, 128], F32)
    s_esel = SB(stC2, "s_esel", [128, 128], F32)
    s_tokf = SB(stC2, "s_tokf", [128, 16], F32)
    stC2b = ExitStack()
    s_smine = SB(stC2b, "s_smine", [128, 128], F32)
    s_mask = SB(stC2b, "s_mask", [128, 128], F32)
    s_sml = SB(stC2b, "s_sml", [128, 16], F32)
    s_sb1 = SB(stC2b, "s_sb1", [1, 16], F32)
    s_cs = SB(stC2b, "s_cs", [128, 1], F32)
    s_cnt = SB(stC2b, "s_cnt", [128, 8], F32)
    s_P1 = SB(stC2b, "s_P1", [128, 128], F32)
    s_B2 = SB(stC2b, "s_B2", [128, 4], F32)
    s_row = SB(stC2b, "s_row", [1, 128], F32)

    stPb = ExitStack()
    qf = [PS(stPb, f"qf{i}", [128, 512], F32) for i in range(4)]

    FZ5 = fence(excl=("gpsimd",))
    FZ5c = fence(excl=("gpsimd", "sync"))
    ph = phase()
    wl = []
    for dst, wsrc in [(s_w1, w1td), (s_w3, w3td)]:
        wl.append(ph.add("gpsimd", lambda e, d=dst, s=wsrc: e.dma_start(
            d.ap(),
            s.ap().rearrange("(k p) h -> p k h", p=128)),
            deps=FZ5 + [cc_x1], dma=True))
    wl.append(ph.add("gpsimd", lambda e: e.dma_start(
        s_w2.ap(),
        w2td.ap().rearrange("(k p) c -> p k c", p=128)), dma=True))

    ls = ph.add("scalar", lambda e: e.dma_start(s_smine[:, :], sc_out.ap()),
                deps=[cc_sc] + FZ5c, dma=True)
    r1_ = ph.add("vector", lambda e: e.reduce_max(
        s_sml[:, 0:1], s_smine[:, :], axis=AXL.X), deps=[ls])
    r2_ = ph.add("vector", lambda e: e.tensor_reduce(
        s_sml[:, 1:2], s_smine[:, :], axis=AXL.X, op=AOT.min), deps=[ls])
    tr_ = ph.add("tensor", lambda e: e.transpose(
        qf[0][0:1, 0:128], s_sml[:, 0:1], IDENTF), deps=[r1_])
    tr2_ = ph.add("tensor", lambda e: e.transpose(
        qf[1][0:1, 0:128], s_sml[:, 1:2], IDENTF), deps=[r2_])
    mx_ = ph.add("vector", lambda e: e.reduce_max(
        qf[0][0:1, 128:129], qf[0][0:1, 0:128], axis=AXL.X), deps=[tr_])
    mn_ = ph.add("vector", lambda e: e.tensor_reduce(
        qf[1][0:1, 128:129], qf[1][0:1, 0:128], axis=AXL.X, op=AOT.min),
        deps=[tr2_])
    cpa = ph.add("vector", lambda e: e.tensor_copy(s_sb1[0:1, 0:1],
                                                   qf[0][0:1, 128:129]),
                 deps=[mx_])
    cpb = ph.add("vector", lambda e: e.tensor_copy(s_sb1[0:1, 1:2],
                                                   qf[1][0:1, 128:129]),
                 deps=[mn_])
    bc_ = ph.add("tensor", lambda e: e.matmul(
        qf[1][:, 0:2], ones1r, s_sb1[0:1, 0:2], start=True, stop=True),
        deps=[cpa, cpb])
    lo1 = ph.add("vector", lambda e: e.tensor_copy(s_sml[:, 3:4],
                                                   qf[1][:, 0:1]), deps=[bc_])
    lo2 = ph.add("vector", lambda e: e.tensor_copy(s_sml[:, 2:3],
                                                   qf[1][:, 1:2]), deps=[bc_])
    ng = ph.add("vector", lambda e: e.tensor_scalar(
        s_sml[:, 7:8], s_sml[:, 3:4], -1.0, None, AOT.mult), deps=[lo1])
    cz0 = ph.add("vector", lambda e: e.memset(s_cnt[:, 0:1], 0.0), deps=FZ5)
    st1 = ph.add("vector", lambda e: e.tensor_tensor(
        s_sml[:, 4:5], s_sml[:, 3:4], s_sml[:, 2:3], AOT.subtract),
        deps=[lo1, lo2])
    st2 = ph.add("vector", lambda e: e.tensor_scalar(
        s_sml[:, 4:5], s_sml[:, 4:5], 0.125, None, AOT.mult), deps=[st1])
    st3 = ph.add("vector", lambda e: e.reciprocal(
        s_sml[:, 5:6], s_sml[:, 4:5]), deps=[st2])
    dep = st3
    su2 = None
    for rnd in range(7):
        uu = ph.add("vector", lambda e: e.tensor_scalar(
            s_cmp[:, :], s_smine[:, :], s_sml[:, 2:3], s_sml[:, 5:6],
            AOT.subtract, AOT.mult), deps=[dep, su2])
        cs = []
        for jj in range(1, 8):
            cj = ph.add("vector", lambda e, jj=jj: e.tensor_scalar(
                s_esel[:, :], s_cmp[:, :], float(jj), None, AOT.is_ge,
                op1=AOT.add, accum_out=s_cnt[:, jj:jj + 1]), deps=[uu])
            cs.append(cj)
        smm = ph.add("tensor", lambda e: e.matmul(
            qf[2][0:8, 0:1], s_cnt[:, :], ones_f, start=True, stop=True),
            deps=cs + [cz0])
        sel = ph.add("vector", lambda e: e.tensor_scalar(
            s_O[0:8, 5:6], qf[2][0:8, 0:1], float(NSLOT) - 0.5, None,
            AOT.is_ge), deps=[smm])
        lmm = ph.add("tensor", lambda e: e.matmul(
            qf[2][0:1, 2:3], s_O[0:8, 5:6], ones_f[0:8, 0:1],
            start=True, stop=True), deps=[sel])
        lcp = ph.add("vector", lambda e: e.tensor_copy(
            s_sb1[0:1, 2:3], qf[2][0:1, 2:3]), deps=[lmm])
        lbc = ph.add("tensor", lambda e: e.matmul(
            qf[3][:, 0:1], ones1r, s_sb1[0:1, 2:3], start=True, stop=True),
            deps=[lcp])
        lev = ph.add("vector", lambda e: e.tensor_copy(
            s_sml[:, 8:9], qf[3][:, 0:1]), deps=[lbc])
        l1 = ph.add("vector", lambda e: e.tensor_tensor(
            s_sml[:, 8:9], s_sml[:, 8:9], s_sml[:, 4:5], AOT.mult), deps=[lev])
        l2 = ph.add("vector", lambda e: e.tensor_tensor(
            s_sml[:, 2:3], s_sml[:, 2:3], s_sml[:, 8:9], AOT.add), deps=[l1])
        dep = l2
        if rnd < 6:
            ph.add("vector", lambda e: e.tensor_scalar(
                s_sml[:, 4:5], s_sml[:, 4:5], 0.125, None, AOT.mult),
                deps=[l1])
            su2 = ph.add("vector", lambda e: e.tensor_scalar(
                s_sml[:, 5:6], s_sml[:, 5:6], 8.0, None, AOT.mult),
                deps=[uu])
    tm2 = dep
    eLO = ph.add("vector", lambda e: e.tensor_tensor(
        s_sml[:, 11:12], s_sml[:, 2:3], s_sml[:, 4:5], AOT.subtract),
        deps=[dep])
    eHI = ph.add("vector", lambda e: e.tensor_scalar(
        s_sml[:, 12:13], s_sml[:, 4:5], 2.0, s_sml[:, 2:3], AOT.mult,
        AOT.add), deps=[dep])
    mk_ = ph.add("vector", lambda e: e.tensor_scalar(
        s_mask[:, :], s_smine[:, :], s_sml[:, 12:13], None, AOT.is_ge),
        deps=[eHI])
    cnd1 = ph.add("vector", lambda e: e.tensor_scalar(
        s_cmp[:, :], s_smine[:, :], s_sml[:, 11:12], None, AOT.is_ge),
        deps=[eLO])
    cnd2 = ph.add("vector", lambda e: e.tensor_tensor(
        s_cmp[:, :], s_cmp[:, :], s_mask[:, :], AOT.subtract),
        deps=[cnd1, mk_])
    chi = ph.add("vector", lambda e: e.tensor_scalar(
        s_esel[:, :], s_mask[:, :], 0.5, None, AOT.is_ge, op1=AOT.add,
        accum_out=s_cnt[:, 1:2]), deps=[mk_, dep])
    qct = ph.add("vector", lambda e: e.tensor_scalar(
        s_esel[:, :], s_cmp[:, :], 0.5, None, AOT.is_ge, op1=AOT.add,
        accum_out=s_cnt[:, 2:3]), deps=[cnd2, chi])
    chm = ph.add("tensor", lambda e: e.matmul(
        qf[2][0:1, 3:4], s_cnt[:, 1:2], ones_f, start=True, stop=True),
        deps=[chi])
    qcm = ph.add("tensor", lambda e: e.matmul(
        qf[2][0:1, 4:5], s_cnt[:, 2:3], ones_f, start=True, stop=True),
        deps=[qct])
    chc = ph.add("vector", lambda e: e.tensor_copy(
        s_sb1[0:1, 4:5], qf[2][0:1, 3:4]), deps=[chm])
    qcc = ph.add("vector", lambda e: e.tensor_copy(
        s_sb1[0:1, 5:6], qf[2][0:1, 4:5]), deps=[qcm])
    cbc = ph.add("tensor", lambda e: e.matmul(
        qf[3][:, 2:4], ones1r, s_sb1[0:1, 4:6], start=True, stop=True),
        deps=[chc, qcc])
    cbe = ph.add("vector", lambda e: e.tensor_copy(
        s_sml[:, 13:15], qf[3][:, 2:4]), deps=[cbc])
    xm1 = ph.add("tensor", lambda e: e.transpose(
        qf[0][:, 0:128], s_cmp[:, :], IDENTF), deps=[cnd2, qct])
    xm2 = ph.add("vector", lambda e: e.tensor_copy(s_cmp[:, :],
                                                   qf[0][:, 0:128]),
                 deps=[xm1])
    xp1 = ph.add("tensor", lambda e: e.matmul(
        qf[1][:, 0:128], LTRIF, s_cmp[:, :], start=True, stop=True),
        deps=[xm2])
    xp2 = ph.add("vector", lambda e: e.tensor_copy(s_P1[:, :], qf[1][:, 0:128]),
                 deps=[xp1])
    xr1 = ph.add("tensor", lambda e: e.transpose(
        qf[2][:, 0:128], s_P1[:, :], IDENTF), deps=[xp2, chc, qcc])
    xr2 = ph.add("vector", lambda e: e.tensor_copy(s_rl[:, :], qf[2][:, 0:128]),
                 deps=[xr1])
    xt1 = ph.add("tensor", lambda e: e.matmul(
        qf[3][0:1, 0:128], e127, s_P1[:, :], start=True, stop=True),
        deps=[xp2, cbe])
    xt2 = ph.add("vector", lambda e: e.tensor_copy(s_row[0:1, :],
                                                   qf[3][0:1, 0:128]),
                 deps=[xt1])
    xu1 = ph.add("tensor", lambda e: e.matmul(
        qf[3][:, 256:257], s_row[0:1, :], ones_f[0:1, 0:1], start=True,
        stop=True), deps=[xt2])
    xu2 = ph.add("vector", lambda e: e.tensor_copy(s_O[:, 2:3],
                                                   qf[3][:, 256:257]),
                 deps=[xu1])
    xo1 = ph.add("tensor", lambda e: e.matmul(
        qf[3][:, 3:4], SLTRIF, s_O[:, 2:3], start=True, stop=True),
        deps=[xu2])
    xo2 = ph.add("vector", lambda e: e.tensor_copy(s_O[:, 0:1], qf[3][:, 3:4]),
                 deps=[xo1])
    xoc = ph.add("vector", lambda e: e.tensor_copy(s_O[:, 2:3], piota),
                 deps=[xo1, xu2])
    xc1 = ph.add("vector", lambda e: e.tensor_scalar(
        s_cmp[:, :], KONST, 0.0, s_O[:, 0:1], AOT.add, AOT.is_ge),
        deps=[xo2, xm2])
    xd1 = ph.add("tensor", lambda e: e.matmul(
        qf[3][:, 128:256], DMAT, s_cmp[:, :], start=True, stop=True),
        deps=[xc1])
    xd2 = ph.add("vector", lambda e: e.tensor_copy(s_d2[:, :],
                                                   qf[3][:, 128:256]),
                 deps=[xd1])
    xb1 = ph.add("tensor", lambda e: e.matmul(
        qf[0][:, 0:4], s_d2[:, :], s_O[:, 0:4], start=True, stop=True),
        deps=[xd2, xoc])
    xb2 = ph.add("vector", lambda e: e.tensor_copy(s_B[:, 0:4], qf[0][:, 0:4]),
                 deps=[xb1])
    xs1 = ph.add("tensor", lambda e: e.matmul(
        qf[1][:, 0:128], s_d2[:, :], s_rl[:, :], start=True, stop=True),
        deps=[xd2, xr2, xp1])
    xq1 = ph.add("vector", lambda e: e.tensor_scalar(
        s_B[:, 4:5], piota, 0.0, s_B[:, 0:1], AOT.add, AOT.subtract),
        deps=[xb2])
    xcs = ph.add("vector", lambda e: e.tensor_scalar(
        s_csel[:, :], qf[1][:, 0:128], s_B[:, 4:5], None, AOT.is_le),
        deps=[xs1, xq1])
    xgg = ph.add("vector", lambda e: e.reduce_sum(
        s_B[:, 5:6], s_csel[:, :], axis=AXL.X), deps=[xcs])
    xtk = ph.add("vector", lambda e: e.tensor_scalar(
        s_B2[:, 0:1], s_B[:, 2:3], 128.0, s_B[:, 5:6], AOT.mult, AOT.add),
        deps=[xgg])
    xcl = ph.add("vector", lambda e: e.tensor_scalar(
        s_B2[:, 0:1], s_B2[:, 0:1], float(NTOK - 1), None, AOT.min),
        deps=[xtk])
    xpb = ph.add("vector", lambda e: e.tensor_copy(s_B2[:, 1:2], s_B[:, 2:3]),
                 deps=[xtk])
    xpg = ph.add("vector", lambda e: e.tensor_copy(s_B2[:, 2:3], s_B[:, 5:6]),
                 deps=[xtk])
    xhf0 = ph.add("vector", lambda e: e.tensor_scalar(
        s_csel[:, :], KONST, s_B2[:, 2:3], None, AOT.is_equal),
        deps=[xpg, xcs])
    xsc1 = ph.add("tensor", lambda e: e.matmul(
        qf[2][:, 128:256], s_d2[:, :], s_smine[:, :], start=True, stop=True),
        deps=[xd2, ls, chc, qcc])
    xsc2 = ph.add("vector", lambda e: e.tensor_copy(
        s_cmp[:, :], qf[2][:, 128:256]), deps=[xsc1, xd1])
    xsc3 = ph.add("vector", lambda e: e.tensor_tensor(
        s_cmp[:, :], s_cmp[:, :], s_csel[:, :], AOT.mult),
        deps=[xsc2, xhf0])
    xsc4 = ph.add("vector", lambda e: e.reduce_sum(
        s_cs[:, 0:1], s_cmp[:, :], axis=AXL.X), deps=[xsc3])
    xv1 = ph.add("tensor", lambda e: e.transpose(
        qf[0][0:1, 0:128], s_cs[:, 0:1], IDENTF), deps=[xsc4, xb1])
    xv2 = ph.add("vector", lambda e: e.tensor_copy(s_row[0:1, :],
                                                   qf[0][0:1, 0:128]),
                 deps=[xv1, xu1])
    xw1 = ph.add("tensor", lambda e: e.matmul(
        qf[1][:, 0:128], ones1r, s_row[0:1, :], start=True, stop=True),
        deps=[xv2, xcs])
    xw2 = ph.add("vector", lambda e: e.tensor_copy(s_esel[:, :],
                                                   qf[1][:, 0:128]),
                 deps=[xw1, qct])
    xv3 = ph.add("tensor", lambda e: e.transpose(
        qf[2][0:1, 0:128], s_B2[:, 0:1], IDENTF), deps=[xcl, xw1])
    xv4 = ph.add("vector", lambda e: e.tensor_copy(s_row[0:1, :],
                                                   qf[2][0:1, 0:128]),
                 deps=[xv3, xw1])
    xw3 = ph.add("tensor", lambda e: e.matmul(
        qf[3][:, 0:128], ones1r, s_row[0:1, :], start=True, stop=True),
        deps=[xv4, xt1])
    xw4 = ph.add("vector", lambda e: e.tensor_copy(s_P1[:, :], qf[3][:, 0:128]),
                 deps=[xw3, xs1])
    xgt = ph.add("vector", lambda e: e.tensor_scalar(
        s_csel[:, :], s_esel[:, :], s_cs[:, 0:1], None, AOT.is_gt),
        deps=[xw2, xcs])
    xeq = ph.add("vector", lambda e: e.tensor_scalar(
        s_cmp[:, :], s_esel[:, :], s_cs[:, 0:1], None, AOT.is_equal),
        deps=[xw2, xsc4])
    xil = ph.add("vector", lambda e: e.tensor_scalar(
        s_rl[:, :], s_P1[:, :], s_B2[:, 0:1], None, AOT.is_lt),
        deps=[xw4, xs1])
    xti = ph.add("vector", lambda e: e.tensor_tensor(
        s_cmp[:, :], s_cmp[:, :], s_rl[:, :], AOT.mult), deps=[xeq, xil])
    xr_ = ph.add("vector", lambda e: e.tensor_tensor(
        s_csel[:, :], s_csel[:, :], s_cmp[:, :], AOT.add), deps=[xgt, xti])
    xvr = ph.add("vector", lambda e: e.tensor_scalar(
        s_rl[:, :], KONST, s_sml[:, 14:15], None, AOT.is_lt),
        deps=[cbe, xti])
    xrm = ph.add("vector", lambda e: e.tensor_tensor(
        s_csel[:, :], s_csel[:, :], s_rl[:, :], AOT.mult), deps=[xr_, xvr])
    xrr = ph.add("vector", lambda e: e.reduce_sum(
        s_B[:, 0:1], s_csel[:, :], axis=AXL.X), deps=[xrm])
    xkp = ph.add("vector", lambda e: e.tensor_scalar(
        s_B[:, 1:2], s_B[:, 0:1], s_sml[:, 13:14], float(NSLOT) - 0.5,
        AOT.add, AOT.is_le), deps=[xrr, cbe])
    xvp = ph.add("vector", lambda e: e.tensor_scalar(
        s_B[:, 2:3], piota, s_sml[:, 14:15], None, AOT.is_lt),
        deps=[cbe, xb2])
    xkf = ph.add("vector", lambda e: e.tensor_tensor(
        s_B2[:, 3:4], s_B[:, 1:2], s_B[:, 2:3], AOT.mult), deps=[xkp, xvp])
    xhp = ph.add("vector", lambda e: e.tensor_scalar(
        s_esel[:, :], KONST, s_B2[:, 1:2], None, AOT.is_equal),
        deps=[xpb, xgt, xeq])
    xhf = ph.add("vector", lambda e: e.tensor_scalar(
        s_csel[:, :], KONST, s_B2[:, 2:3], None, AOT.is_equal),
        deps=[xpg, xrr])
    xhk = ph.add("vector", lambda e: e.tensor_scalar(
        s_esel[:, :], s_esel[:, :], s_B2[:, 3:4], None, AOT.mult),
        deps=[xhp, xkf])
    xum = ph.add("tensor", lambda e: e.matmul(
        qf[2][:, 0:128], s_esel[:, :], s_csel[:, :], start=True, stop=True),
        deps=[xhk, xhf, xv3])
    xue = ph.add("vector", lambda e: e.tensor_copy(s_d2[:, :],
                                                   qf[2][:, 0:128]),
                 deps=[xum])
    bet_ = ph.add("vector", lambda e: e.tensor_scalar(
        s_sml[:, 9:10], s_sml[:, 14:15], 128.5, None, AOT.is_le), deps=[cbe])
    nb_ = ph.add("vector", lambda e: e.tensor_scalar(
        s_sml[:, 5:6], s_sml[:, 9:10], -1.0, 1.0, AOT.mult, AOT.add),
        deps=[bet_])
    avl_ = ph.add("vector", lambda e: e.tensor_scalar(
        s_sml[:, 10:11], s_sml[:, 13:14], -1.0, float(NSLOT), AOT.mult,
        AOT.add), deps=[cbe])
    icn1 = ph.add("vector", lambda e: e.tensor_scalar(
        s_rl[:, :], s_smine[:, :], s_sml[:, 11:12], None, AOT.is_ge),
        deps=[eLO, ls, xrm])
    icn2 = ph.add("vector", lambda e: e.tensor_tensor(
        s_rl[:, :], s_rl[:, :], s_mask[:, :], AOT.subtract),
        deps=[icn1, mk_])
    pio_ = ph.add("vector", lambda e: e.tensor_scalar(
        s_sml[:, 8:9], piota, 128.0, None, AOT.mult), deps=[dep])
    itx1 = ph.add("vector", lambda e: e.tensor_scalar(
        s_P1[:, :], KONST, s_sml[:, 8:9], None, AOT.add),
        deps=[pio_, xil])
    itx2 = ph.add("vector", lambda e: e.tensor_tensor(
        s_P1[:, :], s_P1[:, :], s_rl[:, :], AOT.mult), deps=[itx1, icn2])
    itx3 = ph.add("vector", lambda e: e.tensor_scalar(
        s_cmp[:, :], s_rl[:, :], -1.0e6, 1.0e6, AOT.mult, AOT.add),
        deps=[icn2, xeq])
    itx4 = ph.add("vector", lambda e: e.tensor_tensor(
        s_P1[:, :], s_P1[:, :], s_cmp[:, :], AOT.add), deps=[itx2, itx3])
    iz_ = ph.add("vector", lambda e: e.tensor_scalar(
        s_sml[:, 6:7], piota, 0.0, None, AOT.mult), deps=[dep])
    dep2 = iz_
    prevu = xue
    for istep in (2048.0, 256.0, 32.0, 4.0, 0.5):
        uu2 = ph.add("vector", lambda e, istep=istep: e.tensor_scalar(
            s_esel[:, :], s_P1[:, :], s_sml[:, 6:7], 1.0 / istep,
            AOT.subtract, AOT.mult), deps=[itx4, dep2, xue])
        cs2 = []
        for jj in range(1, 8):
            cj2 = ph.add("vector", lambda e, jj=jj: e.tensor_scalar(
                s_csel[:, :], s_esel[:, :], float(jj), None, AOT.is_lt,
                op1=AOT.add, accum_out=s_cnt[:, jj:jj + 1]),
                deps=[uu2] + ([xue] if jj == 1 else []))
            cs2.append(cj2)
        sm2 = ph.add("tensor", lambda e: e.matmul(
            qf[2][0:8, 0:1], s_cnt[:, :], ones_f, start=True, stop=True),
            deps=cs2 + [cz0, prevu])
        se2 = ph.add("vector", lambda e: e.tensor_scalar(
            s_O[0:8, 5:6], qf[2][0:8, 0:1], s_sml[0:8, 10:11], None,
            AOT.is_le), deps=[sm2, avl_])
        lm2 = ph.add("tensor", lambda e: e.matmul(
            qf[2][0:1, 2:3], s_O[0:8, 5:6], ones_f[0:8, 0:1],
            start=True, stop=True), deps=[se2])
        lc2 = ph.add("vector", lambda e: e.tensor_copy(
            s_sb1[0:1, 2:3], qf[2][0:1, 2:3]), deps=[lm2])
        lb2 = ph.add("tensor", lambda e: e.matmul(
            qf[3][:, 0:1], ones1r, s_sb1[0:1, 2:3], start=True, stop=True),
            deps=[lc2])
        le2 = ph.add("vector", lambda e: e.tensor_copy(
            s_sml[:, 4:5], qf[3][:, 0:1]), deps=[lb2])
        up2 = ph.add("vector", lambda e, istep=istep: e.tensor_scalar(
            s_sml[:, 6:7], s_sml[:, 4:5], istep, s_sml[:, 6:7], AOT.mult,
            AOT.add), deps=[le2])
        dep2 = up2
        prevu = lc2
    imask = ph.add("vector", lambda e: e.tensor_scalar(
        s_esel[:, :], s_P1[:, :], s_sml[:, 6:7], 0.0, AOT.subtract,
        AOT.is_lt), deps=[dep2, itx4])
    bl1 = ph.add("vector", lambda e: e.tensor_scalar(
        s_d2[:, :], s_d2[:, :], s_sml[:, 9:10], None, AOT.mult),
        deps=[xue, bet_])
    bl2 = ph.add("vector", lambda e: e.tensor_scalar(
        s_esel[:, :], s_esel[:, :], s_sml[:, 5:6], None, AOT.mult),
        deps=[imask, nb_])
    bl3 = ph.add("vector", lambda e: e.tensor_tensor(
        s_d2[:, :], s_d2[:, :], s_esel[:, :], AOT.add), deps=[bl1, bl2])
    mkup = ph.add("vector", lambda e: e.tensor_tensor(
        s_mask[:, :], s_mask[:, :], s_d2[:, :], AOT.add), deps=[bl3, cnd2])

    ex_ = ph.add("scalar", lambda e: e.activation(
        s_wtok[:, :], s_smine[:, :], ACTF.Exp, bias=s_sml[:, 7:8]),
        deps=[ng, ls])
    wm_ = ph.add("vector", lambda e: e.tensor_tensor(
        s_wtok[:, :], s_wtok[:, :], s_mask[:, :], AOT.mult), deps=[ex_, mkup])
    zr = ph.add("vector", lambda e: e.reduce_sum(
        s_sml[:, 9:10], s_wtok[:, :], axis=AXL.X), deps=[wm_])
    ztr = ph.add("tensor", lambda e: e.transpose(
        qf[0][0:1, 256:384], s_sml[:, 9:10], IDENTF), deps=[zr])
    zs = ph.add("vector", lambda e: e.reduce_sum(
        qf[0][0:1, 384:385], qf[0][0:1, 256:384], axis=AXL.X), deps=[ztr])
    zc = ph.add("vector", lambda e: e.tensor_copy(s_sb1[0:1, 3:4],
                                                  qf[0][0:1, 384:385]),
                deps=[zs])
    zre = ph.add("vector", lambda e: e.reciprocal(s_sb1[0:1, 3:4],
                                                  s_sb1[0:1, 3:4]), deps=[zc])
    zbc = ph.add("tensor", lambda e: e.matmul(
        qf[1][:, 4:5], ones1r, s_sb1[0:1, 3:4], start=True, stop=True),
        deps=[zre])
    zev = ph.add("vector", lambda e: e.tensor_copy(s_sml[:, 10:11],
                                                   qf[1][:, 4:5]), deps=[zbc])
    wt_ = ph.add("vector", lambda e: e.tensor_scalar(
        s_wtok[:, :], s_wtok[:, :], s_sml[:, 10:11], None, AOT.mult),
        deps=[zev, wm_])

    mtr = ph.add("tensor", lambda e: e.transpose(
        qf[0][:, 0:128], s_mask[:, :], IDENTF), deps=[mkup, xum])
    mte = ph.add("vector", lambda e: e.tensor_copy(s_cmp[:, :], qf[0][:, 0:128]),
                 deps=[mtr])
    p1m = ph.add("tensor", lambda e: e.matmul(
        qf[1][:, 0:128], LTRIF, s_cmp[:, :], start=True, stop=True),
        deps=[mte])
    p1e = ph.add("vector", lambda e: e.tensor_copy(s_P1[:, :], qf[1][:, 0:128]),
                 deps=[p1m])
    rlt = ph.add("tensor", lambda e: e.transpose(
        qf[2][:, 0:128], s_P1[:, :], IDENTF), deps=[p1e])
    rle = ph.add("vector", lambda e: e.tensor_copy(s_rl[:, :], qf[2][:, 0:128]),
                 deps=[rlt])
    trow = ph.add("tensor", lambda e: e.matmul(
        qf[3][0:1, 0:128], e127, s_P1[:, :], start=True, stop=True),
        deps=[p1e])
    trowe = ph.add("vector", lambda e: e.tensor_copy(s_row[0:1, :],
                                                     qf[3][0:1, 0:128]),
                   deps=[trow])
    tpm = ph.add("tensor", lambda e: e.matmul(
        qf[3][:, 256:257], s_row[0:1, :], ones_f[0:1, 0:1], start=True,
        stop=True), deps=[trowe])
    tpe = ph.add("vector", lambda e: e.tensor_copy(s_O[:, 2:3],
                                                   qf[3][:, 256:257]),
                 deps=[tpm])
    omm = ph.add("tensor", lambda e: e.matmul(
        qf[3][:, 3:4], SLTRIF, s_O[:, 2:3], start=True, stop=True), deps=[tpe])
    oev = ph.add("vector", lambda e: e.tensor_copy(s_O[:, 0:1], qf[3][:, 3:4]),
                 deps=[omm])
    o16 = ph.add("tensor", lambda e: e.matmul(
        qf[3][:, 4:5], G16F, s_O[:, 0:1], start=True, stop=True), deps=[oev])
    o16e = ph.add("vector", lambda e: e.tensor_copy(s_O[:, 1:2], qf[3][:, 4:5]),
                  deps=[o16])
    ocp = ph.add("vector", lambda e: e.tensor_copy(s_O[:, 2:3], piota),
                 deps=[omm])
    ocp2 = ph.add("vector", lambda e: e.tensor_copy(s_O[:, 3:4], ownidx),
                  deps=[omm])

    qprev = {0: None, 1: None, 2: None}
    prev_dread = [None]
    ydst_cp = []
    lid_ops = []
    gidx_cp = []
    for t in range(16):
        c1 = ph.add("vector", lambda e, t=t: e.tensor_scalar(
            s_cmp[:, :], KONST, float(128 * t), s_O[:, 0:1], AOT.add,
            AOT.is_ge), deps=[oev, ocp, p1m if t == 0 else None])
        dmm_ = ph.add("tensor", lambda e: e.matmul(
            qf[3][:, 128:256], DMAT, s_cmp[:, :], start=True, stop=True),
            deps=[c1, prev_dread[0]])
        d2_ = ph.add("vector", lambda e: e.tensor_copy(
            s_d2[:, :], qf[3][:, 128:256]), deps=[dmm_])
        prev_dread[0] = d2_
        bm = ph.add("tensor", lambda e: e.matmul(
            qf[0][:, 0:4], s_d2[:, :], s_O[:, 0:4], start=True, stop=True),
            deps=[d2_, o16e, ocp2, qprev[0]])
        be = ph.add("vector", lambda e: e.tensor_copy(s_B[:, 0:4],
                                                      qf[0][:, 0:4]), deps=[bm])
        qprev[0] = be
        rsm = ph.add("tensor", lambda e: e.matmul(
            qf[1][:, 0:128], s_d2[:, :], s_rl[:, :], start=True, stop=True),
            deps=[d2_, rle, qprev[1]])
        wsm = ph.add("tensor", lambda e: e.matmul(
            qf[2][:, 0:128], s_d2[:, :], s_wtok[:, :], start=True, stop=True),
            deps=[d2_, wt_, qprev[2]])
        q1 = ph.add("vector", lambda e, t=t: e.tensor_scalar(
            s_B[:, 4:5], piota, float(128 * t), s_B[:, 0:1], AOT.add,
            AOT.subtract), deps=[be])
        cse = ph.add("vector", lambda e: e.tensor_scalar(
            s_csel[:, :], qf[1][:, 0:128], s_B[:, 4:5], None, AOT.is_le),
            deps=[rsm, q1])
        qprev[1] = cse
        gg = ph.add("vector", lambda e: e.reduce_sum(
            s_B[:, 5:6], s_csel[:, :], axis=AXL.X), deps=[cse])
        tk = ph.add("vector", lambda e, t=t: e.tensor_scalar(
            s_tokf[:, t:t + 1], s_B[:, 2:3], 128.0, s_B[:, 5:6], AOT.mult,
            AOT.add), deps=[gg])
        e0 = ph.add("vector", lambda e: e.tensor_scalar(
            s_esel[:, 0:1], s_csel[:, 0:1], -1.0, 1.0, AOT.mult, AOT.add),
            deps=[cse])
        e1_ = ph.add("vector", lambda e: e.tensor_tensor(
            s_esel[:, 1:128], s_csel[:, 0:127], s_csel[:, 1:128], AOT.subtract),
            deps=[cse, e0])
        ws1 = ph.add("vector", lambda e: e.tensor_tensor(
            s_esel[:, :], s_esel[:, :], qf[2][:, 0:128], AOT.mult),
            deps=[e1_, wsm])
        qprev[2] = ws1
        ws2 = ph.add("vector", lambda e, t=t: e.reduce_sum(
            s_wsl[:, t:t + 1], s_esel[:, :], axis=AXL.X), deps=[ws1])
        rk = ph.add("vector", lambda e, t=t: e.tensor_scalar(
            s_B[:, 7:8], piota, float(128 * t), s_B[:, 1:2], AOT.add,
            AOT.subtract), deps=[be])
        yd = ph.add("vector", lambda e: e.tensor_scalar(
            s_B[:, 6:7], s_B[:, 3:4], float(YBLK), s_B[:, 7:8], AOT.mult,
            AOT.add), deps=[rk, tk])
        va = ph.add("vector", lambda e, t=t: e.tensor_scalar(
            s_B[:, 5:6], s_tokf[:, t:t + 1], float(NTOK) - 0.5, None,
            AOT.is_le), deps=[yd])
        vb = ph.add("vector", lambda e: e.tensor_scalar(
            s_B[:, 4:5], s_B[:, 7:8], float(YBLK) - 0.5, None, AOT.is_le),
            deps=[yd, cse])
        vv = ph.add("vector", lambda e: e.tensor_tensor(
            s_B[:, 5:6], s_B[:, 5:6], s_B[:, 4:5], AOT.mult), deps=[va, vb])
        ydm = ph.add("vector", lambda e: e.tensor_tensor(
            s_B[:, 6:7], s_B[:, 6:7], s_B[:, 5:6], AOT.mult), deps=[vv])
        nv = ph.add("vector", lambda e: e.tensor_scalar(
            s_B[:, 5:6], s_B[:, 5:6], -1.0, 1.0, AOT.mult, AOT.add),
            deps=[ydm])
        tr2 = ph.add("vector", lambda e: e.tensor_scalar(
            s_B[:, 4:5], piota, float(8 * YBLK), None, AOT.add), deps=[vb])
        trm = ph.add("vector", lambda e: e.tensor_tensor(
            s_B[:, 4:5], s_B[:, 4:5], s_B[:, 5:6], AOT.mult), deps=[nv, tr2])
        ydf = ph.add("vector", lambda e: e.tensor_tensor(
            s_B[:, 6:7], s_B[:, 6:7], s_B[:, 4:5], AOT.add), deps=[trm])
        ydcp = ph.add("vector", lambda e, t=t: e.tensor_copy(
            s_ydst[:, t:t + 1], s_B[:, 6:7]), deps=[ydf])
        ydst_cp.append(ydcp)
        lid = ph.add("vector", lambda e, t=t: e.tensor_scalar(
            s_lidf[:, t:t + 1], s_B[:, 3:4], -float(PTOK),
            s_tokf[:, t:t + 1], AOT.mult, AOT.add), deps=[yd, tk])
        lid_ops.append(lid)
        gcp = ph.add("vector", lambda e, t=t: e.tensor_copy(
            s_gidx[:, t:t + 1], s_tokf[:, t:t + 1]), deps=[tk])
        gidx_cp.append(gcp)

    if DEBUG:
        ph.add("gpsimd", lambda e: e.dma_start(
            dbg["d_thr"].ap(), s_sml[:, :]), deps=[tm2, cbe], dma=True)
        ph.add("gpsimd", lambda e: e.dma_start(
            dbg["d_mask"].ap(), s_mask[:, :]), deps=[mkup], dma=True)
        ph.add("gpsimd", lambda e: e.dma_start(
            dbg["d_sm"].ap(), s_smine[:, :]), deps=[mk_], dma=True)
    run(ph, auto_fence=False)

    if DEBUG:
        ph = phase()
        ph.add("gpsimd", lambda e: e.dma_start(
            dbg["d_ydst"].ap(), s_ydst[:, :]), dma=True)
        for nm, sb in [("d_tok", s_tokf), ("d_wsl", s_wsl)]:
            ph.add("gpsimd", lambda e, nm=nm, sb=sb: e.dma_start(
                dbg[nm].ap(), sb[:, :]), dma=True)
        run(ph)

    stC2b.close()
    stD = ExitStack()
    s_xg = SB(stD, "s_xg", [128, NCH, NSLOT])
    stD2 = ExitStack()
    s_xgr = SB(stD2, "s_xgr", [128, 2, 1024])
    stPc = ExitStack()
    pTb = [PS(stPc, f"pTb{i}", [128, 1024], BF16) for i in range(2)]

    CPL6 = fence(excl=("gpsimd", "sync", "scalar"))
    FZ6 = fence() if DEBUG else []
    ph = phase()
    gts = []
    prev_ev2 = None
    for t in range(16):
        par = t % 2
        g = ph.add("gpsimd", lambda e, t=t, par=par: e.indirect_dma_start(
            s_xgr[:, par, :], None, x1all.ap(),
            IOA(s_gidx[:, t:t + 1], 0)),
            deps=[cc_x1, gidx_cp[t]] + (FZ6 if t == 0 else [])
            + ([gts[t - 2][1]] if t >= 2 else []), dma=True)

        def mkt(e, par=par):
            for j in range(NCH):
                ins = e.transpose(pTb[par][:, j * 128:(j + 1) * 128],
                                  s_xgr[:, par, j * 128:(j + 1) * 128], IDENT)
            return ins
        tp = ph.add("tensor", mkt,
                    deps=[g] + ([gts[t - 2][2], gts[t - 2][1]] if t >= 2
                                else []))
        ev = ph.add("scalar", lambda e, t=t, par=par: e.activation(
            bass.AP(s_xg, 128 * t, [[NCH * NSLOT, 128], [NSLOT, NCH], [1, 128]]),
            pTb[par][:, :].rearrange("p (k c) -> p k c", k=NCH), ACTF.Copy),
            deps=[tp])
        gts.append((g, ev, tp))
    run(ph, auto_fence=False)
    stPc.close()
    stPb.close()
    if DEBUG:
        ph = phase()
        ph.add("gpsimd", lambda e: e.dma_start(
            dbg["d_xg"].ap().rearrange("p (k w) -> p k w", k=NCH), s_xg.ap()), dma=True)
        run(ph)

    stD2.close()

    stE = ExitStack()
    s_hh = SB(stE, "s_hh", [128, 16, NSLOT])
    s_sil = SB(stE, "s_sil", [128, 2, 512])
    s_y = SB(stE, "s_y", [128, 2, 1024])
    stPd = ExitStack()
    af = [PS(stPd, f"af{i}", [128, 512], F32) for i in range(4)]
    yf = [PS(stPd, f"yf{i}", [128, 512], F32) for i in range(4)]

    ph = phase()
    wdep = wl
    quads = [af, yf]
    evprev = {}
    hh_done = []
    gidx_n = 0
    for sh in range(2):
        for ht in range(16):
            qd = quads[(sh * 16 + ht) % 2]

            def mkA(e, sh=sh, ht=ht, qd=qd):
                for kc in range(NCH):
                    st_ = (kc == 0)
                    sp = (kc == NCH - 1)
                    for wi, wt in ((0, s_w1), (2, s_w3)):
                        for pc in range(2):
                            ins = e.matmul(
                                qd[wi + pc][:, 0:512],
                                bass.AP(wt, kc * HF + 128 * ht,
                                        [[NCH * HF, 128], [1, 128]]),
                                bass.AP(s_xg, kc * NSLOT + 1024 * sh + 512 * pc,
                                        [[NCH * NSLOT, 128], [1, 512]]),
                                start=st_, stop=sp)
                return ins
            gi = sh * 16 + ht
            deps = list(wdep) + list(CPL6) if gi == 0 else []
            deps += evprev.get(gi - 2, [])
            if gi == 0:
                deps.append(gts[7][1])
            if gi == 1:
                deps.append(gts[15][1])
            if gi == 16:
                deps.append(gts[15][1])
            mmA = ph.add("tensor", mkA, deps=deps)
            prev_hh = hh_done[-1] if hh_done else None
            evs_g = []
            for piece in range(2):
                sl = ph.add("scalar", lambda e, piece=piece, qd=qd: e.activation(
                    s_sil[:, piece, :], qd[piece][:, 0:512], ACTF.Silu),
                    deps=[mmA, prev_hh])
                hhop = ph.add("vector",
                              lambda e, sh=sh, ht=ht, piece=piece, qd=qd:
                              e.tensor_tensor(
                                  bass.AP(s_hh, ht * NSLOT + 1024 * sh + 512 * piece,
                                          [[16 * NSLOT, 128], [1, 512]]),
                                  s_sil[:, piece, :], qd[2 + piece][:, 0:512],
                                  AOT.mult),
                              deps=[sl])
                evs_g.append(hhop)
                prev_hh = hhop
            evprev[gi] = evs_g
            hh_done.append(evs_g[-1])
    idprev = None
    for t in range(16):
        isc = ph.add("gpsimd", lambda e, t=t: e.indirect_dma_start(
            idstage.ap(), IOA(s_ydst[:, t:t + 1], 0),
            s_lidf[:, t:t + 1], None),
            deps=[ydst_cp[t], lid_ops[t], idprev]
            + ([idpf] if t == 0 else []), dma=True)
        idprev = isc
    cc_id = ph.add("gpsimd", lambda e: e.collective_compute(
        "AllToAll", AOT.bypass, replica_groups=RG8,
        ins=[idstage.ap()[0:8 * YBLK, :]], outs=[idrecv.ap()]),
        deps=[idprev], coll=True)
    sdld = None
    for m in range(24):
        sdld = ph.add("gpsimd", lambda e, m=m: e.dma_start(
            s_sdst[:, m:m + 1],
            idrecv.ap()[128 * m:128 * (m + 1), :]),
            deps=[cc_id] if m == 0 else [], dma=True)
    yprev = [None, None, None, None]
    sc_done = []
    for st_ in range(16):
        qsel = (st_ % 2) * 2

        def mkB(e, st_=st_, qsel=qsel):
            for ht in range(16):
                first = (ht == 0)
                last = (ht == 15)
                e.matmul(yf[qsel][:, 0:512],
                         bass.AP(s_hh, ht * NSLOT + 128 * st_,
                                 [[16 * NSLOT, 128], [1, 128]]),
                         bass.AP(s_w2, ht * C, [[16 * C, 128], [1, 512]]),
                         start=first, stop=last)
                ins = e.matmul(yf[qsel + 1][:, 0:512],
                               bass.AP(s_hh, ht * NSLOT + 128 * st_,
                                       [[16 * NSLOT, 128], [1, 128]]),
                               bass.AP(s_w2, ht * C + 512, [[16 * C, 128], [1, 512]]),
                               start=first, stop=last)
            return ins
        deps = hh_done if st_ == 0 else [hh_done[-1]]
        deps = list(deps) + [yprev[qsel], yprev[qsel + 1]]
        mmB = ph.add("tensor", mkB, deps=deps)
        par = st_ % 2
        scdep = [sc_done[st_ - 2]] if st_ >= 2 else []
        ev1 = ph.add("vector", lambda e, st_=st_, par=par, qsel=qsel:
                     e.tensor_scalar(s_y[:, par, 0:512], yf[qsel][:, 0:512],
                                     s_wsl[:, st_:st_ + 1], None, AOT.mult),
                     deps=[mmB] + scdep)
        ev2 = ph.add("vector", lambda e, st_=st_, par=par, qsel=qsel:
                     e.tensor_scalar(s_y[:, par, 512:1024],
                                     yf[qsel + 1][:, 0:512],
                                     s_wsl[:, st_:st_ + 1], None, AOT.mult),
                     deps=[mmB] + scdep)
        yprev[qsel] = ev1
        yprev[qsel + 1] = ev2
        sc = ph.add("gpsimd", lambda e, st_=st_, par=par: e.indirect_dma_start(
            ystage.ap(), IOA(s_ydst[:, st_:st_ + 1], 0),
            s_y[:, par, :], None),
            deps=[ev1, ev2] + ([sc_done[st_ - 2]] if st_ >= 2 else []), dma=True)
        sc_done.append(sc)
    run(ph, auto_fence=False)
    if DEBUG:
        ph = phase()
        ph.add("gpsimd", lambda e: e.dma_start(
            dbg["d_hh"].ap(), bass.AP(s_hh, 0, [[16 * NSLOT, 128], [1, 2048]])),
            dma=True)
        ph.add("gpsimd", lambda e: e.dma_start(
            dbg["d_sd"].ap(), s_sdst[:, :]), deps=[sdld], dma=True)
        run(ph)

    ph = phase()
    cc_y = ph.add("gpsimd", lambda e: e.collective_compute(
        "AllToAll", AOT.bypass, replica_groups=RG8,
        ins=[ystage.ap()[0:8 * YBLK, :]], outs=[yrecv.ap()]),
        deps=sc_done, coll=True)
    run(ph)

    stPd.close()
    stE.close()

    stF = ExitStack()
    s_yr = SB(stF, "s_yr", [128, 24, 1024])
    ph = phase()
    lych = []
    for a in range(4):
        lych.append(ph.add("sync", lambda e, a=a: e.dma_start(
            s_yr[:, 6 * a:6 * (a + 1), :],
            yrecv.ap()[768 * a:768 * (a + 1), :].rearrange(
                "(m p) c -> p m c", p=128)),
            deps=[cc_y] if a == 0 else [], dma=True))
    prev_acc = [None, None]
    for m in range(24):
        Epar = (m // 3) % 2
        acc = moeacc[Epar]
        scm = ph.add("gpsimd", lambda e, m=m, acc=acc: e.indirect_dma_start(
            acc.ap(), IOA(s_sdst[:, m:m + 1], 0),
            s_yr[:, m, :], None, compute_op=AOT.add),
            deps=[lych[m // 6], prev_acc[Epar]] + ([sdld] if m == 0 else []),
            dma=True)
        prev_acc[Epar] = scm
    run(ph)
    if DEBUG:
        ph = phase()
        ph.add("gpsimd", lambda e: e.dma_start(
            dbg["d_yst"].ap(), ystage.ap()[0:8 * YBLK, :]), dma=True)
        ph.add("gpsimd", lambda e: e.dma_start(
            dbg["d_yrv"].ap(), yrecv.ap()), dma=True)
        ph.add("gpsimd", lambda e: e.dma_start(
            dbg["d_mc0"].ap(), moeacc[0].ap()[0:32, :]), dma=True)
        ph.add("gpsimd", lambda e: e.dma_start(
            dbg["d_mc1"].ap(), moeacc[1].ap()[0:32, :]), dma=True)
        run(ph)

    stF.close()
    stD.close()
    stC2.close()
    stW.close()

    stG = ExitStack()
    s_x1tm = SB(stG, "s_x1tm", [128, 16, 1024])
    s_m = [SB(stG, f"s_m{i}", [128, 16, 1024]) for i in range(2)]
    s_s3q = SB(stG, "s_s3q", [128, 2, 1024])
    s_fin = SB(stG, "s_fin", [128, 128], F32)
    s_ou = SB(stG, "s_ou", [128, 2, 1024], F32)
    ph = phase()
    accdone = list(prev_acc)
    lx1t = []
    for a in range(4):
        lx1t.append(ph.add("sync", lambda e, a=a: e.dma_start(
            s_x1tm[:, 4 * a:4 * (a + 1), :],
            x1row.ap()[512 * a:512 * (a + 1), :].rearrange(
                "(t p) c -> p t c", p=128)),
            deps=accdone if a == 0 else [], dma=True))
    lmt = {}
    for t in range(16):
        for a in range(2):
            lmt[(a, t)] = ph.add(
                "gpsimd", lambda e, a=a, t=t: e.dma_start(
                    s_m[a][:, t, :],
                    moeacc[a].ap()[128 * t:128 * (t + 1), :]),
                deps=accdone if (a == 0 and t == 0) else [], dma=True)
    oprev2 = [None, None]
    prev_sq2 = [None, None]
    for t in range(16):
        par = t % 2
        af_ = None
        for a in range(2):
            af_ = ph.add("vector", lambda e, t=t, a=a: e.tensor_tensor(
                s_x1tm[:, t, :], s_x1tm[:, t, :], s_m[a][:, t, :], AOT.add),
                deps=[lx1t[t // 4] if a == 0 else af_, lmt[(a, t)]])
        a2f = af_
        F = 8 * t
        sm = ph.add("vector", lambda e, t=t, F=F: e.reduce_sum(
            s_fin[:, F:F + 1], s_x1tm[:, t, :], axis=AXL.X), deps=[a2f])
        sq = ph.add("scalar", lambda e, t=t, par=par, F=F: e.activation(
            s_s3q[:, par, :], s_x1tm[:, t, :], ACTF.Square,
            accum_out=s_fin[:, F + 1:F + 2]),
            deps=[a2f, prev_sq2[par]])
        prev_sq2[par] = sq
        mu = ph.add("vector", lambda e, F=F: e.tensor_scalar(
            s_fin[:, F:F + 2], s_fin[:, F:F + 2], 1.0 / C, None, AOT.mult),
            deps=[sm, sq])
        v2f = ph.add("vector", lambda e, F=F: e.tensor_scalar(
            s_fin[:, F + 2:F + 3], s_fin[:, F:F + 1], s_fin[:, F:F + 1],
            s_fin[:, F + 1:F + 2], AOT.mult, AOT.subtract), deps=[mu])
        v3f = ph.add("vector", lambda e, F=F: e.tensor_scalar(
            s_fin[:, F + 2:F + 3], s_fin[:, F + 2:F + 3], -1.0, LN_EPS,
            AOT.mult, AOT.add), deps=[v2f])
        v4f = ph.add("vector", lambda e, F=F: e.reciprocal(
            s_fin[:, F + 2:F + 3], s_fin[:, F + 2:F + 3]), deps=[v3f])
        v5f = ph.add("scalar", lambda e, F=F: e.activation(
            s_fin[:, F + 3:F + 4], s_fin[:, F + 2:F + 3], ACTF.Sqrt),
            deps=[v4f])
        v6f = ph.add("vector", lambda e, F=F: e.tensor_scalar(
            s_fin[:, F + 4:F + 5], s_fin[:, F:F + 1], -1.0,
            s_fin[:, F + 3:F + 4], AOT.mult, AOT.mult), deps=[v5f])
        ap_ = ph.add("vector", lambda e, t=t, par=par, F=F: e.tensor_scalar(
            s_ou[:, par, :], s_x1tm[:, t, :], s_fin[:, F + 3:F + 4],
            s_fin[:, F + 4:F + 5], AOT.mult, AOT.add), deps=[v6f, oprev2[par]])
        od = ph.add("sync", lambda e, t=t, par=par: e.dma_start(
            out.ap()[128 * t:128 * (t + 1), :], s_ou[:, par, :]),
            deps=[ap_], dma=True)
        oprev2[par] = od
    run(ph)

    stG.close()
    stC.close()
    emit_all(nc, sems, SCHED)
    st0.close()
    return nc, dbg


_NC_CACHE = {}


def _consts():
    KONST = np.tile(np.arange(128, dtype=np.float32)[None, :], (128, 1))
    piota = np.arange(128, dtype=np.float32)[:, None]
    cstv = np.zeros((128, 772), np.float32)
    cstv[127, 130] = 1.0
    cstv[:, 131] = (np.arange(128) // 16).astype(np.float32)
    dm = np.eye(128, dtype=np.float32)
    for i in range(127):
        dm[i + 1, i] = -1.0
    cstv[:, 644:772] = dm
    cstv[:, 0:128] = KONST
    cstv[:, 128:129] = piota
    cstv[:, 129:130] = 1.0
    cstv[:, 132:260] = np.eye(128, dtype=np.float32)
    f = np.arange(128)
    cstv[:, 260:388] = (f[:, None] <= f[None, :]).astype(np.float32)
    cstv[:, 388:516] = (f[:, None] < f[None, :]).astype(np.float32)
    cstv[:, 516:644] = (f[:, None] == (16 * (f // 16))[None, :]).astype(np.float32)
    cstbv = np.zeros((128, 512), BF)
    cstbv[:, 384:512] = 1.0
    cstbv[:, 0:128] = np.eye(128, dtype=np.float32).astype(BF)
    cstbv[:, 128:256] = (f[:, None] <= f[None, :]).astype(np.float32).astype(BF)
    cstbv[:, 256:384] = ((f[:, None] // 64) == (f[None, :] // 64)).astype(
        np.float32).astype(BF)
    e2bv = np.zeros((2, 128), BF)
    for h in range(2):
        e2bv[h, 64 * h:64 * (h + 1)] = 1.0
    stcv = np.zeros((128, 4), BF)
    stcv[:, 0] = 1.0
    stcv[:, 3] = 1.0
    lt16v = np.zeros((16, 18), np.float32)
    lt16v[:, 0:16] = (np.arange(16)[:, None] <= np.arange(16)[None, :])
    lt16v[15, 16] = 1.0
    return dict(cst=cstv, cstb=cstbv, onesrow=np.ones((1, 128), np.float32),
                e2b=e2bv, eighth8=np.full((8, 128), 0.125, np.float32),
                lt16f=lt16v, stc=stcv)


def _mmask(h):
    m = np.ones((128, 24), np.float32)

    def setm(key, W, base, valid_lo, valid_hi):
        lo, ln = MS[key + "L"]
        ro, rn = MS[key + "R"]
        for i in range(ln):
            pos = base + i
            if not (valid_lo <= pos < valid_hi):
                m[:, lo + i] = 0.0
        for i in range(rn):
            pos = base + W - rn + i
            if not (valid_lo <= pos < valid_hi):
                m[:, ro + i] = 0.0
    O2 = 1024 * h
    O4 = 512 * h
    setm("a0", W_A0, O2 - 3, 0, 2048)
    setm("h0", W_H0, O2 - 1, 0, 2048)
    setm("a1", W_A1, O4 - 1, 0, 1024)
    setm("la", W_A1, O4 - 1, 0, 1024)
    setm("b0", W_B0, O2 - 1, 0, 2048)
    return m


def _diag(filt):
    d = np.zeros((4, NCH, 128, 128), np.float32)
    for t in range(4):
        for k in range(NCH):
            np.fill_diagonal(d[t, k], filt[t, 128 * k:128 * (k + 1)])
    return d


def kernel(L0_attn_x, L0_attn_h_filter, L0_attn_g_filter, L0_attn_h_synth,
           L0_attn_g_synth, L0_attn_norm_gamma, L0_attn_norm_beta,
           L0_ffn_Wg, L0_ffn_W1, L0_ffn_W2, L0_ffn_W3):
    x = np.ascontiguousarray(np.asarray(L0_attn_x, dtype=np.float32))
    hf = np.asarray(L0_attn_h_filter, np.float32)
    gf = np.asarray(L0_attn_g_filter, np.float32)
    hs = np.asarray(L0_attn_h_synth, np.float32)
    gs = np.asarray(L0_attn_g_synth, np.float32)
    gam = np.asarray(L0_attn_norm_gamma, np.float32)
    bet = np.asarray(L0_attn_norm_beta, np.float32)
    Wg = np.asarray(L0_ffn_Wg, np.float32)
    W1 = np.asarray(L0_ffn_W1, np.float32)
    W2 = np.asarray(L0_ffn_W2, np.float32)
    W3 = np.asarray(L0_ffn_W3, np.float32)

    consts = _consts()
    common = dict(consts)
    common["hfd"] = _diag(hf)
    common["gfd"] = _diag(gf)
    common["hsd"] = _diag(hs)
    common["gsd"] = _diag(gs)
    common["gamc"] = np.ascontiguousarray(gam.reshape(NCH, 128).T)
    common["betc"] = np.ascontiguousarray(bet.reshape(NCH, 128).T)
    common["gamr"] = np.ascontiguousarray(gam.reshape(1, C))
    common["betr"] = np.ascontiguousarray(bet.reshape(1, C))
    common["idpat"] = (float(PTOK)
                         + (np.arange(8 * YBLK + 256) % 128)
                         ).astype(np.float32)[:, None]
    common["wgt"] = np.ascontiguousarray(
        Wg.T.reshape(NCH, 128, 8))

    xp = np.zeros((B, S + 2 * HALO, C), np.float32)
    xp[:, HALO:HALO + S, :] = x

    in_maps = []
    for c in range(8):
        b, h = c // 2, c % 2
        o = PTOK * h
        d = dict(common)
        seg = xp[b, o:o + PTOK + 2 * HALO, :]
        d["xfm"] = np.ascontiguousarray(seg.T)
        d["mmask"] = _mmask(h)
        d["w1td"] = np.ascontiguousarray(W1[c].T)
        d["w3td"] = np.ascontiguousarray(W3[c].T)
        d["w2td"] = np.ascontiguousarray(W2[c].T)
        d["pidx"] = ((c ^ 1) * 128 + np.arange(128, dtype=np.uint32))[:, None]
        in_maps.append(d)

    key = "nc"
    if key not in _NC_CACHE:
        with allow_ici_all_to_all():
            _NC_CACHE[key] = build_nc()
    nc, dbgnames = _NC_CACHE[key]

    with allow_ici_all_to_all():
        res = run_bass_kernel_spmd(nc, in_maps, core_ids=list(range(8)),
                                   trace=globals().get("_TRACE", False))
    if globals().get("_TRACE", False):
        kernel._trace_res = res

    outs = [np.asarray(res.results[c]["out"]) for c in range(8)]
    full = np.concatenate(outs, axis=0).reshape(B, S, C)
    if DEBUG:
        kernel._dbg = res.results
    return full
